# version 02
# baseline (speedup 1.0000x reference)
import math
import numpy as np
import ml_dtypes
import concourse.bass as bass
import concourse.mybir as mybir
from concourse.bass_utils import run_bass_kernel_spmd

F32 = mybir.dt.float32
BF16 = mybir.dt.bfloat16
I32 = mybir.dt.int32
AF = mybir.ActivationFunctionType
ALU = mybir.AluOpType
X = mybir.AxisListType.X

T = 2048
D = 2048
NT = T // 128
ND = D // 128
LN2 = float(np.log(2.0))

DMA_SLOTS = 6


def ts_(i, n):
    return slice(i * n, (i + 1) * n)


class _Op:
    __slots__ = ("engine", "seq", "fn", "is_dma", "deps", "reads", "writes",
                 "need_inc", "slot", "slot_use", "manual_waits")

    def __init__(self, engine, seq, fn, is_dma, reads, writes):
        self.engine = engine
        self.seq = seq
        self.fn = fn
        self.is_dma = is_dma
        self.deps = []
        self.reads = reads
        self.writes = writes
        self.need_inc = False
        self.slot = None
        self.slot_use = None
        self.manual_waits = []


class Prog:

    ENGINES = ("sync", "vector", "scalar", "tensor", "gpsimd")

    def __init__(self, nc):
        self.nc = nc
        self.ops = {e: [] for e in self.ENGINES}
        self.all_ops = []
        self.state = {}

    @staticmethod
    def _key(x):
        if isinstance(x, tuple):
            h, sub = x
            return (id(h), sub)
        return (id(x), None)

    def _st(self, key):
        return self.state.setdefault(key, {"w": None, "r": []})

    def emit(self, engine, fn, reads=(), writes=(), dma=False, after=()):
        op = _Op(engine, len(self.ops[engine]), fn, dma,
                 [self._key(r) for r in reads], [self._key(w) for w in writes])
        deps = []
        for k in op.reads:
            st = self._st(k)
            if st["w"] is not None:
                deps.append(st["w"])
        for k in op.writes:
            st = self._st(k)
            if st["w"] is not None:
                deps.append(st["w"])
            deps.extend(st["r"])
        deps.extend(after)
        seen = set()
        for d in deps:
            if d is op or id(d) in seen:
                continue
            seen.add(id(d))
            op.deps.append(d)
        for k in op.reads:
            st = self._st(k)
            if op not in st["r"]:
                st["r"].append(op)
        for k in op.writes:
            st = self._st(k)
            st["w"] = op
            st["r"] = []
        self.ops[engine].append(op)
        self.all_ops.append(op)
        return op

    def build(self, block):
        for e in self.ENGINES:
            n = 0
            for op in self.ops[e]:
                if op.is_dma:
                    op.slot = n % DMA_SLOTS
                    op.slot_use = n // DMA_SLOTS + 1
                    n += 1
        for op in self.all_ops:
            best = {}
            for d in op.deps:
                if (op.engine == "tensor" and d.engine == "tensor"
                        and not d.is_dma and not op.is_dma):
                    continue
                k = ("dma", d.engine, d.slot) if d.is_dma else ("c", d.engine)
                cur = best.get(k)
                if cur is None or d.seq > cur.seq:
                    best[k] = d
            op.deps = list(best.values())
            for d in op.deps:
                d.need_inc = True

        prog_sem, prog_count = {}, {}
        for e in self.ENGINES:
            cnt = 0
            for op in self.ops[e]:
                if not op.is_dma and op.need_inc:
                    cnt += 1
                    prog_count[id(op)] = cnt
            if cnt:
                prog_sem[e] = self.nc.semaphore(f"pg_{e}").__enter__()
        slot_sem = {}
        for e in self.ENGINES:
            for s in {op.slot for op in self.ops[e] if op.is_dma}:
                slot_sem[(e, s)] = self.nc.semaphore(f"dq_{e}{s}").__enter__()

        hwm = {e: {} for e in self.ENGINES}

        def dep_wait(d):
            if d.is_dma:
                return (slot_sem[(d.engine, d.slot)], 16 * d.slot_use)
            return (prog_sem[d.engine], prog_count[id(d)])

        streams = {e: [] for e in self.ENGINES}
        for e in self.ENGINES:
            for op in self.ops[e]:
                waits = []
                for d in op.deps:
                    sem, val = dep_wait(d)
                    if hwm[e].get(id(sem), 0) < val:
                        waits.append((sem, val))
                        hwm[e][id(sem)] = val
                for sem, val in op.manual_waits:
                    if hwm[e].get(id(sem), 0) < val:
                        waits.append((sem, val))
                        hwm[e][id(sem)] = val
                if op.is_dma:
                    sem = slot_sem[(op.engine, op.slot)]
                    prev = 16 * (op.slot_use - 1)
                    if prev > 0 and hwm[e].get(id(sem), 0) < prev:
                        waits.append((sem, prev))
                        hwm[e][id(sem)] = prev
                    inc = (sem, 16)
                elif op.need_inc:
                    assert op.fn is not None, "wait-only op cannot be a producer"
                    inc = (prog_sem[e], 1)
                else:
                    inc = None
                streams[e].append((waits, op.fn, inc))

        for e in self.ENGINES:
            finals = []
            for s_idx in sorted({op.slot for op in self.ops[e] if op.is_dma}):
                last = max(op.slot_use for op in self.ops[e]
                           if op.is_dma and op.slot == s_idx)
                sem = slot_sem[(e, s_idx)]
                if hwm[e].get(id(sem), 0) < 16 * last:
                    finals.append((sem, 16 * last))
            if finals:
                streams[e].append((finals, None, None))

        for e in self.ENGINES:
            stream = streams[e]
            if not stream:
                continue

            def engine_fn(eng, stream=stream):
                for waits, fn, inc in stream:
                    for sem, val in waits:
                        eng.wait_ge(sem, val)
                    if fn is None:
                        continue
                    instr = fn(eng)
                    if inc is not None:
                        assert instr is not None, "producer fn returned None"
                        instr.then_inc(inc[0], inc[1])
            getattr(block, e)(engine_fn)

    def fence(self):
        tails = []
        for e in self.ENGINES:
            real = [op for op in self.ops[e] if op.fn is not None]
            if real:
                tails.append(real[-1])
            dmas = [op for op in real if op.is_dma][-DMA_SLOTS:]
            tails.extend(dmas)
        for e in self.ENGINES:
            self.emit(e, None, after=list(tails))


class Core:

    def __init__(self, nc):
        self.nc = nc
        self.P = Prog(nc)
        self.consts = {}
        self.const_tile = self.sbuf("consts", [128, 16], F32)
        self.pp = [self.psum(f"pp{i}", [128, 512], F32) for i in range(8)]

    def sbuf(self, name, shape, dt, side=None):
        cm = self.nc.sbuf_tensor(name, shape, dt, side=side)
        h = cm.__enter__()
        if not hasattr(self, "_cms"):
            self._cms = {}
            self._order = {}
            self._seq = 0
        self._cms[id(h)] = cm
        self._order[id(h)] = self._seq
        self._seq += 1
        return h

    def free(self, *handles):
        for h in sorted(handles, key=lambda x: -self._order[id(x)]):
            cm = self._cms.pop(id(h))
            cm.__exit__(None, None, None)

    def psum(self, name, shape, dt=F32):
        return self.nc.psum_tensor(name, shape, dt).__enter__()

    def const(self, val):
        val = float(val)
        if val not in self.consts:
            idx = len(self.consts)
            assert idx < 16
            self.consts[val] = idx
            self.P.emit("gpsimd", lambda gp, i=idx, v=val: gp.memset(
                self.const_tile[:, i:i + 1], v),
                writes=[(self.const_tile, idx)])
        return self.const_tile[:, self.consts[val]:self.consts[val] + 1]

    def cread(self, val):
        return (self.const_tile, self.consts[float(val)])



def emit_ln_phase(c, x_src_fn, xn_fm, xn_tm=None, xnq_fm=None,
                  amax_out=None, tag="ln0", xf32_d=None):
    P = c.P
    quant = xnq_fm is not None
    xbuf = [c.sbuf(f"{tag}_x{i}", [128, D], F32) for i in range(2)]
    sqb = c.sbuf(tag + "_sq", [128, D], F32)
    stats = c.sbuf(tag + "_st", [128, 8], F32)
    xntile = [c.sbuf(f"{tag}_xnt{i}", [128, D], BF16) for i in range(2)]
    allocs = [xbuf[0], xbuf[1], sqb, stats, xntile[0], xntile[1]]
    if quant:
        qtile = [c.sbuf(f"{tag}_qt{i}", [128, D], BF16) for i in range(2)]
        am = c.sbuf(tag + "_am", [128, 64], F32)
        lnm = c.sbuf(tag + "_lnm", [128, 64], F32)
        sef = c.sbuf(tag + "_sef", [128, 64], F32)
        sei = c.sbuf(tag + "_sei", [128, 64], I32)
        amb = c.sbuf(tag + "_amb", [128, 64], BF16)
        invs = c.sbuf(tag + "_ivs", [128, 64], F32)
        sv = c.sbuf(tag + "_sv", [128, 64], F32)
        qs = c.sbuf(tag + "_qs", [128, D], F32)
        qi = c.sbuf(tag + "_qi", [128, D], I32)
        qp = c.sbuf(tag + "_qp", [128, D], F32)
        qe = c.sbuf(tag + "_qe", [128, D], F32)
        qm = sqb
        bc = lambda tile: bass.AP(tile, 0, [[64, 128], [1, 64], [0, 32]])
        allocs += [qtile[0], qtile[1], am, lnm, sef, sei, amb, invs, sv,
                   qs, qi, qp, qe]
        for v in (3.0 * LN2, -3.0 * LN2):
            c.const(v)

    done = []
    for t in range(NT):
        b = t % 2
        xb, xnt = xbuf[b], xntile[b]
        src_ap, src_h = x_src_fn(t)
        P.emit("sync", lambda s, a=src_ap, xb=xb: s.dma_start(out=xb[:], in_=a),
               reads=[src_h], writes=[xb], dma=True)
        P.emit("scalar", lambda s, xb=xb: s.activation(
            out=sqb[:], in_=xb[:], func=AF.Square), reads=[xb], writes=[sqb])
        P.emit("vector", lambda v, xb=xb: v.reduce_sum(
            out=stats[:, 0:1], in_=xb[:], axis=X),
            reads=[xb], writes=[(stats, 0)])
        P.emit("vector", lambda v: v.reduce_sum(
            out=stats[:, 1:2], in_=sqb[:], axis=X),
            reads=[sqb], writes=[(stats, 1)])
        P.emit("vector", lambda v: v.tensor_scalar(
            out=stats[:, 2:3], in0=stats[:, 0:1], scalar1=1.0 / D, scalar2=None,
            op0=ALU.mult), reads=[(stats, 0)], writes=[(stats, 2)])
        P.emit("vector", lambda v: v.tensor_scalar(
            out=stats[:, 3:4], in0=stats[:, 1:2], scalar1=1.0 / D, scalar2=None,
            op0=ALU.mult), reads=[(stats, 1)], writes=[(stats, 3)])
        P.emit("vector", lambda v: v.scalar_tensor_tensor(
            out=stats[:, 4:5], in0=stats[:, 2:3], scalar=stats[:, 2:3],
            in1=stats[:, 3:4], op0=ALU.mult, op1=ALU.subtract),
            reads=[(stats, 2), (stats, 3)], writes=[(stats, 4)])
        P.emit("vector", lambda v: v.tensor_scalar(
            out=stats[:, 4:5], in0=stats[:, 4:5], scalar1=-1.0, scalar2=1e-5,
            op0=ALU.mult, op1=ALU.add), reads=[(stats, 4)], writes=[(stats, 4)])
        P.emit("vector", lambda v: v.reciprocal(
            out=stats[:, 5:6], in_=stats[:, 4:5]),
            reads=[(stats, 4)], writes=[(stats, 5)])
        P.emit("scalar", lambda s: s.activation(
            out=stats[:, 6:7], in_=stats[:, 5:6], func=AF.Ln),
            reads=[(stats, 5)], writes=[(stats, 6)])
        P.emit("scalar", lambda s: s.activation(
            out=stats[:, 7:8], in_=stats[:, 6:7], func=AF.Exp, scale=0.5),
            reads=[(stats, 6)], writes=[(stats, 7)])
        P.emit("vector", lambda v, xb=xb: v.tensor_scalar(
            out=xb[:], in0=xb[:], scalar1=stats[:, 2:3],
            scalar2=stats[:, 7:8], op0=ALU.subtract, op1=ALU.mult),
            reads=[xb, (stats, 2), (stats, 7)], writes=[xb])
        if xf32_d is not None:
            done.append(P.emit("gpsimd", lambda g, xb=xb, t=t: g.dma_start(
                out=xf32_d[ts_(t, 128), :], in_=xb[:]),
                reads=[xb], writes=[(xf32_d, t)], dma=True))
        if xn_fm is not None or xn_tm is not None:
            P.emit("vector", lambda v, xb=xb, xnt=xnt: v.tensor_copy(
                out=xnt[:], in_=xb[:]), reads=[xb], writes=[xnt])
        if xn_tm is not None:
            P.emit("vector", lambda v, xnt=xnt, t=t: v.tensor_copy(
                out=xn_tm[:, ts_(t, D)], in_=xnt[:]),
                reads=[xnt], writes=[(xn_tm, t)])
        if amax_out is not None:
            P.emit("vector", lambda v, xb=xb, t=t: v.tensor_reduce(
                out=amax_out[:, t:t + 1], in_=xb[:],
                axis=X, op=ALU.max, apply_absolute_value=True),
                reads=[xb], writes=[(amax_out, t)])
        if xn_fm is not None:
            tr1 = P.emit("scalar", lambda s, xnt=xnt, t=t: s.dma_start(
                out=bass.AP(xn_fm, t * 128, [[ND * T, 128], [T, ND], [1, 128]]),
                in_=xnt[:], transpose=True),
                reads=[xnt], writes=[(xn_fm, t)], dma=True)
            done.append(tr1)

        if quant:
            P.emit("vector", lambda v, xb=xb: v.tensor_reduce(
                out=am[:], in_=bass.AP(xb, 0, [[D, 128], [32, 64], [1, 32]]),
                axis=X, op=ALU.max, apply_absolute_value=True),
                reads=[xb], writes=[am])
            P.emit("vector", lambda v: v.tensor_scalar(
                out=am[:], in0=am[:], scalar1=2.0 ** -30, scalar2=None,
                op0=ALU.max), reads=[am], writes=[am])
            emit_exact_exp2(c, lambda: am[:], [am], sef, sef, amb, lnm, sei,
                            128, 64)
            P.emit("vector", lambda v: v.tensor_scalar(
                out=invs[:], in0=sef[:], scalar1=-float(2 ** 23),
                scalar2=float(129 * 2 ** 23), op0=ALU.mult, op1=ALU.add),
                reads=[sef], writes=[invs])
            P.emit("vector", lambda v: v.tensor_copy(out=sei[:], in_=invs[:]),
                   reads=[invs], writes=[sei])
            P.emit("vector", lambda v, xb=xb: v.tensor_tensor(
                out=qs[:], in0=xb[:], in1=bc(sei.bitcast(F32)), op=ALU.mult),
                reads=[xb, sei], writes=[qs])
            P.emit("scalar", lambda s: s.activation(
                out=qp[:], in_=qs[:], func=AF.Square), reads=[qs], writes=[qp])
            P.emit("scalar", lambda s: s.activation(
                out=qp[:], in_=qp[:], func=AF.Ln), reads=[qp], writes=[qp])
            P.emit("vector", lambda v: v.tensor_scalar(
                out=qp[:], in0=qp[:], scalar1=1.0 / (2 * LN2), scalar2=1e-3,
                op0=ALU.mult, op1=ALU.add), reads=[qp], writes=[qp])
            P.emit("vector", lambda v: v.tensor_scalar(
                out=qp[:], in0=qp[:], scalar1=0.0, scalar2=None, op0=ALU.max),
                reads=[qp], writes=[qp])
            P.emit("vector", lambda v: v.tensor_copy(out=qi[:], in_=qp[:]),
                   reads=[qp], writes=[qi])
            P.emit("vector", lambda v: v.tensor_copy(out=qe[:], in_=qi[:]),
                   reads=[qi], writes=[qe])
            P.emit("vector", lambda v: v.tensor_tensor(
                out=qm[:], in0=qe[:], in1=qp[:], op=ALU.is_gt),
                reads=[qe, qp], writes=[qm])
            P.emit("vector", lambda v: v.tensor_tensor(
                out=qp[:], in0=qe[:], in1=qm[:], op=ALU.subtract),
                reads=[qe, qm], writes=[qp])
            P.emit("scalar", lambda s: s.activation(
                out=qe[:], in_=qp[:], func=AF.Exp, scale=-LN2,
                bias=c.const(3.0 * LN2)),
                reads=[qp, c.cread(3.0 * LN2)], writes=[qe])
            P.emit("vector", lambda v: v.tensor_tensor(
                out=qs[:], in0=qs[:], in1=qe[:], op=ALU.mult),
                reads=[qs, qe], writes=[qs])
            P.emit("vector", lambda v: v.tensor_scalar(
                out=qs[:], in0=qs[:], scalar1=0.5, scalar2=None, op0=ALU.add),
                reads=[qs], writes=[qs])
            P.emit("vector", lambda v: v.tensor_copy(out=qi[:], in_=qs[:]),
                   reads=[qs], writes=[qi])
            P.emit("vector", lambda v: v.tensor_copy(out=qe[:], in_=qi[:]),
                   reads=[qi], writes=[qe])
            P.emit("vector", lambda v: v.tensor_tensor(
                out=qm[:], in0=qe[:], in1=qs[:], op=ALU.is_gt),
                reads=[qe, qs], writes=[qm])
            P.emit("vector", lambda v: v.tensor_tensor(
                out=qs[:], in0=qe[:], in1=qm[:], op=ALU.subtract),
                reads=[qe, qm], writes=[qs])
            P.emit("scalar", lambda s: s.activation(
                out=qe[:], in_=qp[:], func=AF.Exp, scale=LN2,
                bias=c.const(-3.0 * LN2)),
                reads=[qp, c.cread(-3.0 * LN2)], writes=[qe])
            P.emit("vector", lambda v: v.tensor_tensor(
                out=qs[:], in0=qs[:], in1=qe[:], op=ALU.mult),
                reads=[qs, qe], writes=[qs])
            P.emit("vector", lambda v: v.tensor_scalar(
                out=qs[:], in0=qs[:], scalar1=7.5, scalar2=-7.5,
                op0=ALU.min, op1=ALU.max), reads=[qs], writes=[qs])
            P.emit("vector", lambda v: v.tensor_scalar(
                out=sv[:], in0=sef[:], scalar1=float(2 ** 23),
                scalar2=float(125 * 2 ** 23), op0=ALU.mult, op1=ALU.add),
                reads=[sef], writes=[sv])
            P.emit("vector", lambda v: v.tensor_copy(out=sei[:], in_=sv[:]),
                   reads=[sv], writes=[sei])
            P.emit("vector", lambda v, qt=qtile[b]: v.tensor_tensor(
                out=qt[:], in0=qs[:], in1=bc(sei.bitcast(F32)), op=ALU.mult),
                reads=[qs, sei], writes=[qtile[b]])
            tr2 = P.emit("scalar", lambda s, qt=qtile[b], t=t: s.dma_start(
                out=bass.AP(xnq_fm, t * 128, [[ND * T, 128], [T, ND], [1, 128]]),
                in_=qt[:], transpose=True),
                reads=[qtile[b]], writes=[(xnq_fm, t)], dma=True)
            done.append(tr2)
    return done, allocs


def _build_launch_a():
    nc = bass.Bass("TRN2", target_bir_lowering=False)
    h = {}
    EI = lambda n, shp, dt=F32: nc.dram_tensor(n, shp, dt, kind="ExternalInput")
    DI = lambda n, shp, dt: nc.dram_tensor(n, shp, dt)
    EO = lambda n, shp, dt=F32: nc.dram_tensor(n, shp, dt,
                                               kind="ExternalOutput")
    h["x0"] = EI("x0", [T, D])
    h["w_in"] = EI("w_in", [D, DIN0])
    h["cw0"] = EI("cw0", [CONVD0, 4])
    h["cb0"] = EI("cb0", [CONVD0, 1])
    h["dtb0"] = EI("dtb0", [64, 1])
    h["A0"] = EI("A0", [64, 1])
    h["D0r"] = EI("D0r", [1, 64])
    h["nw0"] = EI("nw0", [INTER0, 1])
    h["w_out0"] = EI("w_out0", [INTER0, D])
    h["wg"] = EI("wg", [8192, D])
    h["wu"] = EI("wu", [8192, D])
    h["wd"] = EI("wd", [D, 8192])
    h["w_qkvT"] = EI("w_qkvT", [D, CONV1])
    h["w_zT"] = EI("w_zT", [D, VDIM1])
    h["w_baT"] = EI("w_baT", [D, 64])
    h["cw1"] = EI("cw1", [CONV1, 4])
    h["dtb1"] = EI("dtb1", [32, 1])
    h["alog1"] = EI("alog1", [32, 1])
    h["nw1"] = EI("nw1", [1, 128])
    h["w_outT1"] = EI("w_outT1", [VDIM1, D])
    h["gate_tm_d"] = DI("gate_tm", [T, INTER0], BF16)
    h["xc_tm_d"] = DI("xc_tm", [T, INTER0], BF16)
    h["y_tm_d"] = DI("y_tm", [T, INTER0], BF16)
    h["wdq_fm_d"] = DI("wdq_fm", [8192, D], BF16)
    h["x1a_d"] = DI("x1a", [T, D], F32)
    h["x1b_d"] = DI("x1b", [T, D], F32)
    h["xf_d"] = DI("xf", [T, D], F32)
    h["x1_d"] = DI("x1", [T, D], F32)
    h["qn_fm_d"] = DI("qn_fm", [2048, T], BF16)
    h["kn_fm_d"] = DI("kn_fm", [2048, T], BF16)
    h["kn_tm_d"] = DI("kn_tm", [T, 2048], BF16)
    h["v_tm_d"] = DI("v_tm", [T, VDIM1], BF16)
    h["z1_tm_d"] = DI("z1_tm", [T, VDIM1], BF16)
    h["yq_fm_d"] = DI("yq_fm", [VDIM1, T], BF16)
    h["x2p_d"] = EO("x2p", [T, D])
    h["xn1f_d"] = EO("xn1f", [T, D])
    h["lmax_d"] = EO("lmax", [1, 1])
    import contextlib
    with contextlib.ExitStack() as ctx:
        c = Core(nc)
        xnq_fm = c.sbuf("xnq_fm", [128, ND * T], BF16, side="right")
        block = ctx.enter_context(nc.Block())
        P = c.P
        res = {}
        x0src = lambda t: (h["x0"][ts_(t, 128), :], h["x0"])
        _, lns = emit_ln_phase(c, x0src, None, xnq_fm=xnq_fm, tag="lnq")
        P.fence()
        c.free(*lns)
        h_f = dict(h)
        h_f["x1_d"] = h["xf_d"]
        emit_l0_ffn(c, xnq_fm, h_f, res)
        P.fence()
        c.free(*res.pop("_scratch5"))
        c.free(res.pop("interq_q"))
        c.free(xnq_fm)
        xn_fm = c.sbuf("xn_fm", [128, ND * T], BF16)
        _, lns = emit_ln_phase(c, x0src, xn_fm, tag="lnf")
        P.fence()
        c.free(*lns)
        emit_l0_proj(c, xn_fm, h, res)
        P.fence()
        c.free(xn_fm, *res.pop("_scratch"))
        emit_l0_ssd(c, h, res)
        P.fence()
        c.free(*res.pop("_scratch2"))
        c.free(res.pop("C_fm"), res.pop("B_fm"), res.pop("dt_f"),
               res.pop("A_sb"))
        emit_l0_zout(c, h, res)
        P.fence()
        c.free(*res.pop("_scratch3"))
        h_w = dict(h)
        h_w["x1p_d"] = h["x1_d"]
        emit_l0_wout(c, h_w, res)
        P.fence()
        c.free(*res.pop("_scratch4"))
        c.free(res.pop("z_fm"))
        xn_fm = c.sbuf("xn_fm2", [128, ND * T], BF16)
        amax = c.sbuf("ln1_amax", [128, NT], F32)
        x1src = lambda t: (h["x1_d"][ts_(t, 128), :], h["x1_d"])
        _, lns = emit_ln_phase(c, x1src, xn_fm, amax_out=amax,
                               xf32_d=h["xn1f_d"], tag="ln1")
        id2 = c.sbuf("lm_id", [128, 128], F32)
        lmr = c.sbuf("lm_r", [128, 1], F32)
        lm1 = c.sbuf("lm_1", [1, 1], F32)
        P.emit("gpsimd", lambda g: g.memset(id2[:], 1.0), writes=[(id2, 0)])
        P.emit("gpsimd", lambda g: g.affine_select(
            out=id2[:], in_=id2[:], pattern=[[1, 128]],
            compare_op=ALU.is_equal, fill=0.0, base=0, channel_multiplier=-1),
            reads=[(id2, 0)], writes=[(id2, 1)])
        P.emit("vector", lambda v: v.tensor_reduce(
            out=lmr[:], in_=amax[:], axis=X, op=ALU.max),
            reads=[(amax, t) for t in range(NT)], writes=[lmr])
        P.emit("tensor", lambda te: te.transpose(
            c.pp[0][0:1, 0:128], lmr[:], id2[:]),
            reads=[lmr, (id2, 1)], writes=[c.pp[0]])
        P.emit("vector", lambda v: v.tensor_reduce(
            out=lm1[:], in_=c.pp[0][0:1, 0:128], axis=X, op=ALU.max),
            reads=[c.pp[0]], writes=[lm1])
        P.emit("gpsimd", lambda g: g.dma_start(out=h["lmax_d"][:, :],
                                               in_=lm1[:]),
               reads=[lm1], writes=[h["lmax_d"]], dma=True)
        P.fence()
        c.free(lm1, lmr, id2, *lns)
        c.free(amax)
        emit_l1_proj(c, xn_fm, h, res)
        P.fence()
        c.free(xn_fm, *res.pop("_scratch6"))
        emit_l1_delta(c, h, res)
        P.fence()
        c.free(*res.pop("_scratch7"))
        c.free(res.pop("beta_f"), res.pop("cgb"), res.pop("cg"))
        emit_l1_wout(c, h, res)
        P.build(block)
    return nc


def _build_launch_b():
    nc = bass.Bass("TRN2", target_bir_lowering=False)
    h = {}
    h["fwT"] = nc.dram_tensor("fwT", [D, D], BF16, kind="ExternalInput")
    h["fb"] = nc.dram_tensor("fb", [1, D], BF16, kind="ExternalInput")
    h["gmax"] = nc.dram_tensor("gmax", [1, 8], F32, kind="ExternalInput")
    h["xn1f_d"] = nc.dram_tensor("xn1f", [T, D], F32, kind="ExternalInput")
    h["x2p_d"] = nc.dram_tensor("x2p", [T, D], F32, kind="ExternalInput")
    h["out_d"] = nc.dram_tensor("out", [T, D], F32, kind="ExternalOutput")
    import contextlib
    with contextlib.ExitStack() as ctx:
        c = Core(nc)
        block = ctx.enter_context(nc.Block())
        emit_ffn1(c, h, {})
        c.P.build(block)
    return nc


def kernel(**inputs):
    import numpy as np
    g = lambda k: np.asarray(inputs[k])
    ca = np.ascontiguousarray
    x = g("L0_attn_hidden_states")
    base = {
        "w_in": ca(g("L0_attn_W_in")),
        "cw0": ca(g("L0_attn_conv_w")),
        "cb0": ca(g("L0_attn_conv_b").reshape(CONVD0, 1)),
        "dtb0": ca(g("L0_attn_dt_bias").reshape(64, 1)),
        "A0": ca(g("L0_attn_A").reshape(64, 1)),
        "D0r": ca(g("L0_attn_D").reshape(1, 64)),
        "nw0": ca(g("L0_attn_norm_weight").reshape(INTER0, 1)),
        "w_out0": ca(g("L0_attn_W_out")),
        "wg": ca(g("L0_ffn_w_gate")),
        "wu": ca(g("L0_ffn_w_up")),
        "wd": ca(g("L0_ffn_w_down")),
        "w_qkvT": ca(g("L1_attn_W_qkv").T),
        "w_zT": ca(g("L1_attn_W_z").T),
        "w_baT": ca(np.concatenate([g("L1_attn_W_b"),
                                    g("L1_attn_W_a")], 0).T),
        "cw1": ca(g("L1_attn_conv_w").reshape(CONV1, 4)),
        "dtb1": ca(g("L1_attn_dt_bias").reshape(32, 1)),
        "alog1": ca(g("L1_attn_A_log").reshape(32, 1)),
        "nw1": ca(g("L1_attn_norm_w").reshape(1, 128)),
        "w_outT1": ca(g("L1_attn_W_out").T),
    }
    in_a = [dict(base, x0=ca(x[i])) for i in range(8)]
    nc_a = _build_launch_a()
    ra = run_bass_kernel_spmd(nc_a, in_a, core_ids=list(range(8)))
    gmax = np.concatenate(
        [np.asarray(ra.results[i]["lmax"]).reshape(1, 1) for i in range(8)],
        axis=1).astype(np.float32)
    fwT = ca(g("L1_ffn_weight").T)
    fb = ca(g("L1_ffn_bias").reshape(1, D))
    in_b = [{"fwT": fwT, "fb": fb, "gmax": gmax,
             "xn1f": np.asarray(ra.results[i]["xn1f"]),
             "x2p": np.asarray(ra.results[i]["x2p"])} for i in range(8)]
    nc_b = _build_launch_b()
    rb = run_bass_kernel_spmd(nc_b, in_b, core_ids=list(range(8)))
    return np.stack([np.asarray(rb.results[i]["out"]) for i in range(8)], 0)



INTER0 = 4096
CONVD0 = 4352
DIN0 = 8512


def emit_l0_proj(c, xn_fm, h, res):
    P = c.P
    wbuf = [c.sbuf(f"l0w{i}", [128, 16, 128], BF16) for i in range(2)]
    ps = [c.pp[0], c.pp[1]]
    xcbuf = c.sbuf("l0xc", [128, 2051], F32)
    cacc = c.sbuf("l0cacc", [128, T], F32)
    ctile = [c.sbuf(f"l0ct{i}", [128, T], BF16) for i in range(2)]
    gtile = [c.sbuf(f"l0gt{i}", [128, 512], BF16) for i in range(2)]
    gstag = [c.sbuf(f"l0gs{i}", [128, 4, 128], BF16) for i in range(2)]
    tstag = [c.sbuf(f"l0ts{i}", [128, 16, 128], BF16) for i in range(2)]
    cw_sb = c.sbuf("l0cw", [128, 34, 4], F32)
    cb_sb = c.sbuf("l0cb", [128, 34], F32)
    dtb_sb = c.sbuf("l0dtb", [64, 1], F32)
    A_sb = c.sbuf("l0A", [64, 1], F32, side="right")
    dtraw = c.sbuf("l0dtraw", [64, T], F32)
    dt_f = c.sbuf("l0dt", [64, T], F32, side="right")
    dttmp = c.sbuf("l0dttmp", [64, T], F32)
    B_fm = c.sbuf("l0B", [128, T], BF16, side="right")
    C_fm = c.sbuf("l0C", [128, T], BF16, side="right")
    res.update(B_fm=B_fm, C_fm=C_fm, dt_f=dt_f, A_sb=A_sb)

    P.emit("sync", lambda s: s.dma_start(
        out=cw_sb[:], in_=bass.AP(h["cw0"], 0, [[4, 128], [128 * 4, 34], [1, 4]])),
        reads=[h["cw0"]], writes=[cw_sb], dma=True)
    P.emit("sync", lambda s: s.dma_start(
        out=cb_sb[:], in_=bass.AP(h["cb0"], 0, [[1, 128], [128, 34], [1, 1]]),
        allow_slow_non_contiguous=True),
        reads=[h["cb0"]], writes=[cb_sb], dma=True)
    P.emit("sync", lambda s: s.dma_start(out=dtb_sb[:], in_=h["dtb0"][:, :]),
           reads=[h["dtb0"]], writes=[dtb_sb], dma=True)
    P.emit("sync", lambda s: s.dma_start(out=A_sb[:], in_=h["A0"][:, :]),
           reads=[h["A0"]], writes=[A_sb], dma=True)
    P.emit("gpsimd", lambda g: g.memset(xcbuf[:, 0:3], 0.0), writes=[xcbuf])

    for nt in range(67):
        b = nt % 2
        wb = wbuf[b]
        ncols = 64 if nt == 66 else 128
        P.emit("gpsimd", lambda g, nt=nt, wb=wb, ncols=ncols: g.dma_start(
            out=wb[:, :, 0:ncols],
            in_=bass.AP(h["w_in"], nt * 128,
                        [[DIN0, 128], [128 * DIN0, 16], [1, ncols]])),
            reads=[h["w_in"]], writes=[wb], dma=True)
        for tc in range(4):
            pb = ps[tc % 2]
            for k in range(16):
                P.emit("tensor", lambda te, wb=wb, k=k, tc=tc, pb=pb, ncols=ncols:
                       te.matmul(pb[0:ncols, :], wb[:, k, 0:ncols],
                                 xn_fm[:, k * T + tc * 512: k * T + (tc + 1) * 512],
                                 start=(k == 0), stop=(k == 15)),
                       reads=[wb, xn_fm], writes=[pb])
            if nt < 32:
                gt, gs = gtile[tc % 2], gstag[tc % 2]
                P.emit("scalar", lambda s, pb=pb, gt=gt: s.copy(
                    out=gt[:], in_=pb[:]), reads=[pb], writes=[gt])
                P.emit("scalar", lambda s, gt=gt, gs=gs: s.dma_start(
                    out=gs[:], in_=gt[:], transpose=True),
                    reads=[gt], writes=[gs], dma=True)
                P.emit("sync", lambda s, gs=gs, nt=nt, tc=tc: s.dma_start(
                    out=bass.AP(h["gate_tm_d"], (tc * 512) * INTER0 + nt * 128,
                                [[INTER0, 128], [128 * INTER0, 4], [1, 128]]),
                    in_=gs[:]), reads=[gs], writes=[(h["gate_tm_d"], (nt, tc))],
                    dma=True)
            elif nt < 66:
                P.emit("scalar", lambda s, pb=pb, tc=tc: s.copy(
                    out=xcbuf[:, 3 + tc * 512: 3 + (tc + 1) * 512], in_=pb[:]),
                    reads=[pb], writes=[xcbuf])
            else:
                P.emit("scalar", lambda s, pb=pb, tc=tc: s.copy(
                    out=dtraw[:, tc * 512:(tc + 1) * 512], in_=pb[0:64, :]),
                    reads=[pb], writes=[dtraw])
        if 32 <= nt < 66:
            j = nt - 32
            P.emit("vector", lambda v, j=j: v.tensor_scalar(
                out=cacc[:], in0=xcbuf[:, 0:T], scalar1=cw_sb[:, j, 0:1],
                scalar2=None, op0=ALU.mult),
                reads=[xcbuf, cw_sb], writes=[cacc])
            for k in range(1, 4):
                P.emit("vector", lambda v, j=j, k=k: v.scalar_tensor_tensor(
                    out=cacc[:], in0=xcbuf[:, k:k + T],
                    scalar=cw_sb[:, j, k:k + 1], in1=cacc[:],
                    op0=ALU.mult, op1=ALU.add),
                    reads=[xcbuf, cw_sb, cacc], writes=[cacc])
            dst = ctile[b] if j < 32 else (B_fm if j == 32 else C_fm)
            P.emit("scalar", lambda s, j=j, dst=dst: s.activation(
                out=dst[:], in_=cacc[:], func=AF.Silu, bias=cb_sb[:, j:j + 1]),
                reads=[cacc, cb_sb], writes=[dst])
            if j < 32:
                ct, tst = ctile[b], tstag[b]
                P.emit("scalar", lambda s, ct=ct, tst=tst: s.dma_start(
                    out=tst[:], in_=ct[:], transpose=True),
                    reads=[ct], writes=[tst], dma=True)
                P.emit("sync", lambda s, tst=tst, j=j: s.dma_start(
                    out=bass.AP(h["xc_tm_d"], j * 128,
                                [[INTER0, 128], [128 * INTER0, 16], [1, 128]]),
                    in_=tst[:]), reads=[tst], writes=[(h["xc_tm_d"], j)], dma=True)
    P.emit("vector", lambda v: v.tensor_scalar(
        out=dtraw[:], in0=dtraw[:], scalar1=dtb_sb[:, 0:1], scalar2=None,
        op0=ALU.add), reads=[dtraw, dtb_sb], writes=[dtraw])
    P.emit("vector", lambda v: v.tensor_scalar(
        out=dttmp[:], in0=dtraw[:], scalar1=60.0, scalar2=None,
        op0=ALU.min), reads=[dtraw], writes=[dttmp])
    P.emit("scalar", lambda s: s.activation(
        out=dttmp[:], in_=dttmp[:], func=AF.Exp), reads=[dttmp], writes=[dttmp])
    P.emit("vector", lambda v: v.tensor_scalar(
        out=dttmp[:], in0=dttmp[:], scalar1=1.0, scalar2=None,
        op0=ALU.add), reads=[dttmp], writes=[dttmp])
    P.emit("scalar", lambda s: s.activation(
        out=dttmp[:], in_=dttmp[:], func=AF.Ln), reads=[dttmp], writes=[dttmp])
    P.emit("vector", lambda v: v.tensor_tensor(
        out=dt_f[:], in0=dttmp[:], in1=dtraw[:], op=ALU.max),
        reads=[dttmp, dtraw], writes=[dt_f])
    res["_scratch"] = [wbuf[0], wbuf[1], xcbuf, cacc, ctile[0], ctile[1],
                       gtile[0], gtile[1], gstag[0], gstag[1], tstag[0],
                       tstag[1], dtraw, dttmp, cw_sb, cb_sb, dtb_sb]
    return res



def emit_l0_ssd(c, h, res):
    P = c.P
    NH, HD, NS, L = 64, 64, 128, 128
    NCH = T // L
    B_fm, C_fm, dt_f, A_sb = res["B_fm"], res["C_fm"], res["dt_f"], res["A_sb"]

    dA = c.sbuf("s_dA", [64, T], F32)
    cA = c.sbuf("s_cA", [64, T], F32)
    negcA = c.sbuf("s_ncA", [64, L], F32)
    ident = c.sbuf("s_id", [128, 128], F32)
    identb = c.sbuf("s_idb", [128, 128], BF16)
    maskMB = c.sbuf("s_mb", [128, 128], F32)
    ones1 = c.sbuf("s_one1", [1, 128], F32)
    lhsT_q = [c.sbuf(f"s_lhsT{i}", [2, 16 * L], F32) for i in range(2)]
    rhs_q = [c.sbuf(f"s_rhs{i}", [2, 16 * L], F32) for i in range(2)]
    stack = c.sbuf("s_stk", [128, 128], F32)
    trbuf = c.sbuf("s_tr", [128, 128], F32)
    ecat = c.sbuf("s_ecat", [128, 64], F32)
    replast = c.sbuf("s_rpl", [128, 64], F32)
    dsdtT = c.sbuf("s_dsdt", [128, 64], F32)
    cdrep = c.sbuf("s_cdr", [128, 64], F32)
    larow = c.sbuf("s_larow", [1, 64], F32)
    cbt = c.sbuf("s_cbt", [128, 128], BF16)
    btm = c.sbuf("s_btm", [128, 128], BF16)
    xcw = c.sbuf("s_xcw", [128, 4096], BF16)
    mh = [c.sbuf(f"s_mh{i}", [128, 512], BF16) for i in range(2)]
    etile = c.sbuf("s_et", [128, 512], F32)
    ytile = c.sbuf("s_yt", [128, 4096], BF16)
    yf32 = c.sbuf("s_yf", [128, 512], F32)
    S_sb = c.sbuf("s_S", [128, 4096], F32)
    S_bf = c.sbuf("s_Sb", [128, 4096], BF16)
    ddiag = c.sbuf("s_ddg", [128, NH * 128], BF16)
    drep = c.sbuf("s_drp", [128, 64], F32)
    dr_row = c.sbuf("s_drow", [1, 64], F32)
    xtm = [c.sbuf(f"s_xtm{i}", [128, 4096], BF16) for i in range(2)]

    pp = c.pp

    P.emit("gpsimd", lambda g: g.memset(ident[:], 1.0), writes=[ident])
    P.emit("gpsimd", lambda g: g.affine_select(
        out=ident[:], in_=ident[:], pattern=[[1, 128]],
        compare_op=ALU.is_equal, fill=0.0, base=0, channel_multiplier=-1),
        reads=[ident], writes=[ident])
    P.emit("vector", lambda v: v.tensor_copy(out=identb[:], in_=ident[:]),
           reads=[ident], writes=[identb])
    P.emit("gpsimd", lambda g: g.memset(maskMB[:], 0.0), writes=[maskMB])
    P.emit("gpsimd", lambda g: g.affine_select(
        out=maskMB[:], in_=maskMB[:], pattern=[[1, 128]],
        compare_op=ALU.is_ge, fill=-1e30, base=0, channel_multiplier=-1),
        reads=[maskMB], writes=[maskMB])
    P.emit("gpsimd", lambda g: g.memset(ones1[:], 1.0), writes=[ones1])
    for i in range(2):
        P.emit("gpsimd", lambda g, i=i: g.memset(lhsT_q[i][:, :], 1.0),
               writes=[(lhsT_q[i], "ones"), (lhsT_q[i], "cA")])
        P.emit("gpsimd", lambda g, i=i: g.memset(rhs_q[i][:, :], 1.0),
               writes=[(rhs_q[i], "ones"), (rhs_q[i], "cA")])
    P.emit("gpsimd", lambda g: g.memset(S_sb[:], 0.0), writes=[S_sb])
    P.emit("gpsimd", lambda g: g.memset(S_bf[:], 0.0), writes=[S_bf])
    P.emit("sync", lambda s: s.dma_start(out=dr_row[:], in_=h["D0r"][:, :]),
           reads=[h["D0r"]], writes=[dr_row], dma=True)
    P.emit("tensor", lambda te: te.matmul(
        pp[0][:, 0:64], ones1[:], dr_row[:], start=True, stop=True),
        reads=[ones1, dr_row], writes=[pp[0]])
    P.emit("scalar", lambda s: s.copy(out=drep[:], in_=pp[0][:, 0:64]),
           reads=[pp[0]], writes=[drep])
    for hh in range(NH):
        P.emit("vector", lambda v, hh=hh: v.tensor_scalar(
            out=ddiag[:, ts_(hh, 128)], in0=identb[:], scalar1=drep[:, hh:hh + 1],
            scalar2=None, op0=ALU.mult),
            reads=[identb, drep], writes=[(ddiag, hh)])
    P.emit("vector", lambda v: v.tensor_scalar(
        out=dA[:], in0=dt_f[:], scalar1=A_sb[:, 0:1], scalar2=None,
        op0=ALU.mult), reads=[dt_f, A_sb], writes=[dA])
    for ci in range(NCH):
        P.emit("vector", lambda v, ci=ci: v.tensor_tensor_scan(
            out=cA[:, ts_(ci, L)], data0=dA[:, ts_(ci, L)],
            data1=dA[:, ts_(ci, L)], initial=0.0, op0=ALU.add, op1=ALU.bypass),
            reads=[dA], writes=[(cA, ci)])

    for ci in range(NCH):
        bb = ci % 2
        cAc = (cA, ci)
        P.emit("sync", lambda s, ci=ci, d=xtm[bb]: s.dma_start(
            out=d[:], in_=h["xc_tm_d"][ts_(ci, 128), :]),
            reads=[(h["xc_tm_d"], j) for j in range(32)], writes=[xtm[bb]],
            dma=True)
        P.emit("vector", lambda v, ci=ci: v.tensor_scalar(
            out=negcA[:], in0=cA[:, ts_(ci, L)], scalar1=-1.0, scalar2=None,
            op0=ALU.mult), reads=[cAc], writes=[negcA])
        P.emit("vector", lambda v, ci=ci: v.tensor_copy(
            out=stack[0:64, :], in_=cA[:, ts_(ci, L)]),
            reads=[cAc], writes=[(stack, 0)])
        P.emit("vector", lambda v, ci=ci: v.tensor_copy(
            out=stack[64:128, :], in_=dt_f[:, ts_(ci, L)]),
            reads=[dt_f], writes=[(stack, 1)])
        P.emit("tensor", lambda te: te.transpose(pp[2][:, 0:128], stack[:], ident[:]),
               reads=[(stack, 0), (stack, 1), ident], writes=[pp[2]])
        P.emit("scalar", lambda s: s.copy(out=trbuf[:], in_=pp[2][:, 0:128]),
               reads=[pp[2]], writes=[trbuf])
        P.emit("tensor", lambda te, ci=ci: te.transpose(
            pp[3].bitcast(BF16)[:, 0:128], B_fm[:, ts_(ci, L)], identb[:]),
            reads=[B_fm, identb], writes=[pp[3]])
        P.emit("scalar", lambda s: s.copy(out=btm[:], in_=pp[3].bitcast(BF16)[:, 0:128]),
               reads=[pp[3]], writes=[btm])
        P.emit("scalar", lambda s, ci=ci: s.dma_start(
            out=larow[:], in_=cA[:, ci * L + L - 1: ci * L + L],
            allow_slow_non_contiguous=True),
            reads=[cAc], writes=[larow], dma=True)
        P.emit("tensor", lambda te: te.matmul(
            pp[2][:, 128:192], ones1[:], larow[:],
            start=True, stop=True), reads=[ones1, larow], writes=[pp[2]])
        P.emit("scalar", lambda s: s.copy(out=replast[:], in_=pp[2][:, 128:192]),
               reads=[pp[2]], writes=[replast])
        P.emit("scalar", lambda s: s.activation(
            out=ecat[:], in_=trbuf[:, 0:64], func=AF.Exp),
            reads=[trbuf], writes=[ecat])
        P.emit("vector", lambda v: v.tensor_tensor(
            out=dsdtT[:], in0=replast[:], in1=trbuf[:, 0:64], op=ALU.subtract),
            reads=[replast, trbuf], writes=[dsdtT])
        P.emit("scalar", lambda s: s.activation(
            out=dsdtT[:], in_=dsdtT[:], func=AF.Exp),
            reads=[dsdtT], writes=[dsdtT])
        P.emit("vector", lambda v: v.tensor_tensor(
            out=dsdtT[:], in0=dsdtT[:], in1=trbuf[:, 64:128], op=ALU.mult),
            reads=[dsdtT, trbuf], writes=[dsdtT])
        P.emit("scalar", lambda s: s.activation(
            out=cdrep[:], in_=replast[:], func=AF.Exp),
            reads=[replast], writes=[cdrep])
        P.emit("tensor", lambda te, ci=ci: te.matmul(
            pp[2][:, 256:384], B_fm[:, ts_(ci, L)],
            C_fm[:, ts_(ci, L)], start=True, stop=True),
            reads=[B_fm, C_fm], writes=[pp[2]])
        P.emit("scalar", lambda s: s.copy(out=cbt[:], in_=pp[2][:, 256:384]),
               reads=[pp[2]], writes=[cbt])
        P.emit("vector", lambda v, d=xtm[bb]: v.tensor_tensor(
            out=xcw[:], in0=d[:],
            in1=bass.AP(dsdtT, 0, [[64, 128], [1, 64], [0, 64]]), op=ALU.mult),
            reads=[xtm[bb], dsdtT], writes=[xcw])

        for q in range(4):
            qb = q % 2
            lq, rq = lhsT_q[qb], rhs_q[qb]
            P.emit("scalar", lambda s, q=q, lq=lq: s.dma_start(
                out=lq[0:1, :], in_=negcA[ts_(q, 16), :],
                allow_slow_non_contiguous=True),
                reads=[negcA], writes=[(lq, "cA")], dma=True)
            P.emit("scalar", lambda s, q=q, ci=ci, rq=rq: s.dma_start(
                out=rq[1:2, :], in_=cA[ts_(q, 16), ts_(ci, L)],
                allow_slow_non_contiguous=True),
                reads=[cAc], writes=[(rq, "cA")], dma=True)
            for g4 in range(4):
                gi = g4 % 2
                mtile = mh[gi]
                t1p = pp[2] if gi == 0 else pp[3]
                ya = pp[4 + gi]
                yb_ = pp[6 + gi]
                for hh4 in range(4):
                    hl = g4 * 4 + hh4
                    P.emit("tensor", lambda te, hl=hl, hh4=hh4, t1p=t1p,
                           lq=lq, rq=rq: te.matmul(
                        t1p[:, ts_(hh4, 128)], lq[:, ts_(hl, 128)],
                        rq[:, ts_(hl, 128)], start=True, stop=True),
                        reads=[(lq, "cA"), (lq, "ones"), (rq, "cA"),
                               (rq, "ones")], writes=[t1p])
                P.emit("vector", lambda v, t1p=t1p: v.tensor_tensor(
                    out=etile[:], in0=t1p[:],
                    in1=bass.AP(maskMB, 0, [[128, 128], [0, 4], [1, 128]]),
                    op=ALU.add), reads=[t1p, maskMB], writes=[etile])
                P.emit("scalar", lambda s: s.activation(
                    out=etile[:], in_=etile[:], func=AF.Exp),
                    reads=[etile], writes=[etile])
                hh0 = q * 16 + g4 * 4
                P.emit("vector", lambda v, hh0=hh0: v.tensor_tensor(
                    out=etile[:], in0=etile[:],
                    in1=bass.AP(trbuf, 64 + hh0, [[128, 128], [1, 4], [0, 128]]),
                    op=ALU.mult), reads=[etile, trbuf], writes=[etile])
                P.emit("vector", lambda v, mtile=mtile: v.tensor_tensor(
                    out=mtile[:], in0=etile[:],
                    in1=bass.AP(cbt, 0, [[128, 128], [0, 4], [1, 128]]),
                    op=ALU.mult), reads=[etile, cbt], writes=[mtile])
                for hh4 in range(4):
                    hh = hh0 + hh4
                    col = hh4 * 64
                    P.emit("tensor", lambda te, hh=hh, hh4=hh4, mtile=mtile,
                           ya=ya, col=col, d=xtm[bb]: te.matmul(
                        ya[:, col:col + 64], mtile[:, ts_(hh4, 128)],
                        d[:, ts_(hh, 64)], start=True, stop=False),
                        reads=[mtile, xtm[bb]], writes=[ya])
                    P.emit("tensor", lambda te, hh=hh, ya=ya, col=col,
                           d=xtm[bb]: te.matmul(
                        ya[:, col:col + 64], ddiag[:, ts_(hh, 128)],
                        d[:, ts_(hh, 64)], start=False, stop=True),
                        reads=[(ddiag, hh), xtm[bb]], writes=[ya])
                    P.emit("tensor", lambda te, hh=hh, ci=ci, yb_=yb_,
                           col=col: te.matmul(
                        yb_[:, col:col + 64], C_fm[:, ts_(ci, 128)],
                        S_bf[:, ts_(hh, 64)], start=True, stop=True),
                        reads=[C_fm, S_bf], writes=[yb_])
                P.emit("vector", lambda v, yb_=yb_, hh0=hh0: v.tensor_tensor(
                    out=yf32[:, 0:256], in0=yb_[:, 0:256],
                    in1=bass.AP(ecat, hh0, [[64, 128], [1, 4], [0, 64]]),
                    op=ALU.mult), reads=[yb_, ecat], writes=[yf32])
                P.emit("vector", lambda v, ya=ya, hh0=hh0: v.tensor_tensor(
                    out=ytile[:, hh0 * 64:hh0 * 64 + 256], in0=yf32[:, 0:256],
                    in1=ya[:, 0:256],
                    op=ALU.add), reads=[yf32, ya], writes=[(ytile, hh0)])
        for g8 in range(8):
            stp = pp[4 + (g8 % 2)]
            for hh8 in range(8):
                hh = g8 * 8 + hh8
                P.emit("tensor", lambda te, hh=hh, hh8=hh8, stp=stp: te.matmul(
                    stp[:, hh8 * 64:(hh8 + 1) * 64], btm[:],
                    xcw[:, ts_(hh, 64)], start=True, stop=True),
                    reads=[btm, xcw], writes=[stp])
            P.emit("vector", lambda v, g8=g8: v.tensor_tensor(
                out=S_sb[:, g8 * 512:(g8 + 1) * 512],
                in0=S_sb[:, g8 * 512:(g8 + 1) * 512],
                in1=bass.AP(cdrep, g8 * 8, [[64, 128], [1, 8], [0, 64]]),
                op=ALU.mult), reads=[S_sb, cdrep], writes=[S_sb])
            P.emit("vector", lambda v, g8=g8, stp=stp: v.tensor_tensor(
                out=S_sb[:, g8 * 512:(g8 + 1) * 512],
                in0=S_sb[:, g8 * 512:(g8 + 1) * 512], in1=stp[:],
                op=ALU.add), reads=[S_sb, stp], writes=[S_sb])
            P.emit("vector", lambda v, g8=g8: v.tensor_copy(
                out=S_bf[:, g8 * 512:(g8 + 1) * 512],
                in_=S_sb[:, g8 * 512:(g8 + 1) * 512]),
                reads=[S_sb], writes=[S_bf])
        P.emit("gpsimd", lambda g, ci=ci: g.dma_start(
            out=h["y_tm_d"][ts_(ci, 128), :], in_=ytile[:]),
            reads=[(ytile, hh0) for hh0 in range(0, 64, 4)],
            writes=[(h["y_tm_d"], ci)], dma=True)
    res["_scratch2"] = [dA, cA, negcA, ident, identb, maskMB, ones1,
                        lhsT_q[0], lhsT_q[1], rhs_q[0], rhs_q[1], stack,
                        trbuf, ecat, replast, dsdtT, cdrep, larow, cbt, btm,
                        xcw, mh[0], mh[1], etile, ytile, yf32, S_sb, S_bf,
                        ddiag, drep, dr_row, xtm[0], xtm[1]]
    return res



def emit_l0_zout(c, h, res):
    P = c.P
    z_fm = c.sbuf("z_fm", [128, 32 * T], BF16, side="right")
    ybuf = [c.sbuf("z_y0", [128, 4096], BF16)] * 2
    gbuf = [c.sbuf("z_g0", [128, 4096], BF16)] * 2
    sg = c.sbuf("z_sg", [128, 4096], BF16)
    u = c.sbuf("z_u", [128, 4096], F32)
    sq = c.sbuf("z_sq", [128, 4096], BF16)
    zst = c.sbuf("z_st", [128, 4], F32)
    ztile = [c.sbuf("z_zt0", [128, 4096], BF16)] * 2
    zstag = [c.sbuf("z_zs0", [128, 32, 128], BF16)] * 2

    for t in range(NT):
        b = t % 2
        yb, gb, zt, zs = ybuf[b], gbuf[b], ztile[b], zstag[b]
        P.emit("sync", lambda s, t=t, yb=yb: s.dma_start(
            out=yb[:], in_=h["y_tm_d"][ts_(t, 128), :]),
            reads=[(h["y_tm_d"], t)], writes=[yb], dma=True)
        P.emit("sync", lambda s, t=t, gb=gb: s.dma_start(
            out=gb[:], in_=h["gate_tm_d"][ts_(t, 128), :]),
            reads=[(h["gate_tm_d"], (nt, tc)) for nt in range(32)
                   for tc in range(4)], writes=[gb], dma=True)
        P.emit("scalar", lambda s, gb=gb: s.activation(
            out=sg[:], in_=gb[:], func=AF.Silu), reads=[gb], writes=[sg])
        P.emit("vector", lambda v, yb=yb: v.tensor_tensor(
            out=u[:], in0=yb[:], in1=sg[:], op=ALU.mult),
            reads=[yb, sg], writes=[u])
        P.emit("scalar", lambda s: s.activation(
            out=sq[:], in_=u[:], func=AF.Square), reads=[u], writes=[sq])
        P.emit("vector", lambda v: v.reduce_sum(
            out=zst[:, 0:1], in_=sq[:], axis=X), reads=[sq], writes=[(zst, 0)])
        P.emit("vector", lambda v: v.tensor_scalar(
            out=zst[:, 1:2], in0=zst[:, 0:1], scalar1=1.0 / 4096, scalar2=1e-5,
            op0=ALU.mult, op1=ALU.add), reads=[(zst, 0)], writes=[(zst, 1)])
        P.emit("vector", lambda v: v.reciprocal(
            out=zst[:, 2:3], in_=zst[:, 1:2]), reads=[(zst, 1)], writes=[(zst, 2)])
        P.emit("scalar", lambda s: s.activation(
            out=zst[:, 3:4], in_=zst[:, 2:3], func=AF.Ln),
            reads=[(zst, 2)], writes=[(zst, 3)])
        P.emit("scalar", lambda s: s.activation(
            out=zst[:, 3:4], in_=zst[:, 3:4], func=AF.Exp, scale=0.5),
            reads=[(zst, 3)], writes=[(zst, 3)])
        P.emit("vector", lambda v, zt=zt: v.tensor_scalar(
            out=zt[:], in0=u[:], scalar1=zst[:, 3:4], scalar2=None,
            op0=ALU.mult), reads=[u, (zst, 3)], writes=[zt])
        P.emit("scalar", lambda s, zt=zt, zs=zs: s.dma_start(
            out=zs[:], in_=zt[:], transpose=True),
            reads=[zt], writes=[zs], dma=True)
        P.emit("gpsimd", lambda g, zs=zs, t=t: g.dma_start(
            out=bass.AP(z_fm, t * 128, [[32 * T, 128], [T, 32], [1, 128]]),
            in_=zs[:]), reads=[zs], writes=[(z_fm, t)], dma=True)
    res["z_fm"] = z_fm
    res["_scratch3"] = [ybuf[0], gbuf[0], sg, u, sq, zst,
                        ztile[0], zstag[0]]
    return res


def emit_l0_wout(c, h, res):
    P = c.P
    z_fm = res["z_fm"]
    nw_sb = c.sbuf("wo_nw", [128, 32], F32)
    wdc = c.sbuf("wo_w", [128, 32, 256], BF16)
    x0b = [c.sbuf(f"wo_x{i}", [128, 256], F32) for i in range(2)]
    ob = [c.sbuf(f"wo_o{i}", [128, 256], F32) for i in range(2)]
    pp = c.pp

    P.emit("sync", lambda s: s.dma_start(
        out=nw_sb[:], in_=bass.AP(h["nw0"], 0, [[1, 128], [128, 32], [1, 1]]),
        allow_slow_non_contiguous=True),
        reads=[h["nw0"]], writes=[nw_sb], dma=True)

    for dc in range(8):
        P.emit("gpsimd", lambda g, dc=dc: g.dma_start(
            out=wdc[:], in_=bass.AP(h["w_out0"], dc * 256,
                                    [[D, 128], [128 * D, 32], [1, 256]])),
            reads=[h["w_out0"]], writes=[(wdc, "raw")], dma=True)
        for k in range(32):
            P.emit("vector", lambda v, k=k: v.tensor_scalar(
                out=wdc[:, k, :], in0=wdc[:, k, :], scalar1=nw_sb[:, k:k + 1],
                scalar2=None, op0=ALU.mult),
                reads=[(wdc, "raw"), nw_sb], writes=[(wdc, "raw"), (wdc, k)])
        for t in range(NT):
            bank = pp[t // 2]
            half = (t % 2) * 256
            for k in range(32):
                P.emit("tensor", lambda te, k=k, t=t, bank=bank, half=half:
                       te.matmul(
                    bank[:, half:half + 256],
                    z_fm[:, k * T + t * 128: k * T + (t + 1) * 128],
                    wdc[:, k, :], start=(k == 0), stop=(k == 31)),
                    reads=[(z_fm, t), (wdc, k)], writes=[bank])
            b2 = t % 2
            P.emit("sync", lambda s, t=t, dc=dc, xb=x0b[b2]: s.dma_start(
                out=xb[:], in_=h["x0"][ts_(t, 128), dc * 256:(dc + 1) * 256]),
                reads=[h["x0"]], writes=[x0b[b2]], dma=True)
            P.emit("vector", lambda v, bank=bank, half=half, xb=x0b[b2],
                   o=ob[b2]: v.tensor_tensor(
                out=o[:], in0=bank[:, half:half + 256], in1=xb[:], op=ALU.add),
                reads=[bank, x0b[b2]], writes=[ob[b2]])
            if "xf_d" in h:
                P.emit("sync", lambda s, t=t, dc=dc, xb=x0b[b2]: s.dma_start(
                    out=xb[:],
                    in_=h["xf_d"][ts_(t, 128), dc * 256:(dc + 1) * 256]),
                    reads=[h["xf_d"]], writes=[x0b[b2]], dma=True)
                P.emit("vector", lambda v, xb=x0b[b2], o=ob[b2]:
                       v.tensor_tensor(
                    out=o[:], in0=o[:], in1=xb[:], op=ALU.add),
                    reads=[ob[b2], x0b[b2]], writes=[ob[b2]])
            P.emit("gpsimd", lambda g, t=t, dc=dc, o=ob[b2]: g.dma_start(
                out=h["x1p_d"][ts_(t, 128), dc * 256:(dc + 1) * 256], in_=o[:]),
                reads=[ob[b2]], writes=[(h["x1p_d"], (t, dc))], dma=True)
    res["_scratch4"] = [nw_sb, wdc, x0b[0], x0b[1], ob[0], ob[1]]
    return res



def emit_exact_exp2(c, a_ap_fn, a_reads, out, out_key, tb, tf, ti, pc, W):
    P = c.P
    P.emit("vector", lambda v: v.tensor_copy(out=tb[0:pc, 0:W], in_=a_ap_fn()),
           reads=list(a_reads), writes=[tb])
    P.emit("vector", lambda v: v.tensor_copy(out=tf[0:pc, 0:W],
                                             in_=tb[0:pc, 0:W]),
           reads=[tb], writes=[tf])
    P.emit("vector", lambda v: v.tensor_copy(
        out=out[0:pc, 0:W], in_=tf.bitcast(I32)[0:pc, 0:W]),
        reads=[tf], writes=[out_key])
    P.emit("vector", lambda v: v.tensor_scalar(
        out=out[0:pc, 0:W], in0=out[0:pc, 0:W], scalar1=2.0 ** -23,
        scalar2=-127.0, op0=ALU.mult, op1=ALU.add),
        reads=[out_key], writes=[out_key])
    P.emit("vector", lambda v: v.tensor_copy(out=ti[0:pc, 0:W],
                                             in_=out[0:pc, 0:W]),
           reads=[out_key], writes=[ti])
    P.emit("vector", lambda v: v.tensor_copy(out=tf[0:pc, 0:W],
                                             in_=ti[0:pc, 0:W]),
           reads=[ti], writes=[tf])
    P.emit("vector", lambda v: v.tensor_tensor(
        out=out[0:pc, 0:W], in0=tf[0:pc, 0:W], in1=out[0:pc, 0:W],
        op=ALU.is_gt), reads=[tf, out_key], writes=[out_key])
    P.emit("vector", lambda v: v.tensor_tensor(
        out=out[0:pc, 0:W], in0=tf[0:pc, 0:W], in1=out[0:pc, 0:W],
        op=ALU.subtract), reads=[tf, out_key], writes=[out_key])
    P.emit("vector", lambda v: v.tensor_scalar(
        out=tf[0:pc, 0:W], in0=out[0:pc, 0:W], scalar1=float(2 ** 23),
        scalar2=float(127 * 2 ** 23), op0=ALU.mult, op1=ALU.add),
        reads=[out_key], writes=[tf])
    P.emit("vector", lambda v: v.tensor_copy(out=ti[0:pc, 0:W],
                                             in_=tf[0:pc, 0:W]),
           reads=[tf], writes=[ti])
    P.emit("vector", lambda v: v.tensor_tensor(
        out=tf[0:pc, 0:W], in0=ti.bitcast(F32)[0:pc, 0:W], in1=a_ap_fn(),
        op=ALU.is_gt), reads=[ti] + list(a_reads), writes=[tf])
    P.emit("vector", lambda v: v.tensor_tensor(
        out=out[0:pc, 0:W], in0=out[0:pc, 0:W], in1=tf[0:pc, 0:W],
        op=ALU.subtract), reads=[out_key, tf], writes=[out_key])


def mk_mx_scratch(c, W, tag, rne=False, share_qb=None):
    nb = W // 32
    scr = {
        "qs": c.sbuf(tag + "_qs", [128, W], F32),
        "qp": c.sbuf(tag + "_qp", [128, W], F32),
        "qe": c.sbuf(tag + "_qe", [128, W], F32),
        "qm": c.sbuf(tag + "_qm", [128, W], F32),
        "qi": c.sbuf(tag + "_qi", [128, W], I32),
        "am": c.sbuf(tag + "_am", [128, nb], F32),
        "lnm": c.sbuf(tag + "_lnm", [128, nb], F32),
        "sef": c.sbuf(tag + "_sef", [128, nb], F32),
        "sei": c.sbuf(tag + "_sei", [128, nb], I32),
        "ivs": c.sbuf(tag + "_ivs", [128, nb], F32),
        "sv": c.sbuf(tag + "_sv", [128, nb], F32),
        "amb": c.sbuf(tag + "_amb", [128, nb], BF16),
        "W": W,
        "list": None,
    }
    if rne:
        scr["qb"] = share_qb if share_qb is not None else c.sbuf(
            tag + "_qb", [128, W], BF16)
        scr["qx"] = c.sbuf(tag + "_qx", [128, W], I32)
        scr["qb_shared"] = share_qb is not None
    return scr


def mx_scratch_list(scr):
    ks = ["qs", "qp", "qe", "qm", "qi", "am", "lnm", "sef", "sei", "ivs",
          "sv", "amb"]
    if "qb" in scr:
        ks += ["qb", "qx"]
    if scr.get("qb_shared"):
        ks.remove("qb")
    return [scr[k] for k in ks]


def emit_mx_quant(c, src, src_key, dst, dst_key, scr, mfrac, emax, maxn,
                  pcount=128, src_off=0, dst_off=0, rne=False):
    P = c.P
    W = scr["W"]
    nb = W // 32
    qs, qp, qe, qm, qi = scr["qs"], scr["qp"], scr["qe"], scr["qm"], scr["qi"]
    am, lnm, sef, sei = scr["am"], scr["lnm"], scr["sef"], scr["sei"]
    ivs, sv, amb = scr["ivs"], scr["sv"], scr["amb"]
    pc = pcount
    bcst_i = lambda: bass.AP(sei.bitcast(F32), 0, [[nb, pc], [1, nb], [0, 32]])
    for v in (mfrac * LN2, -mfrac * LN2):
        c.const(v)
    srcK = [src_key] if not isinstance(src_key, list) else src_key

    sw = src.shape[1]
    P.emit("vector", lambda v: v.tensor_reduce(
        out=am[0:pc, :], in_=bass.AP(src, src_off, [[sw, pc], [32, nb], [1, 32]]),
        axis=X, op=ALU.max, apply_absolute_value=True),
        reads=srcK, writes=[am])
    P.emit("vector", lambda v: v.tensor_scalar(
        out=am[0:pc, :], in0=am[0:pc, :], scalar1=2.0 ** -30, scalar2=None,
        op0=ALU.max), reads=[am], writes=[am])
    emit_exact_exp2(c, lambda: am[0:pc, :], [am], sef, sef, amb, lnm, sei,
                    pc, nb)
    P.emit("vector", lambda v: v.tensor_scalar(
        out=ivs[0:pc, :], in0=sef[0:pc, :], scalar1=-float(2 ** 23),
        scalar2=float((127 + emax) * 2 ** 23), op0=ALU.mult, op1=ALU.add),
        reads=[sef], writes=[ivs])
    P.emit("vector", lambda v: v.tensor_copy(out=sei[0:pc, :], in_=ivs[0:pc, :]),
           reads=[ivs], writes=[sei])
    P.emit("vector", lambda v: v.tensor_tensor(
        out=qs[0:pc, :], in0=bass.AP(src, src_off, [[sw, pc], [1, W]]),
        in1=bcst_i(), op=ALU.mult), reads=srcK + [sei], writes=[qs])

    if not rne:
        P.emit("scalar", lambda s: s.activation(
            out=qp[0:pc, :], in_=qs[0:pc, :], func=AF.Square),
            reads=[qs], writes=[qp])
        P.emit("vector", lambda v: v.tensor_scalar(
            out=qp[0:pc, :], in0=qp[0:pc, :], scalar1=1e-30, scalar2=None,
            op0=ALU.max), reads=[qp], writes=[qp])
        P.emit("scalar", lambda s: s.activation(
            out=qp[0:pc, :], in_=qp[0:pc, :], func=AF.Ln),
            reads=[qp], writes=[qp])
        P.emit("vector", lambda v: v.tensor_scalar(
            out=qp[0:pc, :], in0=qp[0:pc, :], scalar1=1.0 / (2 * LN2),
            scalar2=1e-3, op0=ALU.mult, op1=ALU.add), reads=[qp], writes=[qp])
        P.emit("vector", lambda v: v.tensor_scalar(
            out=qp[0:pc, :], in0=qp[0:pc, :], scalar1=0.0, scalar2=None,
            op0=ALU.max), reads=[qp], writes=[qp])
        P.emit("vector", lambda v: v.tensor_copy(out=qi[0:pc, :],
                                                 in_=qp[0:pc, :]),
               reads=[qp], writes=[qi])
        P.emit("vector", lambda v: v.tensor_copy(out=qe[0:pc, :],
                                                 in_=qi[0:pc, :]),
               reads=[qi], writes=[qe])
        P.emit("vector", lambda v: v.tensor_tensor(
            out=qm[0:pc, :], in0=qe[0:pc, :], in1=qp[0:pc, :], op=ALU.is_gt),
            reads=[qe, qp], writes=[qm])
        P.emit("vector", lambda v: v.tensor_tensor(
            out=qp[0:pc, :], in0=qe[0:pc, :], in1=qm[0:pc, :],
            op=ALU.subtract), reads=[qe, qm], writes=[qp])
        P.emit("scalar", lambda s: s.activation(
            out=qe[0:pc, :], in_=qp[0:pc, :], func=AF.Exp, scale=-LN2,
            bias=c.const(mfrac * LN2)),
            reads=[qp, c.cread(mfrac * LN2)], writes=[qe])
        P.emit("vector", lambda v: v.tensor_tensor(
            out=qs[0:pc, :], in0=qs[0:pc, :], in1=qe[0:pc, :], op=ALU.mult),
            reads=[qs, qe], writes=[qs])
        P.emit("vector", lambda v: v.tensor_scalar(
            out=qs[0:pc, :], in0=qs[0:pc, :], scalar1=0.5, scalar2=None,
            op0=ALU.add), reads=[qs], writes=[qs])
        P.emit("vector", lambda v: v.tensor_copy(out=qi[0:pc, :],
                                                 in_=qs[0:pc, :]),
               reads=[qs], writes=[qi])
        P.emit("vector", lambda v: v.tensor_copy(out=qe[0:pc, :],
                                                 in_=qi[0:pc, :]),
               reads=[qi], writes=[qe])
        P.emit("vector", lambda v: v.tensor_tensor(
            out=qm[0:pc, :], in0=qe[0:pc, :], in1=qs[0:pc, :], op=ALU.is_gt),
            reads=[qe, qs], writes=[qm])
        P.emit("vector", lambda v: v.tensor_tensor(
            out=qs[0:pc, :], in0=qe[0:pc, :], in1=qm[0:pc, :],
            op=ALU.subtract), reads=[qe, qm], writes=[qs])
        P.emit("scalar", lambda s: s.activation(
            out=qe[0:pc, :], in_=qp[0:pc, :], func=AF.Exp, scale=LN2,
            bias=c.const(-mfrac * LN2)),
            reads=[qp, c.cread(-mfrac * LN2)], writes=[qe])
        P.emit("vector", lambda v: v.tensor_tensor(
            out=qs[0:pc, :], in0=qs[0:pc, :], in1=qe[0:pc, :], op=ALU.mult),
            reads=[qs, qe], writes=[qs])
        P.emit("vector", lambda v: v.tensor_scalar(
            out=qs[0:pc, :], in0=qs[0:pc, :], scalar1=float(maxn),
            scalar2=float(-maxn), op0=ALU.min, op1=ALU.max),
            reads=[qs], writes=[qs])
        qfin = qs
    else:
        qb, qx = scr["qb"], scr["qx"]
        P.emit("scalar", lambda s: s.activation(
            out=qe[0:pc, :], in_=qs[0:pc, :], func=AF.Abs),
            reads=[qs], writes=[qe])
        P.emit("vector", lambda v: v.tensor_scalar(
            out=qe[0:pc, :], in0=qe[0:pc, :], scalar1=2.0 ** -40, scalar2=None,
            op0=ALU.max), reads=[qe], writes=[qe])
        emit_exact_exp2(c, lambda: qe[0:pc, :], [qe], qp, qp, qb, qm, qi,
                        pc, W)
        P.emit("vector", lambda v: v.tensor_scalar(
            out=qp[0:pc, :], in0=qp[0:pc, :], scalar1=0.0, scalar2=None,
            op0=ALU.max), reads=[qp], writes=[qp])
        P.emit("vector", lambda v: v.tensor_scalar(
            out=qm[0:pc, :], in0=qp[0:pc, :], scalar1=-float(2 ** 23),
            scalar2=float((127 + mfrac) * 2 ** 23), op0=ALU.mult, op1=ALU.add),
            reads=[qp], writes=[qm])
        P.emit("vector", lambda v: v.tensor_copy(out=qx[0:pc, :],
                                                 in_=qm[0:pc, :]),
               reads=[qm], writes=[qx])
        P.emit("vector", lambda v: v.tensor_tensor(
            out=qs[0:pc, :], in0=qs[0:pc, :],
            in1=qx.bitcast(F32)[0:pc, 0:W], op=ALU.mult),
            reads=[qs, qx], writes=[qs])
        P.emit("vector", lambda v: v.tensor_scalar(
            out=qe[0:pc, :], in0=qs[0:pc, :], scalar1=0.5, scalar2=None,
            op0=ALU.add), reads=[qs], writes=[qe])
        P.emit("vector", lambda v: v.tensor_copy(out=qi[0:pc, :],
                                                 in_=qe[0:pc, :]),
               reads=[qe], writes=[qi])
        P.emit("vector", lambda v: v.tensor_copy(out=qm[0:pc, :],
                                                 in_=qi[0:pc, :]),
               reads=[qi], writes=[qm])
        P.emit("vector", lambda v: v.tensor_tensor(
            out=qe[0:pc, :], in0=qm[0:pc, :], in1=qe[0:pc, :], op=ALU.is_gt),
            reads=[qm, qe], writes=[qe])
        P.emit("vector", lambda v: v.tensor_tensor(
            out=qm[0:pc, :], in0=qm[0:pc, :], in1=qe[0:pc, :],
            op=ALU.subtract), reads=[qm, qe], writes=[qm])
        P.emit("vector", lambda v: v.scalar_tensor_tensor(
            out=qe[0:pc, :], in0=qm[0:pc, :], scalar=0.5, in1=qs[0:pc, :],
            op0=ALU.subtract, op1=ALU.is_equal),
            reads=[qm, qs], writes=[qe])
        P.emit("vector", lambda v: v.tensor_scalar(
            out=qs[0:pc, :], in0=qp[0:pc, :], scalar1=float(2 ** 23),
            scalar2=float((127 - mfrac) * 2 ** 23), op0=ALU.mult, op1=ALU.add),
            reads=[qp], writes=[qs])
        P.emit("vector", lambda v: v.tensor_copy(out=qx[0:pc, :],
                                                 in_=qs[0:pc, :]),
               reads=[qs], writes=[qx])
        P.emit("vector", lambda v: v.tensor_scalar(
            out=qp[0:pc, :], in0=qm[0:pc, :], scalar1=0.5, scalar2=None,
            op0=ALU.mult), reads=[qm], writes=[qp])
        P.emit("vector", lambda v: v.tensor_copy(out=qi[0:pc, :],
                                                 in_=qp[0:pc, :]),
               reads=[qp], writes=[qi])
        P.emit("vector", lambda v: v.tensor_copy(out=qs[0:pc, :],
                                                 in_=qi[0:pc, :]),
               reads=[qi], writes=[qs])
        P.emit("vector", lambda v: v.tensor_tensor(
            out=qs[0:pc, :], in0=qs[0:pc, :], in1=qp[0:pc, :],
            op=ALU.subtract), reads=[qs, qp], writes=[qs])
        P.emit("scalar", lambda s: s.activation(
            out=qs[0:pc, :], in_=qs[0:pc, :], func=AF.Square, scale=2.0),
            reads=[qs], writes=[qs])
        P.emit("vector", lambda v: v.tensor_tensor(
            out=qe[0:pc, :], in0=qe[0:pc, :], in1=qs[0:pc, :], op=ALU.mult),
            reads=[qe, qs], writes=[qe])
        P.emit("vector", lambda v: v.tensor_tensor(
            out=qm[0:pc, :], in0=qm[0:pc, :], in1=qe[0:pc, :],
            op=ALU.subtract), reads=[qm, qe], writes=[qm])
        P.emit("vector", lambda v: v.tensor_tensor(
            out=qm[0:pc, :], in0=qm[0:pc, :],
            in1=qx.bitcast(F32)[0:pc, 0:W], op=ALU.mult),
            reads=[qm, qx], writes=[qm])
        P.emit("vector", lambda v: v.tensor_scalar(
            out=qm[0:pc, :], in0=qm[0:pc, :], scalar1=float(maxn),
            scalar2=float(-maxn), op0=ALU.min, op1=ALU.max),
            reads=[qm], writes=[qm])
        qfin = qm

    P.emit("vector", lambda v: v.tensor_scalar(
        out=sv[0:pc, :], in0=sef[0:pc, :], scalar1=float(2 ** 23),
        scalar2=float((127 - emax) * 2 ** 23), op0=ALU.mult, op1=ALU.add),
        reads=[sef], writes=[sv])
    P.emit("vector", lambda v: v.tensor_copy(out=sei[0:pc, :], in_=sv[0:pc, :]),
           reads=[sv], writes=[sei])
    dw = dst.shape[1]
    P.emit("vector", lambda v, qfin=qfin: v.tensor_tensor(
        out=bass.AP(dst, dst_off, [[dw, pc], [1, W]]), in0=qfin[0:pc, :],
        in1=bcst_i(), op=ALU.mult),
        reads=[qfin, sei], writes=[dst_key])



def emit_l0_ffn(c, xnq_fm, h, res):
    P = c.P
    pp = c.pp
    scrA = mk_mx_scratch(c, 512, "f0qA", rne=True)
    scrs = [scrA, mk_mx_scratch(c, 512, "f0qB", rne=True,
                                share_qb=scrA["qb"])]
    W2 = scrs[0]["W"]
    qcnt = [0]
    wgq = c.sbuf("f0_wgq", [128, 4096], BF16)
    wuq = c.sbuf("f0_wuq", [128, 4096], BF16)
    wq2 = c.sbuf("f0_wq2", [128, 2048], BF16)
    gate_sb = c.sbuf("f0_gate", [128, 4096], BF16)
    sg = c.sbuf("f0_sg", [128, 256], BF16)
    inter_sb = c.sbuf("f0_int", [128, 2048], F32)
    intq = wq2
    tstag = c.sbuf("f0_tst", [128, 2048], BF16)
    interq_q = c.sbuf("f0_iq", [128, 16 * T], BF16, side="right")
    res["interq_q"] = interq_q
    wdall = c.sbuf("f0_wda", [128, 16, 256], BF16)
    wdt = [None, None]
    x1b = [c.sbuf("f0_x10", [128, 256], F32)] * 2
    ob = [c.sbuf("f0_o0", [128, 256], F32)] * 2
    I0 = 8192

    def quant_band(src_rows, wh, colw, dst, dst_key, mfrac, maxn):
        for cc in range(colw // W2):
            scr = scrs[qcnt[0] % 2]
            qcnt[0] += 1
            wraw = scr["qp"]
            P.emit("sync", lambda s, cc=cc, src_rows=src_rows, wh=wh,
                   wraw=wraw, scr=scr: s.dma_start(
                out=wraw[:, :],
                in_=wh[src_rows:src_rows + 128, cc * W2:(cc + 1) * W2]),
                reads=[wh], writes=[scr["qp"]], dma=True)
            emit_mx_quant(c, wraw, scr["qp"], dst, dst_key, scr,
                          mfrac=mfrac, emax=2.0, maxn=maxn, dst_off=cc * W2,
                          rne=True)

    for hb in range(16):
        for cc in range(4):
            quant_band(hb * 128, h["wd"], 0, wq2, wq2, 1.0, 6.0) if False else None
            for sub in range(2048 // W2):
                scr = scrs[qcnt[0] % 2]
                qcnt[0] += 1
                wraw = scr["qp"]
                P.emit("sync", lambda s, hb=hb, cc=cc, sub=sub, wraw=wraw,
                       scr=scr: s.dma_start(
                    out=wraw[:, :],
                    in_=h["wd"][ts_(hb, 128),
                                cc * 2048 + sub * W2:cc * 2048 + (sub + 1) * W2]),
                    reads=[h["wd"]], writes=[scr["qp"]], dma=True)
                emit_mx_quant(c, wraw, scr["qp"], wq2, wq2, scr,
                              mfrac=1.0, emax=2.0, maxn=6.0, dst_off=sub * W2,
                              rne=True)
            P.emit("scalar", lambda s: s.dma_start(
                out=bass.AP(tstag, 0, [[2048, 128], [128, 16], [1, 128]]),
                in_=wq2[:], transpose=True),
                reads=[wq2], writes=[(tstag, "w")], dma=True)
            P.emit("sync", lambda s, hb=hb, cc=cc: s.dma_start(
                out=bass.AP(h["wdq_fm_d"], cc * 2048 * D + hb * 128,
                            [[D, 128], [128 * D, 16], [1, 128]]),
                in_=bass.AP(tstag, 0, [[2048, 128], [128, 16], [1, 128]])),
                reads=[(tstag, "w")], writes=[(h["wdq_fm_d"], (hb, cc))],
                dma=True)

    for p4 in range(4):
        for n in range(8):
            ibase = p4 * 2048 + n * 256
            for wh, dst in ((h["wg"], wgq), (h["wu"], wuq)):
                for band in range(2):
                    quant_band(ibase + band * 128, wh, 2048, wq2, wq2,
                               1.0, 6.0)
                    P.emit("scalar", lambda s, dst=dst, band=band: s.dma_start(
                        out=bass.AP(dst, band * 128,
                                    [[4096, 128], [256, 16], [1, 128]]),
                        in_=wq2[:], transpose=True),
                        reads=[wq2], writes=[(dst, band)], dma=True)
            for t in range(NT):
                bank, half = pp[t // 2], (t % 2) * 256
                for k in range(16):
                    P.emit("tensor", lambda te, t=t, k=k, bank=bank, half=half:
                           te.matmul(
                        bank[:, half:half + 256],
                        xnq_fm[:, k * T + t * 128:k * T + (t + 1) * 128],
                        wgq[:, k * 256:(k + 1) * 256],
                        start=(k == 0), stop=(k == 15)),
                        reads=[(xnq_fm, t), (wgq, 0), (wgq, 1)],
                        writes=[bank])
                P.emit("scalar", lambda s, t=t, bank=bank, half=half: s.copy(
                    out=gate_sb[:, t * 256:(t + 1) * 256],
                    in_=bank[:, half:half + 256]),
                    reads=[bank], writes=[(gate_sb, t)])
            for th in range(2):
                for tt in range(8):
                    t = th * 8 + tt
                    bank, half = pp[t // 2], (t % 2) * 256
                    for k in range(16):
                        P.emit("tensor", lambda te, t=t, k=k, bank=bank,
                               half=half: te.matmul(
                            bank[:, half:half + 256],
                            xnq_fm[:, k * T + t * 128:k * T + (t + 1) * 128],
                            wuq[:, k * 256:(k + 1) * 256],
                            start=(k == 0), stop=(k == 15)),
                            reads=[(xnq_fm, t), (wuq, 0), (wuq, 1)],
                            writes=[bank])
                    P.emit("scalar", lambda s, t=t: s.activation(
                        out=sg[:], in_=gate_sb[:, t * 256:(t + 1) * 256],
                        func=AF.Silu), reads=[(gate_sb, t)], writes=[sg])
                    P.emit("vector", lambda v, tt=tt, bank=bank, half=half:
                           v.tensor_tensor(
                        out=inter_sb[:, tt * 256:(tt + 1) * 256], in0=sg[:],
                        in1=bank[:, half:half + 256], op=ALU.mult),
                        reads=[sg, bank], writes=[(inter_sb, tt)])
                for sub in range(2048 // W2):
                    scr = scrs[qcnt[0] % 2]
                    qcnt[0] += 1
                    emit_mx_quant(c, inter_sb,
                                  [(inter_sb, tt) for tt in range(8)],
                                  intq, intq, scr, mfrac=3.0, emax=2.0,
                                  maxn=7.5, src_off=sub * W2,
                                  dst_off=sub * W2)
                P.emit("scalar", lambda s: s.dma_start(
                    out=bass.AP(tstag, 0, [[2048, 128], [128, 16], [1, 128]]),
                    in_=intq[:], transpose=True),
                    reads=[intq], writes=[(tstag, "w")], dma=True)
                for ih in range(2):
                    P.emit("scalar", lambda s, n=n, th=th, ih=ih: s.dma_start(
                        out=bass.AP(interq_q, (2 * n + ih) * T + th * 1024,
                                    [[16 * T, 128], [128, 8], [1, 128]]),
                        in_=bass.AP(tstag, ih * 128,
                                    [[2048, 128], [256, 8], [1, 128]])),
                        reads=[(tstag, "w")],
                        writes=[(interq_q, (2 * n + ih, th))], dma=True)
        src_d = None if p4 == 0 else (h["x1a_d"] if p4 == 1 else
                                      (h["x1b_d"] if p4 == 2 else h["x1a_d"]))
        dst_d = h["x1a_d"] if p4 == 0 else (h["x1b_d"] if p4 == 1 else
                                            (h["x1a_d"] if p4 == 2 else h["x1_d"]))
        for hc in range(8):
            P.emit("sync", lambda s, p4=p4, hc=hc: s.dma_start(
                out=wdall[:], in_=bass.AP(
                    h["wdq_fm_d"], p4 * 16 * 128 * D + hc * 256,
                    [[D, 128], [128 * D, 16], [1, 256]])),
                reads=[(h["wdq_fm_d"], (hb, cc)) for hb in range(16)
                       for cc in range(4)],
                writes=[wdall], dma=True)
            for t in range(NT):
                bank, half = pp[t // 2], (t % 2) * 256
                b2 = t % 2
                for k in range(16):
                    P.emit("tensor", lambda te, t=t, k=k, n2=None, bank=bank,
                           half=half: te.matmul(
                        bank[:, half:half + 256],
                        interq_q[:, k * T + t * 128:k * T + (t + 1) * 128],
                        wdall[:, k, :], start=(k == 0), stop=(k == 15)),
                        reads=[(interq_q, (k, t // 8)), wdall], writes=[bank])
                if p4 == 0:
                    P.emit("scalar", lambda s, bank=bank, half=half,
                           o=ob[b2]: s.copy(
                        out=o[:], in_=bank[:, half:half + 256]),
                        reads=[bank], writes=[ob[b2]])
                else:
                    P.emit("sync", lambda s, t=t, hc=hc, xb=x1b[b2], sd=src_d:
                           s.dma_start(
                        out=xb[:], in_=sd[ts_(t, 128), hc * 256:(hc + 1) * 256]),
                        reads=[(src_d, (t, hc))], writes=[x1b[b2]], dma=True)
                    P.emit("vector", lambda v, bank=bank, half=half,
                           xb=x1b[b2], o=ob[b2]: v.tensor_tensor(
                        out=o[:], in0=bank[:, half:half + 256], in1=xb[:],
                        op=ALU.add), reads=[bank, x1b[b2]], writes=[ob[b2]])
                P.emit("gpsimd", lambda g, t=t, hc=hc, o=ob[b2], dd=dst_d:
                       g.dma_start(
                    out=dd[ts_(t, 128), hc * 256:(hc + 1) * 256], in_=o[:]),
                    reads=[ob[b2]], writes=[(dst_d, (t, hc))], dma=True)
    res["_scratch5"] = (mx_scratch_list(scrs[0]) + mx_scratch_list(scrs[1])
        + [wgq, wuq, wq2, gate_sb, sg, inter_sb, tstag, wdall,
           x1b[0], ob[0]])
    return res



CONV1 = 8192
VDIM1 = 4096


def emit_l1_proj(c, xn_fm, h, res):
    P = c.P
    pp = c.pp
    wbuf = [c.sbuf(f"l1w{i}", [128, 16, 128], BF16) for i in range(2)]
    xcbuf = c.sbuf("l1xc", [128, 2051], F32)
    cacc = c.sbuf("l1cacc", [128, T], F32)
    ctile = [c.sbuf(f"l1ct{i}", [128, T], BF16) for i in range(2)]
    cw_sb = c.sbuf("l1cw", [128, 64, 4], F32)
    tstag = [c.sbuf(f"l1ts{i}", [128, 16, 128], BF16) for i in range(2)]
    sqt = c.sbuf("l1sq", [128, T], F32)
    nrow = c.sbuf("l1nr", [1, T], F32)
    nrep = c.sbuf("l1nrep", [128, T], F32)
    ones1 = c.sbuf("l1one", [1, 128], F32)
    onescol = c.sbuf("l1onec", [128, 1], F32)
    ntile = c.sbuf("l1nt", [128, T], BF16)
    ba_sb = c.sbuf("l1ba", [64, T], F32)
    dtb_sb = c.sbuf("l1dtb", [32, 1], F32)
    alog_sb = c.sbuf("l1alog", [32, 1], F32)
    cg = c.sbuf("l1cg", [32, T], F32, side="right")
    cgb = c.sbuf("l1cgb", [32, T], F32, side="right")
    beta_f = c.sbuf("l1beta", [32, T], F32, side="right")
    gtmp = c.sbuf("l1gtmp", [32, T], F32)
    gtmp2 = c.sbuf("l1gt2", [32, T], F32)
    res.update(cg=cg, cgb=cgb, beta_f=beta_f)

    P.emit("sync", lambda s: s.dma_start(
        out=cw_sb[:], in_=bass.AP(h["cw1"], 0, [[4, 128], [128 * 4, 64], [1, 4]])),
        reads=[h["cw1"]], writes=[cw_sb], dma=True)
    P.emit("sync", lambda s: s.dma_start(out=dtb_sb[:], in_=h["dtb1"][:, :]),
           reads=[h["dtb1"]], writes=[dtb_sb], dma=True)
    P.emit("sync", lambda s: s.dma_start(out=alog_sb[:], in_=h["alog1"][:, :]),
           reads=[h["alog1"]], writes=[alog_sb], dma=True)
    P.emit("gpsimd", lambda g: g.memset(xcbuf[:, 0:3], 0.0), writes=[xcbuf])
    P.emit("gpsimd", lambda g: g.memset(ones1[:], 1.0), writes=[ones1])
    P.emit("gpsimd", lambda g: g.memset(onescol[:], 1.0), writes=[onescol])

    for nt in range(64):
        b = nt % 2
        wb = wbuf[b]
        P.emit("gpsimd", lambda g, nt=nt, wb=wb: g.dma_start(
            out=wb[:], in_=bass.AP(h["w_qkvT"], nt * 128,
                                   [[CONV1, 128], [128 * CONV1, 16], [1, 128]])),
            reads=[h["w_qkvT"]], writes=[wb], dma=True)
        for tc in range(4):
            pb = pp[tc % 2]
            for k in range(16):
                P.emit("tensor", lambda te, wb=wb, k=k, tc=tc, pb=pb:
                       te.matmul(pb[:, :], wb[:, k, :],
                                 xn_fm[:, k * T + tc * 512:k * T + (tc + 1) * 512],
                                 start=(k == 0), stop=(k == 15)),
                       reads=[wb, (xn_fm, "all")], writes=[pb])
            P.emit("scalar", lambda s, pb=pb, tc=tc: s.copy(
                out=xcbuf[:, 3 + tc * 512:3 + (tc + 1) * 512], in_=pb[:]),
                reads=[pb], writes=[xcbuf])
        ct = ctile[b]
        P.emit("vector", lambda v, nt=nt: v.tensor_scalar(
            out=cacc[:], in0=xcbuf[:, 0:T], scalar1=cw_sb[:, nt, 0:1],
            scalar2=None, op0=ALU.mult), reads=[xcbuf, cw_sb], writes=[cacc])
        for k in range(1, 4):
            P.emit("vector", lambda v, nt=nt, k=k: v.scalar_tensor_tensor(
                out=cacc[:], in0=xcbuf[:, k:k + T],
                scalar=cw_sb[:, nt, k:k + 1], in1=cacc[:],
                op0=ALU.mult, op1=ALU.add),
                reads=[xcbuf, cw_sb, cacc], writes=[cacc])
        P.emit("scalar", lambda s, ct=ct: s.activation(
            out=ct[:], in_=cacc[:], func=AF.Silu), reads=[cacc], writes=[ct])
        if nt < 32:
            P.emit("scalar", lambda s, ct=ct: s.activation(
                out=sqt[:], in_=ct[:], func=AF.Square), reads=[ct], writes=[sqt])
            for tc in range(4):
                P.emit("tensor", lambda te, tc=tc: te.matmul(
                    pp[2][0:1, 0:512], onescol[:],
                    sqt[:, tc * 512:(tc + 1) * 512], start=True, stop=True),
                    reads=[sqt, onescol], writes=[pp[2]])
                P.emit("scalar", lambda s, tc=tc: s.copy(
                    out=nrow[0:1, tc * 512:(tc + 1) * 512],
                    in_=pp[2][0:1, 0:512]), reads=[pp[2]], writes=[nrow])
            P.emit("vector", lambda v: v.tensor_scalar(
                out=nrow[:], in0=nrow[:], scalar1=1e-24, scalar2=None,
                op0=ALU.max), reads=[nrow], writes=[nrow])
            P.emit("scalar", lambda s: s.activation(
                out=nrow[:], in_=nrow[:], func=AF.Ln), reads=[nrow], writes=[nrow])
            P.emit("scalar", lambda s: s.activation(
                out=nrow[:], in_=nrow[:], func=AF.Exp, scale=-0.5),
                reads=[nrow], writes=[nrow])
            for tc in range(4):
                P.emit("tensor", lambda te, tc=tc: te.matmul(
                    pp[3][:, 0:512], ones1[:],
                    nrow[0:1, tc * 512:(tc + 1) * 512], start=True, stop=True),
                    reads=[nrow, ones1], writes=[pp[3]])
                P.emit("scalar", lambda s, tc=tc: s.copy(
                    out=nrep[:, tc * 512:(tc + 1) * 512], in_=pp[3][:, 0:512]),
                    reads=[pp[3]], writes=[nrep])
            P.emit("vector", lambda v, ct=ct: v.tensor_tensor(
                out=ntile[:], in0=ct[:], in1=nrep[:], op=ALU.mult),
                reads=[ct, nrep], writes=[ntile])
            dst_fm = h["qn_fm_d"] if nt < 16 else h["kn_fm_d"]
            jj = nt if nt < 16 else nt - 16
            P.emit("sync", lambda s, jj=jj, dst=dst_fm: s.dma_start(
                out=dst[ts_(jj, 128), :], in_=ntile[:]),
                reads=[ntile], writes=[(dst_fm, jj)], dma=True)
            if nt >= 16:
                tst = tstag[b]
                P.emit("scalar", lambda s, tst=tst: s.dma_start(
                    out=tst[:], in_=ntile[:], transpose=True),
                    reads=[ntile], writes=[tst], dma=True)
                P.emit("sync", lambda s, tst=tst, jj=jj: s.dma_start(
                    out=bass.AP(h["kn_tm_d"], jj * 128,
                                [[2048, 128], [128 * 2048, 16], [1, 128]]),
                    in_=tst[:]), reads=[tst], writes=[(h["kn_tm_d"], jj)],
                    dma=True)
        else:
            jj = nt - 32
            tst = tstag[b]
            P.emit("scalar", lambda s, ct=ct, tst=tst: s.dma_start(
                out=tst[:], in_=ct[:], transpose=True),
                reads=[ct], writes=[tst], dma=True)
            P.emit("sync", lambda s, tst=tst, jj=jj: s.dma_start(
                out=bass.AP(h["v_tm_d"], jj * 128,
                            [[VDIM1, 128], [128 * VDIM1, 16], [1, 128]]),
                in_=tst[:]), reads=[tst], writes=[(h["v_tm_d"], jj)], dma=True)

    for nt in range(32):
        b = nt % 2
        wb = wbuf[b]
        P.emit("gpsimd", lambda g, nt=nt, wb=wb: g.dma_start(
            out=wb[:], in_=bass.AP(h["w_zT"], nt * 128,
                                   [[VDIM1, 128], [128 * VDIM1, 16], [1, 128]])),
            reads=[h["w_zT"]], writes=[wb], dma=True)
        ct = ctile[b]
        for tc in range(4):
            pb = pp[tc % 2]
            for k in range(16):
                P.emit("tensor", lambda te, wb=wb, k=k, tc=tc, pb=pb:
                       te.matmul(pb[:, :], wb[:, k, :],
                                 xn_fm[:, k * T + tc * 512:k * T + (tc + 1) * 512],
                                 start=(k == 0), stop=(k == 15)),
                       reads=[wb, (xn_fm, "all")], writes=[pb])
            P.emit("scalar", lambda s, pb=pb, tc=tc, ct=ct: s.copy(
                out=ct[:, tc * 512:(tc + 1) * 512], in_=pb[:]),
                reads=[pb], writes=[ct])
        tst = tstag[b]
        P.emit("scalar", lambda s, ct=ct, tst=tst: s.dma_start(
            out=tst[:], in_=ct[:], transpose=True),
            reads=[ct], writes=[tst], dma=True)
        P.emit("sync", lambda s, tst=tst, nt=nt: s.dma_start(
            out=bass.AP(h["z1_tm_d"], nt * 128,
                        [[VDIM1, 128], [128 * VDIM1, 16], [1, 128]]),
            in_=tst[:]), reads=[tst], writes=[(h["z1_tm_d"], nt)], dma=True)

    P.emit("gpsimd", lambda g: g.dma_start(
        out=wbuf[0][:, :, 0:64],
        in_=bass.AP(h["w_baT"], 0, [[64, 128], [128 * 64, 16], [1, 64]])),
        reads=[h["w_baT"]], writes=[wbuf[0]], dma=True)
    for tc in range(4):
        pb = pp[tc % 2]
        for k in range(16):
            P.emit("tensor", lambda te, k=k, tc=tc, pb=pb: te.matmul(
                pb[0:64, :], wbuf[0][:, k, 0:64],
                xn_fm[:, k * T + tc * 512:k * T + (tc + 1) * 512],
                start=(k == 0), stop=(k == 15)),
                reads=[wbuf[0], (xn_fm, "all")], writes=[pb])
        P.emit("scalar", lambda s, pb=pb, tc=tc: s.copy(
            out=ba_sb[:, tc * 512:(tc + 1) * 512], in_=pb[0:64, :]),
            reads=[pb], writes=[ba_sb])
    P.emit("scalar", lambda s: s.activation(
        out=beta_f[:], in_=ba_sb[0:32, :], func=AF.Sigmoid),
        reads=[ba_sb], writes=[beta_f])
    P.emit("scalar", lambda s: s.activation(
        out=gtmp2[:], in_=beta_f[:], func=AF.Ln),
        reads=[beta_f], writes=[gtmp2])
    P.emit("scalar", lambda s: s.activation(
        out=alog_sb[:], in_=alog_sb[:], func=AF.Exp),
        reads=[alog_sb], writes=[alog_sb])
    P.emit("vector", lambda v: v.tensor_scalar(
        out=alog_sb[:], in0=alog_sb[:], scalar1=-1.0, scalar2=None,
        op0=ALU.mult), reads=[alog_sb], writes=[alog_sb])
    P.emit("vector", lambda v: v.tensor_scalar(
        out=gtmp[:], in0=ba_sb[32:64, :], scalar1=dtb_sb[:, 0:1], scalar2=None,
        op0=ALU.add), reads=[ba_sb, dtb_sb], writes=[gtmp])
    P.emit("vector", lambda v: v.tensor_scalar(
        out=cg[:], in0=gtmp[:], scalar1=60.0, scalar2=None,
        op0=ALU.min), reads=[gtmp], writes=[cg])
    P.emit("scalar", lambda s: s.activation(
        out=cg[:], in_=cg[:], func=AF.Exp), reads=[cg], writes=[cg])
    P.emit("vector", lambda v: v.tensor_scalar(
        out=cg[:], in0=cg[:], scalar1=1.0, scalar2=None,
        op0=ALU.add), reads=[cg], writes=[cg])
    P.emit("scalar", lambda s: s.activation(
        out=cg[:], in_=cg[:], func=AF.Ln), reads=[cg], writes=[cg])
    P.emit("vector", lambda v: v.tensor_tensor(
        out=gtmp[:], in0=cg[:], in1=gtmp[:], op=ALU.max),
        reads=[cg, gtmp], writes=[gtmp])
    P.emit("vector", lambda v: v.tensor_scalar(
        out=gtmp[:], in0=gtmp[:], scalar1=alog_sb[:, 0:1], scalar2=None,
        op0=ALU.mult), reads=[gtmp, alog_sb], writes=[gtmp])
    for ci in range(16):
        P.emit("vector", lambda v, ci=ci: v.tensor_tensor_scan(
            out=cg[:, ts_(ci, 128)], data0=gtmp[:, ts_(ci, 128)],
            data1=gtmp[:, ts_(ci, 128)], initial=0.0, op0=ALU.add,
            op1=ALU.bypass), reads=[gtmp], writes=[(cg, ci)])
    P.emit("vector", lambda v: v.tensor_tensor(
        out=cgb[:], in0=bass.AP(cg, 0, [[T, 32], [1, T]]), in1=gtmp2[:],
        op=ALU.add), reads=[(cg, ci) for ci in range(16)] + [gtmp2],
        writes=[cgb])
    res["_scratch6"] = [wbuf[0], wbuf[1], xcbuf, cacc, ctile[0], ctile[1],
                       cw_sb, tstag[0], tstag[1], sqt, nrow, nrep, ones1,
                       onescol, ntile, ba_sb, dtb_sb, alog_sb, gtmp, gtmp2]
    return res



def emit_l1_delta(c, h, res):
    P = c.P
    pp = c.pp
    cg, cgb, beta_f = res["cg"], res["cgb"], res["beta_f"]
    NH, HD = 32, 128
    NCH = 16

    qn_c = [c.sbuf("l1_qn0", [128, 2048], BF16)] * 2
    kn_c = [c.sbuf("l1_kn0", [128, 2048], BF16)] * 2
    ktm_c = [c.sbuf("l1_ktm0", [128, 2048], BF16)] * 2
    v_c = [c.sbuf("l1_v0", [128, 4096], BF16)] * 2
    z_c = [c.sbuf("l1_z0", [128, 4096], BF16)] * 2
    S_sb = c.sbuf("l1_S", [128, 4096], F32)
    S_bf = c.sbuf("l1_Sb", [128, 4096], BF16)
    trb = c.sbuf("l1_trb", [128, 64], F32)
    stack = c.sbuf("l1_stk", [128, 128], F32)
    gamT = c.sbuf("l1_gT", [128, 32], F32)
    bgT = c.sbuf("l1_bgT", [128, 32], F32)
    dsT = c.sbuf("l1_dsT", [128, 32], F32)
    cdT = c.sbuf("l1_cdT", [128, 32], F32)
    larow = c.sbuf("l1_lar", [1, 32], F32)
    rpl = c.sbuf("l1_rpl", [128, 32], F32)
    ones1 = c.sbuf("l1_one", [1, 128], F32)
    ident = c.sbuf("l1_id", [128, 128], F32)
    identb = c.sbuf("l1_idb", [128, 128], BF16)
    mbS = c.sbuf("l1_mbS", [128, 128], F32)
    mbT = c.sbuf("l1_mbT", [128, 128], F32)
    mbQ = c.sbuf("l1_mbQ", [128, 128], F32)
    negcg = c.sbuf("l1_ncg", [32, 128], F32)
    stgA = c.sbuf("l1_sgA", [2, 512], F32)
    stgB = c.sbuf("l1_sgB", [2, 512], F32)
    stgC = c.sbuf("l1_sgC", [2, 512], F32)
    stgD = c.sbuf("l1_sgD", [2, 512], F32)
    stgE = c.sbuf("l1_sgE", [2, 512], F32)
    gneg = c.sbuf("l1_G", [128, 2048], BF16)
    gqk = c.sbuf("l1_Gqk", [128, 2048], BF16)
    etile = c.sbuf("l1_et", [128, 512], F32)
    nst = [c.sbuf(f"l1_nst{i}", [128, 512], BF16) for i in range(2)]
    nts = [c.sbuf(f"l1_nts{i}", [128, 512], BF16) for i in range(2)]
    aqk = c.sbuf("l1_aqk", [128, 512], BF16)
    rX = c.sbuf("l1_X", [128, 512], BF16)
    uU = c.sbuf("l1_U", [128, 512], BF16)
    o_sb = c.sbuf("l1_o", [128, 4096], BF16)
    usq = c.sbuf("l1_usq", [128, 4096], BF16)
    rtmp = c.sbuf("l1_rtmp", [128, 512], F32)
    orms = c.sbuf("l1_orms", [128, 32], F32)
    nw1rep = c.sbuf("l1_nw1", [128, 128], F32)
    nw1row = c.sbuf("l1_nw1r", [1, 128], F32)
    ostag = c.sbuf("l1_ost", [128, 2048], BF16)

    P.emit("gpsimd", lambda g: g.memset(ones1[:], 1.0), writes=[ones1])
    P.emit("gpsimd", lambda g: g.memset(ident[:], 1.0), writes=[ident])
    P.emit("gpsimd", lambda g: g.affine_select(
        out=ident[:], in_=ident[:], pattern=[[1, 128]],
        compare_op=ALU.is_equal, fill=0.0, base=0, channel_multiplier=-1),
        reads=[ident], writes=[ident])
    P.emit("vector", lambda v: v.tensor_copy(out=identb[:], in_=ident[:]),
           reads=[ident], writes=[identb])
    for mb, base, cm, cop in ((mbS, -1, -1, ALU.is_ge), (mbT, -1, 1, ALU.is_ge),
                              (mbQ, 0, -1, ALU.is_ge)):
        pat = [[1, 128]] if mb is not mbT else [[-1, 128]]
        P.emit("gpsimd", lambda g, mb=mb: g.memset(mb[:], 0.0), writes=[mb])
        P.emit("gpsimd", lambda g, mb=mb, pat=pat, base=base, cm=cm, cop=cop:
               g.affine_select(out=mb[:], in_=mb[:], pattern=pat,
                               compare_op=cop, fill=-1e30, base=base,
                               channel_multiplier=cm),
               reads=[mb], writes=[mb])
    P.emit("gpsimd", lambda g: g.memset(S_sb[:], 0.0), writes=[S_sb])
    P.emit("gpsimd", lambda g: g.memset(S_bf[:], 0.0), writes=[S_bf])
    for stg, fill in ((stgA, 1.0), (stgB, 1.0), (stgC, 1.0), (stgD, 1.0),
                      (stgE, 1.0)):
        P.emit("gpsimd", lambda g, stg=stg, fill=fill: g.memset(stg[:], fill),
               writes=[(stg, "ones"), (stg, "cg")])
    P.emit("sync", lambda s: s.dma_start(out=nw1row[:], in_=h["nw1"][:, :]),
           reads=[h["nw1"]], writes=[nw1row], dma=True)
    P.emit("tensor", lambda te: te.matmul(
        pp[0][:, 0:128], ones1[:], nw1row[:], start=True, stop=True),
        reads=[ones1, nw1row], writes=[pp[0]])
    P.emit("scalar", lambda s: s.copy(out=nw1rep[:], in_=pp[0][:, 0:128]),
           reads=[pp[0]], writes=[nw1rep])

    for ci in range(NCH):
        bb = ci % 2
        qn, kn, ktm, vv, zz = qn_c[bb], kn_c[bb], ktm_c[bb], v_c[bb], z_c[bb]
        P.emit("sync", lambda s, ci=ci, d=qn: s.dma_start(
            out=bass.AP(d, 0, [[2048, 128], [128, 16], [1, 128]]),
            in_=bass.AP(h["qn_fm_d"], ci * 128, [[T, 128], [128 * T, 16], [1, 128]])),
            reads=[(h["qn_fm_d"], j) for j in range(16)], writes=[qn], dma=True)
        P.emit("sync", lambda s, ci=ci, d=kn: s.dma_start(
            out=bass.AP(d, 0, [[2048, 128], [128, 16], [1, 128]]),
            in_=bass.AP(h["kn_fm_d"], ci * 128, [[T, 128], [128 * T, 16], [1, 128]])),
            reads=[(h["kn_fm_d"], j) for j in range(16)], writes=[kn], dma=True)
        P.emit("sync", lambda s, ci=ci, d=ktm: s.dma_start(
            out=d[:], in_=h["kn_tm_d"][ts_(ci, 128), :]),
            reads=[(h["kn_tm_d"], j) for j in range(16)], writes=[ktm], dma=True)
        P.emit("sync", lambda s, ci=ci, d=vv: s.dma_start(
            out=d[:], in_=h["v_tm_d"][ts_(ci, 128), :]),
            reads=[(h["v_tm_d"], j) for j in range(32)], writes=[vv], dma=True)
        P.emit("sync", lambda s, ci=ci, d=zz: s.dma_start(
            out=d[:], in_=h["z1_tm_d"][ts_(ci, 128), :]),
            reads=[(h["z1_tm_d"], j) for j in range(32)], writes=[zz], dma=True)
        P.emit("vector", lambda v, ci=ci: v.tensor_copy(
            out=stack[0:32, :], in_=cg[:, ts_(ci, 128)]),
            reads=[(cg, ci)], writes=[(stack, 0)])
        P.emit("vector", lambda v, ci=ci: v.tensor_copy(
            out=stack[32:64, :], in_=beta_f[:, ts_(ci, 128)]),
            reads=[beta_f], writes=[(stack, 1)])
        P.emit("tensor", lambda te: te.transpose(pp[1][:, 0:128], stack[:],
                                                 ident[:]),
               reads=[(stack, 0), (stack, 1), ident], writes=[pp[1]])
        P.emit("scalar", lambda s: s.copy(out=trb[:], in_=pp[1][:, 0:64]),
               reads=[pp[1]], writes=[trb])
        P.emit("scalar", lambda s: s.activation(
            out=gamT[:], in_=trb[:, 0:32], func=AF.Exp),
            reads=[trb], writes=[gamT])
        P.emit("vector", lambda v: v.tensor_tensor(
            out=bgT[:], in0=trb[:, 32:64], in1=gamT[:], op=ALU.mult),
            reads=[trb, gamT], writes=[bgT])
        P.emit("scalar", lambda s, ci=ci: s.dma_start(
            out=larow[:], in_=cg[:, ci * 128 + 127: ci * 128 + 128],
            allow_slow_non_contiguous=True),
            reads=[(cg, ci)], writes=[larow], dma=True)
        P.emit("tensor", lambda te: te.matmul(
            pp[1][:, 128:160], ones1[:], larow[:], start=True, stop=True),
            reads=[ones1, larow], writes=[pp[1]])
        P.emit("scalar", lambda s: s.copy(out=rpl[:], in_=pp[1][:, 128:160]),
               reads=[pp[1]], writes=[rpl])
        P.emit("vector", lambda v: v.tensor_tensor(
            out=dsT[:], in0=rpl[:], in1=trb[:, 0:32], op=ALU.subtract),
            reads=[rpl, trb], writes=[dsT])
        P.emit("scalar", lambda s: s.activation(
            out=dsT[:], in_=dsT[:], func=AF.Exp), reads=[dsT], writes=[dsT])
        P.emit("scalar", lambda s: s.activation(
            out=cdT[:], in_=rpl[:], func=AF.Exp), reads=[rpl], writes=[cdT])
        P.emit("vector", lambda v, ci=ci: v.tensor_scalar(
            out=negcg[:], in0=cg[:, ts_(ci, 128)], scalar1=-1.0, scalar2=None,
            op0=ALU.mult), reads=[(cg, ci)], writes=[negcg])
        for kh in range(16):
            gp_ = pp[6 + kh % 2]
            P.emit("tensor", lambda te, kh=kh, gp_=gp_, kn=kn: te.matmul(
                gp_[:, 0:128], kn[:, ts_(kh, 128)], kn[:, ts_(kh, 128)],
                start=True, stop=True), reads=[kn], writes=[gp_])
            P.emit("vector", lambda v, kh=kh, gp_=gp_: v.tensor_scalar(
                out=gneg[:, ts_(kh, 128)], in0=gp_[:, 0:128], scalar1=-1.0,
                scalar2=None, op0=ALU.mult), reads=[gp_], writes=[(gneg, kh)])
            P.emit("tensor", lambda te, kh=kh, gp_=gp_, kn=kn, qn=qn: te.matmul(
                gp_[:, 128:256], kn[:, ts_(kh, 128)], qn[:, ts_(kh, 128)],
                start=True, stop=True), reads=[kn, qn], writes=[gp_])
            P.emit("scalar", lambda s, kh=kh, gp_=gp_: s.copy(
                out=gqk[:, ts_(kh, 128)], in_=gp_[:, 128:256]),
                reads=[gp_], writes=[(gqk, kh)])
        for sp in range(8):
            h0 = sp * 4
            P.emit("scalar", lambda s, sp=sp: s.dma_start(
                out=stgA[0:1, :], in_=negcg[ts_(sp, 4), :],
                allow_slow_non_contiguous=True),
                reads=[negcg], writes=[(stgA, "cg")], dma=True)
            P.emit("scalar", lambda s, sp=sp, ci=ci: s.dma_start(
                out=stgB[1:2, :], in_=cgb[ts_(sp, 4), ts_(ci, 128)],
                allow_slow_non_contiguous=True),
                reads=[cgb], writes=[(stgB, "cg")], dma=True)
            P.emit("scalar", lambda s, sp=sp, ci=ci: s.dma_start(
                out=stgC[0:1, :], in_=cgb[ts_(sp, 4), ts_(ci, 128)],
                allow_slow_non_contiguous=True),
                reads=[cgb], writes=[(stgC, "cg")], dma=True)
            P.emit("scalar", lambda s, sp=sp: s.dma_start(
                out=stgD[1:2, :], in_=negcg[ts_(sp, 4), :],
                allow_slow_non_contiguous=True),
                reads=[negcg], writes=[(stgD, "cg")], dma=True)
            P.emit("scalar", lambda s, sp=sp, ci=ci: s.dma_start(
                out=stgE[1:2, :], in_=cg[ts_(sp, 4), ts_(ci, 128)],
                allow_slow_non_contiguous=True),
                reads=[(cg, ci)], writes=[(stgE, "cg")], dma=True)
            for g4 in range(1):
                hh0 = h0
                s4 = slice(0, 512)
                gb = bass.AP(gneg, (hh0 // 2) * 128,
                             [[2048, 128], [128, 2], [0, 2], [1, 128]])
                for lay, stg_l, stg_r, mb, dstt in (
                        (0, stgA, stgB, mbS, nst[0]),
                        (1, stgC, stgD, mbT, nts[0])):
                    pb = pp[0 + lay]
                    for h4 in range(4):
                        P.emit("tensor", lambda te, h4=h4, stg_l=stg_l,
                               stg_r=stg_r, pb=pb, s4=s4: te.matmul(
                            pb[:, h4 * 128:(h4 + 1) * 128],
                            stg_l[:, s4.start + h4 * 128:s4.start + (h4 + 1) * 128],
                            stg_r[:, s4.start + h4 * 128:s4.start + (h4 + 1) * 128],
                            start=True, stop=True),
                            reads=[(stg_l, "cg"), (stg_l, "ones"),
                                   (stg_r, "cg"), (stg_r, "ones")],
                            writes=[pb])
                    P.emit("vector", lambda v, pb=pb, mb=mb: v.tensor_tensor(
                        out=etile[:], in0=pb[:],
                        in1=bass.AP(mb, 0, [[128, 128], [0, 4], [1, 128]]),
                        op=ALU.add), reads=[pb, mb], writes=[etile])
                    P.emit("scalar", lambda s: s.activation(
                        out=etile[:], in_=etile[:], func=AF.Exp),
                        reads=[etile], writes=[etile])
                    P.emit("vector", lambda v, dstt=dstt, gb=gb: v.tensor_tensor(
                        out=dstt[:], in0=etile[:], in1=gb, op=ALU.mult),
                        reads=[etile, gneg], writes=[dstt])
                pb = pp[2]
                for h4 in range(4):
                    hh = hh0 + h4
                    P.emit("tensor", lambda te, hh=hh, h4=h4, pb=pb, kn=kn:
                           te.matmul(
                        pb[:, h4 * 128:(h4 + 1) * 128],
                        kn[:, ts_(hh // 2, 128)], S_bf[:, ts_(hh, 128)],
                        start=True, stop=True),
                        reads=[kn, S_bf], writes=[pb])
                P.emit("vector", lambda v, pb=pb, hh0=hh0: v.tensor_tensor(
                    out=etile[:], in0=pb[:],
                    in1=bass.AP(bgT, hh0, [[32, 128], [1, 4], [0, 128]]),
                    op=ALU.mult), reads=[pb, bgT], writes=[etile])
                P.emit("vector", lambda v, hh0=hh0, vv=vv: v.tensor_tensor(
                    out=rtmp[:], in0=vv[:, hh0 * 128:(hh0 + 4) * 128],
                    in1=bass.AP(trb, 32 + hh0, [[64, 128], [1, 4], [0, 128]]),
                    op=ALU.mult), reads=[vv, trb], writes=[rtmp])
                P.emit("vector", lambda v: v.tensor_tensor(
                    out=rX[:], in0=rtmp[:], in1=etile[:],
                    op=ALU.subtract), reads=[rtmp, etile], writes=[rX])
                cur = 0
                for lev in range(7):
                    pb = pp[3]
                    for h4 in range(4):
                        P.emit("tensor", lambda te, h4=h4, pb=pb, cur=cur:
                               te.matmul(
                            pb[:, h4 * 128:(h4 + 1) * 128],
                            nst[cur][:, h4 * 128:(h4 + 1) * 128],
                            rX[:, h4 * 128:(h4 + 1) * 128],
                            start=True, stop=False),
                            reads=[nst[cur], rX], writes=[pb])
                        P.emit("tensor", lambda te, h4=h4, pb=pb: te.matmul(
                            pb[:, h4 * 128:(h4 + 1) * 128], identb[:],
                            rX[:, h4 * 128:(h4 + 1) * 128],
                            start=False, stop=True),
                            reads=[identb, rX], writes=[pb])
                    P.emit("scalar", lambda s, pb=pb: s.copy(
                        out=rX[:], in_=pb[:]), reads=[pb], writes=[rX])
                    if lev < 6:
                        pb2 = pp[4]
                        pb3 = pp[5]
                        for h4 in range(4):
                            P.emit("tensor", lambda te, h4=h4, pb2=pb2, cur=cur:
                                   te.matmul(
                                pb2[:, h4 * 128:(h4 + 1) * 128],
                                nts[cur][:, h4 * 128:(h4 + 1) * 128],
                                nst[cur][:, h4 * 128:(h4 + 1) * 128],
                                start=True, stop=True),
                                reads=[nts[cur], nst[cur]], writes=[pb2])
                            P.emit("tensor", lambda te, h4=h4, pb3=pb3, cur=cur:
                                   te.matmul(
                                pb3[:, h4 * 128:(h4 + 1) * 128],
                                nst[cur][:, h4 * 128:(h4 + 1) * 128],
                                nts[cur][:, h4 * 128:(h4 + 1) * 128],
                                start=True, stop=True),
                                reads=[nts[cur], nst[cur]], writes=[pb3])
                        P.emit("scalar", lambda s, pb2=pb2, cur=cur: s.copy(
                            out=nst[1 - cur][:], in_=pb2[:]),
                            reads=[pb2], writes=[nst[1 - cur]])
                        P.emit("scalar", lambda s, pb3=pb3, cur=cur: s.copy(
                            out=nts[1 - cur][:], in_=pb3[:]),
                            reads=[pb3], writes=[nts[1 - cur]])
                        cur = 1 - cur
                pb = pp[0]
                for h4 in range(4):
                    P.emit("tensor", lambda te, h4=h4, pb=pb, s4=s4: te.matmul(
                        pb[:, h4 * 128:(h4 + 1) * 128],
                        stgA[:, s4.start + h4 * 128:s4.start + (h4 + 1) * 128],
                        stgE[:, s4.start + h4 * 128:s4.start + (h4 + 1) * 128],
                        start=True, stop=True),
                        reads=[(stgA, "cg"), (stgA, "ones"), (stgE, "cg"),
                               (stgE, "ones")], writes=[pb])
                P.emit("vector", lambda v, pb=pb: v.tensor_tensor(
                    out=etile[:], in0=pb[:],
                    in1=bass.AP(mbQ, 0, [[128, 128], [0, 4], [1, 128]]),
                    op=ALU.add), reads=[pb, mbQ], writes=[etile])
                P.emit("scalar", lambda s: s.activation(
                    out=etile[:], in_=etile[:], func=AF.Exp),
                    reads=[etile], writes=[etile])
                gbq = bass.AP(gqk, (hh0 // 2) * 128,
                              [[2048, 128], [128, 2], [0, 2], [1, 128]])
                P.emit("vector", lambda v, gbq=gbq: v.tensor_tensor(
                    out=aqk[:], in0=etile[:], in1=gbq, op=ALU.mult),
                    reads=[etile, gqk], writes=[aqk])
                pb = pp[2]
                for h4 in range(4):
                    hh = hh0 + h4
                    P.emit("tensor", lambda te, hh=hh, h4=h4, pb=pb, qn=qn:
                           te.matmul(
                        pb[:, h4 * 128:(h4 + 1) * 128],
                        qn[:, ts_(hh // 2, 128)], S_bf[:, ts_(hh, 128)],
                        start=True, stop=True),
                        reads=[qn, S_bf], writes=[pb])
                pb2 = pp[3]
                for h4 in range(4):
                    P.emit("tensor", lambda te, h4=h4, pb2=pb2: te.matmul(
                        pb2[:, h4 * 128:(h4 + 1) * 128],
                        aqk[:, h4 * 128:(h4 + 1) * 128],
                        rX[:, h4 * 128:(h4 + 1) * 128],
                        start=True, stop=True),
                        reads=[aqk, rX], writes=[pb2])
                P.emit("vector", lambda v, pb=pb, hh0=hh0: v.tensor_tensor(
                    out=etile[:], in0=pb[:],
                    in1=bass.AP(gamT, hh0, [[32, 128], [1, 4], [0, 128]]),
                    op=ALU.mult), reads=[pb, gamT], writes=[etile])
                P.emit("vector", lambda v, pb2=pb2, hh0=hh0: v.tensor_tensor(
                    out=o_sb[:, hh0 * 128:(hh0 + 4) * 128], in0=etile[:],
                    in1=pb2[:], op=ALU.add),
                    reads=[etile, pb2], writes=[(o_sb, hh0)])
                P.emit("vector", lambda v, hh0=hh0: v.tensor_tensor(
                    out=uU[:], in0=rX[:],
                    in1=bass.AP(dsT, hh0, [[32, 128], [1, 4], [0, 128]]),
                    op=ALU.mult), reads=[rX, dsT], writes=[uU])
                pb3 = pp[5]
                for h4 in range(4):
                    hh = hh0 + h4
                    P.emit("tensor", lambda te, hh=hh, h4=h4, pb3=pb3,
                           ktm=ktm: te.matmul(
                        pb3[:, h4 * 128:(h4 + 1) * 128],
                        ktm[:, ts_(hh // 2, 128)],
                        uU[:, h4 * 128:(h4 + 1) * 128],
                        start=True, stop=True),
                        reads=[ktm, uU], writes=[pb3])
                P.emit("vector", lambda v, hh0=hh0: v.tensor_tensor(
                    out=S_sb[:, hh0 * 128:(hh0 + 4) * 128],
                    in0=S_sb[:, hh0 * 128:(hh0 + 4) * 128],
                    in1=bass.AP(cdT, hh0, [[32, 128], [1, 4], [0, 128]]),
                    op=ALU.mult), reads=[S_sb, cdT], writes=[S_sb])
                P.emit("vector", lambda v, hh0=hh0, pb3=pb3: v.tensor_tensor(
                    out=S_sb[:, hh0 * 128:(hh0 + 4) * 128],
                    in0=S_sb[:, hh0 * 128:(hh0 + 4) * 128], in1=pb3[:],
                    op=ALU.add), reads=[S_sb, pb3], writes=[S_sb])
                P.emit("vector", lambda v, hh0=hh0: v.tensor_copy(
                    out=S_bf[:, hh0 * 128:(hh0 + 4) * 128],
                    in_=S_sb[:, hh0 * 128:(hh0 + 4) * 128]),
                    reads=[S_sb], writes=[S_bf])
        P.emit("scalar", lambda s, zz=zz: s.activation(
            out=usq[:], in_=zz[:], func=AF.Silu), reads=[zz], writes=[usq])
        P.emit("vector", lambda v: v.tensor_tensor(
            out=o_sb[:], in0=o_sb[:], in1=usq[:], op=ALU.mult),
            reads=[(o_sb, hh0_) for hh0_ in range(0, 32, 4)] + [usq],
            writes=[(o_sb, "f")])
        P.emit("scalar", lambda s: s.activation(
            out=usq[:], in_=o_sb[:], func=AF.Square),
            reads=[(o_sb, "f")], writes=[usq])
        P.emit("vector", lambda v: v.tensor_reduce(
            out=orms[:], in_=bass.AP(usq, 0, [[4096, 128], [128, 32], [1, 128]]),
            axis=X, op=ALU.add), reads=[usq], writes=[orms])
        P.emit("vector", lambda v: v.tensor_scalar(
            out=orms[:], in0=orms[:], scalar1=1.0 / 128, scalar2=1e-6,
            op0=ALU.mult, op1=ALU.add), reads=[orms], writes=[orms])
        P.emit("vector", lambda v: v.tensor_scalar(
            out=orms[:], in0=orms[:], scalar1=1e-30, scalar2=None,
            op0=ALU.max), reads=[orms], writes=[orms])
        P.emit("scalar", lambda s: s.activation(
            out=orms[:], in_=orms[:], func=AF.Ln), reads=[orms], writes=[orms])
        P.emit("scalar", lambda s: s.activation(
            out=orms[:], in_=orms[:], func=AF.Exp, scale=-0.5),
            reads=[orms], writes=[orms])
        P.emit("vector", lambda v: v.tensor_tensor(
            out=o_sb[:], in0=o_sb[:],
            in1=bass.AP(orms, 0, [[32, 128], [1, 32], [0, 128]]),
            op=ALU.mult), reads=[(o_sb, "f"), orms], writes=[(o_sb, "f")])
        P.emit("vector", lambda v: v.tensor_tensor(
            out=o_sb[:], in0=o_sb[:],
            in1=bass.AP(nw1rep, 0, [[128, 128], [0, 32], [1, 128]]),
            op=ALU.mult), reads=[(o_sb, "f"), nw1rep], writes=[(o_sb, "f")])
        for hf in range(2):
            P.emit("scalar", lambda s, hf=hf: s.dma_start(
                out=bass.AP(ostag, 0, [[2048, 128], [128, 16], [1, 128]]),
                in_=o_sb[:, hf * 2048:(hf + 1) * 2048], transpose=True),
                reads=[(o_sb, "f")], writes=[ostag], dma=True)
            P.emit("gpsimd", lambda g, ci=ci, hf=hf: g.dma_start(
                out=bass.AP(h["yq_fm_d"], hf * 2048 * T + ci * 128,
                            [[T, 128], [128 * T, 16], [1, 128]]),
                in_=bass.AP(ostag, 0, [[2048, 128], [128, 16], [1, 128]])),
                reads=[ostag], writes=[(h["yq_fm_d"], (ci, hf))], dma=True)
    res["_scratch7"] = [qn_c[0], kn_c[0], ktm_c[0],
                        v_c[0], z_c[0], S_sb, S_bf, trb, stack,
                        gamT, bgT, dsT, cdT, larow, rpl, ones1, ident, identb,
                        mbS, mbT, mbQ, negcg, stgA, stgB, stgC, stgD, stgE,
                        gneg, gqk, etile, nst[0], nst[1], nts[0], nts[1], aqk, rX,
                        uU, o_sb, usq, rtmp, orms, nw1rep, nw1row, ostag]
    return res



def emit_l1_wout(c, h, res):
    P = c.P
    pp = c.pp
    yq_sb = c.sbuf("w1_yq", [128, 32 * T], BF16)
    wdc = [c.sbuf(f"w1_w{i}", [128, 32, 256], BF16) for i in range(2)]
    x1b = [c.sbuf(f"w1_x{i}", [128, 256], F32) for i in range(2)]
    ob = [c.sbuf(f"w1_o{i}", [128, 256], F32) for i in range(2)]
    P.emit("sync", lambda s: s.dma_start(
        out=bass.AP(yq_sb, 0, [[32 * T, 128], [T, 32], [1, T]]),
        in_=bass.AP(h["yq_fm_d"], 0, [[T, 128], [128 * T, 32], [1, T]])),
        reads=[(h["yq_fm_d"], (ci, hf)) for ci in range(16) for hf in range(2)],
        writes=[yq_sb], dma=True)
    for dc in range(8):
        wb = wdc[dc % 2]
        P.emit("gpsimd", lambda g, dc=dc, wb=wb: g.dma_start(
            out=wb[:], in_=bass.AP(h["w_outT1"], dc * 256,
                                   [[D, 128], [128 * D, 32], [1, 256]])),
            reads=[h["w_outT1"]], writes=[wb], dma=True)
        for t in range(NT):
            bank = pp[t // 2]
            half = (t % 2) * 256
            for k in range(32):
                P.emit("tensor", lambda te, k=k, t=t, bank=bank, half=half,
                       wb=wb: te.matmul(
                    bank[:, half:half + 256],
                    yq_sb[:, k * T + t * 128: k * T + (t + 1) * 128],
                    wb[:, k, :], start=(k == 0), stop=(k == 31)),
                    reads=[yq_sb, wb], writes=[bank])
            b2 = t % 2
            P.emit("sync", lambda s, t=t, dc=dc, xb=x1b[b2]: s.dma_start(
                out=xb[:], in_=h["x1_d"][ts_(t, 128), dc * 256:(dc + 1) * 256]),
                reads=[h["x1_d"]], writes=[x1b[b2]], dma=True)
            P.emit("vector", lambda v, bank=bank, half=half, xb=x1b[b2],
                   o=ob[b2]: v.tensor_tensor(
                out=o[:], in0=bank[:, half:half + 256], in1=xb[:], op=ALU.add),
                reads=[bank, x1b[b2]], writes=[ob[b2]])
            P.emit("gpsimd", lambda g, t=t, dc=dc, o=ob[b2]: g.dma_start(
                out=h["x2p_d"][ts_(t, 128), dc * 256:(dc + 1) * 256], in_=o[:]),
                reads=[ob[b2]], writes=[(h["x2p_d"], (t, dc))], dma=True)
    res["_scratch8"] = [yq_sb, wdc[0], wdc[1], x1b[0], x1b[1], ob[0], ob[1]]
    return res



def emit_ffn1(c, h, res):
    P = c.P
    pp = c.pp
    wq_fm = c.sbuf("f1_wq", [128, 16 * 2048], BF16, side="right")
    res["wq_fm"] = wq_fm
    qs = c.sbuf("f1_qs", [128, 2048], F32)
    qp = c.sbuf("f1_qp", [128, 2048], F32)
    qe = c.sbuf("f1_qe", [128, 2048], F32)
    qm = c.sbuf("f1_qm", [128, 2048], F32)
    qf = c.sbuf("f1_qf", [128, 2048], F32)
    sgn = c.sbuf("f1_sg", [128, 2048], F32)
    qi = c.sbuf("f1_qi", [128, 2048], I32)
    qx = c.sbuf("f1_qx", [128, 2048], I32)
    qb = c.sbuf("f1_qb", [128, 2048], BF16)
    fw_sb = c.sbuf("f1_fw", [128, 2048], BF16)
    xf = [c.sbuf("f1_xf0", [128, 2048], F32)] * 2
    xqb = c.sbuf("f1_xqb", [128, 2048], BF16)
    xstag = c.sbuf("f1_xst", [128, 2048], BF16)
    yq16 = c.sbuf("f1_y16", [128, 2048], BF16)
    yf32 = c.sbuf("f1_yf", [128, 2048], F32)
    x2b = [c.sbuf("f1_x20", [128, 2048], F32)] * 2
    ob = [c.sbuf("f1_o0", [128, 2048], F32)] * 2
    bias_rep = c.sbuf("f1_bias", [128, 2048], BF16)
    wm = c.sbuf("f1_wm", [128, 16], F32)
    sc = c.sbuf("f1_sc", [1, 16], F32)
    sci = c.sbuf("f1_sci", [1, 16], I32)
    scb = c.sbuf("f1_scb", [1, 16], BF16)
    g8 = c.sbuf("f1_g8", [1, 8], F32)
    rep = c.sbuf("f1_rep", [128, 8], F32)
    ones_r = c.sbuf("f1_or", [1, 128], F32)
    ones_rb = c.sbuf("f1_orb", [1, 128], BF16)
    ident = c.sbuf("f1_id", [128, 128], F32)
    c.const(3.0 * LN2)
    c.const(-3.0 * LN2)

    P.emit("gpsimd", lambda g: g.memset(sc[:], 0.0),
           writes=[(sc, kk) for kk in
                   ("ws", "xs", "b", "p2f", "m2", 5, 6, 7, 8, 9, 10, 11)])
    P.emit("gpsimd", lambda g: g.memset(ones_r[:], 1.0), writes=[ones_r])
    P.emit("gpsimd", lambda g: g.memset(ones_rb[:], 1.0), writes=[ones_rb])
    P.emit("gpsimd", lambda g: g.memset(ident[:], 1.0), writes=[(ident, 0)])
    P.emit("gpsimd", lambda g: g.affine_select(
        out=ident[:], in_=ident[:], pattern=[[1, 128]],
        compare_op=ALU.is_equal, fill=0.0, base=0, channel_multiplier=-1),
        reads=[(ident, 0)], writes=[(ident, 1)])
    for k in range(16):
        P.emit("sync", lambda s, k=k: s.dma_start(
            out=fw_sb[:], in_=h["fwT"][ts_(k, 128), :]),
            reads=[h["fwT"]], writes=[fw_sb], dma=True)
        P.emit("vector", lambda v, k=k: v.tensor_reduce(
            out=wm[:, k:k + 1], in_=fw_sb[:], axis=X, op=ALU.max,
            apply_absolute_value=True), reads=[fw_sb], writes=[(wm, k)])
    P.emit("vector", lambda v: v.tensor_reduce(
        out=wm[:, 0:1], in_=wm[:, 0:16], axis=X, op=ALU.max),
        reads=[(wm, k) for k in range(16)], writes=[(wm, "m"), (wm, 0)])
    P.emit("tensor", lambda te: te.transpose(pp[0][0:1, 0:128], wm[:, 0:1],
                                             ident[:]),
           reads=[(wm, "m"), (ident, 1)], writes=[pp[0]])
    P.emit("scalar", lambda s: s.copy(out=sc[0:1, 8:9], in_=pp[0][0:1, 0:1]),
           reads=[pp[0]], writes=[(sc, 8)])
    P.emit("vector", lambda v: v.tensor_reduce(
        out=sc[0:1, 0:1], in_=pp[0][0:1, 0:128], axis=X, op=ALU.max),
        reads=[pp[0]], writes=[(sc, "ws")])
    P.emit("vector", lambda v: v.tensor_scalar(
        out=sc[0:1, 0:1], in0=sc[0:1, 0:1], scalar1=1e-6, scalar2=None,
        op0=ALU.max), reads=[(sc, "ws")], writes=[(sc, "ws")])
    P.emit("sync", lambda s: s.dma_start(out=g8[:], in_=h["gmax"][:, :]),
           reads=[h["gmax"]], writes=[g8], dma=True)
    P.emit("vector", lambda v: v.tensor_reduce(
        out=sc[0:1, 1:2], in_=g8[:], axis=X, op=ALU.max),
        reads=[g8], writes=[(sc, "xs")])
    P.emit("vector", lambda v: v.tensor_scalar(
        out=sc[0:1, 1:2], in0=sc[0:1, 1:2], scalar1=1e-6, scalar2=None,
        op0=ALU.max), reads=[(sc, "xs")], writes=[(sc, "xs")])
    wsx = c.sbuf("f1_wsx", [1, 1], F32)
    bxo = c.sbuf("f1_bxo", [1, 1], F32)
    tbx = c.sbuf("f1_tbx", [1, 1], BF16)
    tfx = c.sbuf("f1_tfx", [1, 1], F32)
    tix = c.sbuf("f1_tix", [1, 1], I32)
    P.emit("vector", lambda v: v.tensor_copy(out=wsx[:], in_=sc[0:1, 0:1]),
           reads=[(sc, "ws")], writes=[wsx])
    emit_exact_exp2(c, lambda: wsx[0:1, 0:1], [wsx], bxo, bxo, tbx, tfx, tix,
                    1, 1)
    P.emit("vector", lambda v: v.tensor_scalar(
        out=sc[0:1, 3:4], in0=bxo[0:1, 0:1], scalar1=-float(2 ** 23),
        scalar2=float(134 * 2 ** 23), op0=ALU.mult, op1=ALU.add),
        reads=[bxo], writes=[(sc, "p2f")])
    P.emit("vector", lambda v: v.tensor_copy(out=sci[0:1, 2:3],
                                             in_=sc[0:1, 3:4]),
           reads=[(sc, "p2f")], writes=[(sci, "p2")])
    P.emit("vector", lambda v: v.tensor_tensor(
        out=sc[0:1, 4:5], in0=sc[0:1, 0:1],
        in1=sci.bitcast(F32)[0:1, 2:3], op=ALU.mult),
        reads=[(sc, "ws"), (sci, "p2")], writes=[(sc, "m2")])
    for src_j, dst_j, sk in ((4, 5, "m2"), (1, 6, "xs")):
        P.emit("vector", lambda v, src_j=src_j, dst_j=dst_j: v.reciprocal(
            out=sc[0:1, dst_j:dst_j + 1], in_=sc[0:1, src_j:src_j + 1]),
            reads=[(sc, sk)], writes=[(sc, dst_j)])
        P.emit("vector", lambda v, src_j=src_j, dst_j=dst_j: v.tensor_tensor(
            out=sc[0:1, 7:8], in0=sc[0:1, src_j:src_j + 1],
            in1=sc[0:1, dst_j:dst_j + 1], op=ALU.mult),
            reads=[(sc, sk), (sc, dst_j)], writes=[(sc, 7)])
        P.emit("vector", lambda v: v.tensor_scalar(
            out=sc[0:1, 7:8], in0=sc[0:1, 7:8], scalar1=-1.0, scalar2=2.0,
            op0=ALU.mult, op1=ALU.add), reads=[(sc, 7)], writes=[(sc, 7)])
        P.emit("vector", lambda v, dst_j=dst_j: v.tensor_tensor(
            out=sc[0:1, dst_j:dst_j + 1], in0=sc[0:1, dst_j:dst_j + 1],
            in1=sc[0:1, 7:8], op=ALU.mult),
            reads=[(sc, dst_j), (sc, 7)], writes=[(sc, dst_j)])
    P.emit("vector", lambda v: v.tensor_tensor(
        out=sc[0:1, 9:10], in0=sc[0:1, 0:1], in1=sc[0:1, 1:2], op=ALU.mult),
        reads=[(sc, "ws"), (sc, "xs")], writes=[(sc, 9)])
    P.emit("vector", lambda v: v.tensor_scalar(
        out=sc[0:1, 10:11], in0=sc[0:1, 4:5], scalar1=2.0, scalar2=None,
        op0=ALU.mult), reads=[(sc, "m2")], writes=[(sc, 10)])
    P.emit("vector", lambda v: v.tensor_copy(
        out=sc[0:1, 11:12], in_=sci.bitcast(F32)[0:1, 2:3]),
        reads=[(sci, "p2")], writes=[(sc, 11)])
    P.emit("tensor", lambda te: te.matmul(
        pp[1][:, 0:16], ones_r[:], sc[0:1, 0:16], start=True, stop=True),
        reads=[ones_r, (sc, 11), (sc, 5), (sc, "m2"), (sc, 10), (sc, 9),
               (sc, 6), (sc, 7), (sc, 8), (sc, "ws"), (sc, "xs"),
               (sc, "p2f")], writes=[pp[1]])
    P.emit("scalar", lambda s: s.copy(out=rep[:, 0:8], in_=pp[1][:, 0:8]),
           reads=[pp[1]], writes=[(rep, 0)])
    P.emit("scalar", lambda s: s.copy(out=wm[:, 0:8], in_=pp[1][:, 8:16]),
           reads=[pp[1]],
           writes=[(wm, "rep2"), (wm, "m")] + [(wm, kk) for kk in range(8)])
    p2c = lambda: wm[:, 3:4]
    rm2c = lambda: rep[:, 5:6]
    m2c = lambda: rep[:, 4:5]
    m2x2c = lambda: wm[:, 2:3]
    xswsc = lambda: wm[:, 1:2]
    rxsc = lambda: rep[:, 6:7]
    P.emit("sync", lambda s: s.dma_start(out=qb[0:1, 0:2048], in_=h["fb"][:, :]),
           reads=[h["fb"]], writes=[(qb, "bias")], dma=True)
    for occ in range(4):
        P.emit("tensor", lambda te, occ=occ: te.matmul(
            pp[2][:, 0:512], ones_rb[:], qb[0:1, occ * 512:(occ + 1) * 512],
            start=True, stop=True),
            reads=[ones_rb, (qb, "bias")], writes=[pp[2]])
        P.emit("scalar", lambda s, occ=occ: s.copy(
            out=bias_rep[:, occ * 512:(occ + 1) * 512], in_=pp[2][:, 0:512]),
            reads=[pp[2]], writes=[(bias_rep, occ)])

    for k in range(16):
        P.emit("sync", lambda s, k=k: s.dma_start(
            out=fw_sb[:], in_=h["fwT"][ts_(k, 128), :]),
            reads=[h["fwT"]], writes=[fw_sb], dma=True)
        P.emit("vector", lambda v: v.tensor_scalar(
            out=qs[:], in0=fw_sb[:], scalar1=p2c(), scalar2=None,
            op0=ALU.mult), reads=[fw_sb, (wm, "rep2")], writes=[qs])
        P.emit("scalar", lambda s: s.activation(
            out=sgn[:], in_=qs[:], func=AF.Sign), reads=[qs], writes=[sgn])
        P.emit("scalar", lambda s: s.activation(
            out=qe[:], in_=qs[:], func=AF.Abs), reads=[qs], writes=[qe])
        P.emit("vector", lambda v: v.tensor_scalar(
            out=qm[:], in0=qe[:], scalar1=rm2c(), scalar2=None, op0=ALU.mult),
            reads=[qe, (rep, 0)], writes=[qm])
        P.emit("vector", lambda v: v.tensor_scalar(
            out=qm[:], in0=qm[:], scalar1=1e-38, scalar2=None, op0=ALU.max),
            reads=[qm], writes=[qm])
        P.emit("scalar", lambda s: s.activation(
            out=qm[:], in_=qm[:], func=AF.Ln), reads=[qm], writes=[qm])
        P.emit("vector", lambda v: v.tensor_scalar(
            out=qm[:], in0=qm[:], scalar1=1.0 / LN2, scalar2=1e-3,
            op0=ALU.mult, op1=ALU.add), reads=[qm], writes=[qm])
        P.emit("vector", lambda v: v.tensor_copy(out=qi[:], in_=qm[:]),
               reads=[qm], writes=[qi])
        P.emit("vector", lambda v: v.tensor_copy(out=qp[:], in_=qi[:]),
               reads=[qi], writes=[qp])
        P.emit("vector", lambda v: v.tensor_tensor(
            out=qf[:], in0=qp[:], in1=qm[:], op=ALU.is_gt),
            reads=[qp, qm], writes=[qf])
        P.emit("vector", lambda v: v.tensor_tensor(
            out=qp[:], in0=qp[:], in1=qf[:], op=ALU.subtract),
            reads=[qp, qf], writes=[qp])
        P.emit("vector", lambda v: v.tensor_scalar(
            out=qp[:], in0=qp[:], scalar1=-6.0, scalar2=None, op0=ALU.max),
            reads=[qp], writes=[qp])
        P.emit("vector", lambda v: v.tensor_scalar(
            out=qm[:], in0=qp[:], scalar1=-float(2 ** 23),
            scalar2=float(130 * 2 ** 23), op0=ALU.mult, op1=ALU.add),
            reads=[qp], writes=[qm])
        P.emit("vector", lambda v: v.tensor_copy(out=qx[:], in_=qm[:]),
               reads=[qm], writes=[qx])
        P.emit("vector", lambda v: v.tensor_tensor(
            out=qs[:], in0=qe[:], in1=qx.bitcast(F32)[:, 0:2048],
            op=ALU.mult), reads=[qe, qx], writes=[qs])
        P.emit("vector", lambda v: v.tensor_scalar(
            out=qe[:], in0=qs[:], scalar1=2.0, scalar2=None, op0=ALU.mult),
            reads=[qs], writes=[qe])
        P.emit("vector", lambda v: v.tensor_scalar(
            out=qm[:], in0=qs[:], scalar1=rm2c(), scalar2=0.5,
            op0=ALU.mult, op1=ALU.add), reads=[qs, (rep, 0)], writes=[qm])
        P.emit("vector", lambda v: v.tensor_copy(out=qi[:], in_=qm[:]),
               reads=[qm], writes=[qi])
        P.emit("vector", lambda v: v.tensor_copy(out=qs[:], in_=qi[:]),
               reads=[qi], writes=[qs])
        P.emit("vector", lambda v: v.tensor_tensor(
            out=qf[:], in0=qs[:], in1=qm[:], op=ALU.is_gt),
            reads=[qs, qm], writes=[qf])
        P.emit("vector", lambda v: v.tensor_tensor(
            out=qs[:], in0=qs[:], in1=qf[:], op=ALU.subtract),
            reads=[qs, qf], writes=[qs])
        P.emit("vector", lambda v: v.tensor_scalar(
            out=qf[:], in0=qs[:], scalar1=0.5, scalar2=None, op0=ALU.mult),
            reads=[qs], writes=[qf])
        P.emit("vector", lambda v: v.tensor_copy(out=qi[:], in_=qf[:]),
               reads=[qf], writes=[qi])
        P.emit("vector", lambda v: v.tensor_copy(out=qm[:], in_=qi[:]),
               reads=[qi], writes=[qm])
        P.emit("vector", lambda v: v.tensor_tensor(
            out=qm[:], in0=qm[:], in1=qf[:], op=ALU.subtract),
            reads=[qm, qf], writes=[qm])
        P.emit("scalar", lambda s: s.activation(
            out=qm[:], in_=qm[:], func=AF.Square, scale=2.0),
            reads=[qm], writes=[qm])
        P.emit("vector", lambda v: v.tensor_copy(out=qb[:], in_=qm[:]),
               reads=[qm], writes=[(qb, "odd"), (qb, "bias")])
        P.emit("vector", lambda v: v.tensor_scalar(
            out=qm[:], in0=qs[:], scalar1=2.0, scalar2=-1.0,
            op0=ALU.mult, op1=ALU.add), reads=[qs], writes=[qm])
        P.emit("vector", lambda v: v.tensor_scalar(
            out=qm[:], in0=qm[:], scalar1=m2c(), scalar2=None, op0=ALU.mult),
            reads=[qm, (rep, 0)], writes=[qm])
        P.emit("vector", lambda v: v.tensor_tensor(
            out=qm[:], in0=qe[:], in1=qm[:], op=ALU.subtract),
            reads=[qe, qm], writes=[qm])
        P.emit("vector", lambda v: v.tensor_scalar(
            out=qe[:], in0=qm[:], scalar1=m2x2c(), scalar2=None,
            op0=ALU.subtract), reads=[qm, (wm, "rep2")], writes=[qe])
        P.emit("vector", lambda v: v.tensor_scalar(
            out=qf[:], in0=qm[:], scalar1=0.0, scalar2=None, op0=ALU.is_lt),
            reads=[qm], writes=[qf])
        P.emit("vector", lambda v: v.tensor_tensor(
            out=qs[:], in0=qs[:], in1=qf[:], op=ALU.subtract),
            reads=[qs, qf], writes=[qs])
        P.emit("vector", lambda v: v.tensor_scalar(
            out=qf[:], in0=qe[:], scalar1=0.0, scalar2=None, op0=ALU.is_gt),
            reads=[qe], writes=[qf])
        P.emit("vector", lambda v: v.tensor_tensor(
            out=qs[:], in0=qs[:], in1=qf[:], op=ALU.add),
            reads=[qs, qf], writes=[qs])
        P.emit("vector", lambda v: v.tensor_scalar(
            out=qf[:], in0=qm[:], scalar1=0.0, scalar2=None, op0=ALU.is_equal),
            reads=[qm], writes=[qf])
        P.emit("vector", lambda v: v.tensor_tensor(
            out=qf[:], in0=qf[:], in1=qb[:], op=ALU.mult),
            reads=[qf, (qb, "odd")], writes=[qf])
        P.emit("vector", lambda v: v.tensor_tensor(
            out=qs[:], in0=qs[:], in1=qf[:], op=ALU.subtract),
            reads=[qs, qf], writes=[qs])
        P.emit("vector", lambda v: v.tensor_scalar(
            out=qf[:], in0=qe[:], scalar1=0.0, scalar2=None, op0=ALU.is_equal),
            reads=[qe], writes=[qf])
        P.emit("vector", lambda v: v.tensor_tensor(
            out=qf[:], in0=qf[:], in1=qb[:], op=ALU.mult),
            reads=[qf, (qb, "odd")], writes=[qf])
        P.emit("vector", lambda v: v.tensor_tensor(
            out=qs[:], in0=qs[:], in1=qf[:], op=ALU.add),
            reads=[qs, qf], writes=[qs])
        P.emit("vector", lambda v: v.tensor_scalar(
            out=qm[:], in0=qp[:], scalar1=float(2 ** 23),
            scalar2=float(124 * 2 ** 23), op0=ALU.mult, op1=ALU.add),
            reads=[qp], writes=[qm])
        P.emit("vector", lambda v: v.tensor_copy(out=qx[:], in_=qm[:]),
               reads=[qm], writes=[qx])
        P.emit("vector", lambda v: v.tensor_tensor(
            out=qs[:], in0=qs[:], in1=qx.bitcast(F32)[:, 0:2048],
            op=ALU.mult), reads=[qs, qx], writes=[qs])
        P.emit("vector", lambda v: v.tensor_tensor(
            out=qs[:], in0=qs[:], in1=sgn[:], op=ALU.mult),
            reads=[qs, sgn], writes=[qs])
        P.emit("vector", lambda v, k=k: v.tensor_copy(
            out=wq_fm[:, ts_(k, 2048)], in_=qs[:]),
            reads=[qs], writes=[(wq_fm, k)])

    for t in range(NT):
        b = t % 2
        xb = xf[b]
        P.emit("sync", lambda s, t=t, xb=xb: s.dma_start(
            out=xb[:], in_=h["xn1f_d"][ts_(t, 128), :]),
            reads=[h["xn1f_d"]], writes=[xb], dma=True)
        P.emit("vector", lambda v, xb=xb: v.tensor_scalar(
            out=qs[:], in0=xb[:], scalar1=rxsc(), scalar2=None, op0=ALU.mult),
            reads=[xb, (rep, 0)], writes=[qs])
        P.emit("scalar", lambda s: s.activation(
            out=qp[:], in_=qs[:], func=AF.Square), reads=[qs], writes=[qp])
        P.emit("vector", lambda v: v.tensor_scalar(
            out=qp[:], in0=qp[:], scalar1=1e-38, scalar2=None, op0=ALU.max),
            reads=[qp], writes=[qp])
        P.emit("scalar", lambda s: s.activation(
            out=qp[:], in_=qp[:], func=AF.Ln), reads=[qp], writes=[qp])
        P.emit("vector", lambda v: v.tensor_scalar(
            out=qp[:], in0=qp[:], scalar1=1.0 / (2 * LN2), scalar2=1e-3,
            op0=ALU.mult, op1=ALU.add), reads=[qp], writes=[qp])
        P.emit("vector", lambda v: v.tensor_copy(out=qi[:], in_=qp[:]),
               reads=[qp], writes=[qi])
        P.emit("vector", lambda v: v.tensor_copy(out=qe[:], in_=qi[:]),
               reads=[qi], writes=[qe])
        P.emit("vector", lambda v: v.tensor_tensor(
            out=qm[:], in0=qe[:], in1=qp[:], op=ALU.is_gt),
            reads=[qe, qp], writes=[qm])
        P.emit("vector", lambda v: v.tensor_tensor(
            out=qp[:], in0=qe[:], in1=qm[:], op=ALU.subtract),
            reads=[qe, qm], writes=[qp])
        P.emit("vector", lambda v: v.tensor_scalar(
            out=qp[:], in0=qp[:], scalar1=-6.0, scalar2=None, op0=ALU.max),
            reads=[qp], writes=[qp])
        P.emit("scalar", lambda s: s.activation(
            out=qe[:], in_=qp[:], func=AF.Exp, scale=-LN2,
            bias=c.const(3.0 * LN2)),
            reads=[qp, c.cread(3.0 * LN2)], writes=[qe])
        P.emit("vector", lambda v: v.tensor_tensor(
            out=qm[:], in0=qs[:], in1=qe[:], op=ALU.mult),
            reads=[qs, qe], writes=[qm])
        P.emit("vector", lambda v: v.tensor_scalar(
            out=qm[:], in0=qm[:], scalar1=0.5, scalar2=None, op0=ALU.add),
            reads=[qm], writes=[qm])
        P.emit("vector", lambda v: v.tensor_copy(out=qi[:], in_=qm[:]),
               reads=[qm], writes=[qi])
        P.emit("vector", lambda v: v.tensor_copy(out=qs[:], in_=qi[:]),
               reads=[qi], writes=[qs])
        P.emit("vector", lambda v: v.tensor_tensor(
            out=qf[:], in0=qs[:], in1=qm[:], op=ALU.is_gt),
            reads=[qs, qm], writes=[qf])
        P.emit("vector", lambda v: v.tensor_tensor(
            out=qs[:], in0=qs[:], in1=qf[:], op=ALU.subtract),
            reads=[qs, qf], writes=[qs])
        P.emit("scalar", lambda s: s.activation(
            out=qe[:], in_=qp[:], func=AF.Exp, scale=LN2,
            bias=c.const(-3.0 * LN2)),
            reads=[qp, c.cread(-3.0 * LN2)], writes=[qe])
        P.emit("vector", lambda v: v.tensor_tensor(
            out=qs[:], in0=qs[:], in1=qe[:], op=ALU.mult),
            reads=[qs, qe], writes=[qs])
        P.emit("vector", lambda v: v.tensor_copy(out=xqb[:], in_=qs[:]),
               reads=[qs], writes=[xqb])
        P.emit("scalar", lambda s: s.dma_start(
            out=bass.AP(xstag, 0, [[2048, 128], [128, 16], [1, 128]]),
            in_=xqb[:], transpose=True), reads=[xqb], writes=[xstag],
            dma=True)
        for oc in range(4):
            for k in range(16):
                P.emit("tensor", lambda te, oc=oc, k=k: te.matmul(
                    pp[oc][:, 0:512], xstag[:, ts_(k, 128)],
                    wq_fm[:, k * 2048 + oc * 512:k * 2048 + (oc + 1) * 512],
                    start=(k == 0), stop=(k == 15)),
                    reads=[xstag, (wq_fm, k)], writes=[pp[oc]])
        P.emit("sync", lambda s, t=t, x2=x2b[b]: s.dma_start(
            out=x2[:], in_=h["x2p_d"][ts_(t, 128), :]),
            reads=[h["x2p_d"]], writes=[x2b[b]], dma=True)
        for oc in range(4):
            P.emit("vector", lambda v, oc=oc: v.tensor_scalar(
                out=yf32[:, oc * 512:(oc + 1) * 512], in0=pp[oc][:, 0:512],
                scalar1=xswsc(), scalar2=None, op0=ALU.mult),
                reads=[pp[oc], (wm, "rep2")], writes=[(yf32, oc)])
        P.emit("vector", lambda v: v.tensor_copy(out=yq16[:], in_=yf32[:]),
               reads=[(yf32, oc_) for oc_ in range(4)], writes=[yq16])
        P.emit("vector", lambda v: v.tensor_tensor(
            out=yq16[:], in0=yq16[:], in1=bias_rep[:], op=ALU.add),
            reads=[yq16] + [(bias_rep, occ) for occ in range(4)],
            writes=[yq16])
        P.emit("vector", lambda v: v.tensor_copy(out=yf32[:], in_=yq16[:]),
               reads=[yq16],
               writes=[(yf32, "up")] + [(yf32, oc_) for oc_ in range(4)])
        P.emit("vector", lambda v, b=b, o=ob[b]: v.tensor_tensor(
            out=o[:], in0=yf32[:], in1=x2b[b][:], op=ALU.add),
            reads=[(yf32, "up"), x2b[b]], writes=[ob[b]])
        P.emit("gpsimd", lambda g, t=t, o=ob[b]: g.dma_start(
            out=h["out_d"][ts_(t, 128), :], in_=o[:]),
            reads=[ob[b]], writes=[(h["out_d"], t)], dma=True)
    return res


# version 03 (final)
# speedup vs baseline: 1.1748x; 1.1748x over previous
import math
import numpy as np
import ml_dtypes
import concourse.bass as bass
import concourse.mybir as mybir
from concourse.bass_utils import run_bass_kernel_spmd

F32 = mybir.dt.float32
BF16 = mybir.dt.bfloat16
I32 = mybir.dt.int32
AF = mybir.ActivationFunctionType
ALU = mybir.AluOpType
X = mybir.AxisListType.X

T = 2048
D = 2048
NT = T // 128
ND = D // 128
LN2 = float(np.log(2.0))

DMA_SLOTS = 6


def ts_(i, n):
    return slice(i * n, (i + 1) * n)


class _Op:
    __slots__ = ("engine", "seq", "fn", "is_dma", "deps", "reads", "writes",
                 "need_inc", "slot", "slot_use", "manual_waits")

    def __init__(self, engine, seq, fn, is_dma, reads, writes):
        self.engine = engine
        self.seq = seq
        self.fn = fn
        self.is_dma = is_dma
        self.deps = []
        self.reads = reads
        self.writes = writes
        self.need_inc = False
        self.slot = None
        self.slot_use = None
        self.manual_waits = []


class Prog:

    ENGINES = ("sync", "vector", "scalar", "tensor", "gpsimd")

    def __init__(self, nc):
        self.nc = nc
        self.ops = {e: [] for e in self.ENGINES}
        self.all_ops = []
        self.state = {}

    @staticmethod
    def _key(x):
        if isinstance(x, tuple):
            h, sub = x
            return (id(h), sub)
        return (id(x), None)

    def _st(self, key):
        return self.state.setdefault(key, {"w": None, "r": []})

    def emit(self, engine, fn, reads=(), writes=(), dma=False, after=()):
        op = _Op(engine, len(self.ops[engine]), fn, dma,
                 [self._key(r) for r in reads], [self._key(w) for w in writes])
        deps = []
        for k in op.reads:
            st = self._st(k)
            if st["w"] is not None:
                deps.append(st["w"])
        for k in op.writes:
            st = self._st(k)
            if st["w"] is not None:
                deps.append(st["w"])
            deps.extend(st["r"])
        deps.extend(after)
        seen = set()
        for d in deps:
            if d is op or id(d) in seen:
                continue
            seen.add(id(d))
            op.deps.append(d)
        for k in op.reads:
            st = self._st(k)
            if op not in st["r"]:
                st["r"].append(op)
        for k in op.writes:
            st = self._st(k)
            st["w"] = op
            st["r"] = []
        self.ops[engine].append(op)
        self.all_ops.append(op)
        return op

    def build(self, block):
        for e in self.ENGINES:
            n = 0
            for op in self.ops[e]:
                if op.is_dma:
                    op.slot = n % DMA_SLOTS
                    op.slot_use = n // DMA_SLOTS + 1
                    n += 1
        for op in self.all_ops:
            best = {}
            for d in op.deps:
                if (op.engine == "tensor" and d.engine == "tensor"
                        and not d.is_dma and not op.is_dma):
                    continue
                k = ("dma", d.engine, d.slot) if d.is_dma else ("c", d.engine)
                cur = best.get(k)
                if cur is None or d.seq > cur.seq:
                    best[k] = d
            op.deps = list(best.values())
            for d in op.deps:
                d.need_inc = True

        prog_sem, prog_count = {}, {}
        for e in self.ENGINES:
            cnt = 0
            for op in self.ops[e]:
                if not op.is_dma and op.need_inc:
                    cnt += 1
                    prog_count[id(op)] = cnt
            if cnt:
                prog_sem[e] = self.nc.semaphore(f"pg_{e}").__enter__()
        slot_sem = {}
        for e in self.ENGINES:
            for s in {op.slot for op in self.ops[e] if op.is_dma}:
                slot_sem[(e, s)] = self.nc.semaphore(f"dq_{e}{s}").__enter__()

        hwm = {e: {} for e in self.ENGINES}

        def dep_wait(d):
            if d.is_dma:
                return (slot_sem[(d.engine, d.slot)], 16 * d.slot_use)
            return (prog_sem[d.engine], prog_count[id(d)])

        streams = {e: [] for e in self.ENGINES}
        for e in self.ENGINES:
            for op in self.ops[e]:
                waits = []
                for d in op.deps:
                    sem, val = dep_wait(d)
                    if hwm[e].get(id(sem), 0) < val:
                        waits.append((sem, val))
                        hwm[e][id(sem)] = val
                for sem, val in op.manual_waits:
                    if hwm[e].get(id(sem), 0) < val:
                        waits.append((sem, val))
                        hwm[e][id(sem)] = val
                if op.is_dma:
                    sem = slot_sem[(op.engine, op.slot)]
                    prev = 16 * (op.slot_use - 1)
                    if prev > 0 and hwm[e].get(id(sem), 0) < prev:
                        waits.append((sem, prev))
                        hwm[e][id(sem)] = prev
                    inc = (sem, 16)
                elif op.need_inc:
                    assert op.fn is not None, "wait-only op cannot be a producer"
                    inc = (prog_sem[e], 1)
                else:
                    inc = None
                streams[e].append((waits, op.fn, inc))

        for e in self.ENGINES:
            finals = []
            for s_idx in sorted({op.slot for op in self.ops[e] if op.is_dma}):
                last = max(op.slot_use for op in self.ops[e]
                           if op.is_dma and op.slot == s_idx)
                sem = slot_sem[(e, s_idx)]
                if hwm[e].get(id(sem), 0) < 16 * last:
                    finals.append((sem, 16 * last))
            if finals:
                streams[e].append((finals, None, None))

        for e in self.ENGINES:
            stream = streams[e]
            if not stream:
                continue

            def engine_fn(eng, stream=stream):
                for waits, fn, inc in stream:
                    for sem, val in waits:
                        eng.wait_ge(sem, val)
                    if fn is None:
                        continue
                    instr = fn(eng)
                    if inc is not None:
                        assert instr is not None, "producer fn returned None"
                        instr.then_inc(inc[0], inc[1])
            getattr(block, e)(engine_fn)

    def fence(self):
        tails = []
        for e in self.ENGINES:
            real = [op for op in self.ops[e] if op.fn is not None]
            if real:
                tails.append(real[-1])
            dmas = [op for op in real if op.is_dma][-DMA_SLOTS:]
            tails.extend(dmas)
        for e in self.ENGINES:
            self.emit(e, None, after=list(tails))


class Core:

    def __init__(self, nc):
        self.nc = nc
        self.P = Prog(nc)
        self.consts = {}
        self.const_tile = self.sbuf("consts", [128, 16], F32)
        self.pp = [self.psum(f"pp{i}", [128, 512], F32) for i in range(8)]

    def sbuf(self, name, shape, dt, side=None):
        cm = self.nc.sbuf_tensor(name, shape, dt, side=side)
        h = cm.__enter__()
        if not hasattr(self, "_cms"):
            self._cms = {}
            self._order = {}
            self._seq = 0
        self._cms[id(h)] = cm
        self._order[id(h)] = self._seq
        self._seq += 1
        return h

    def free(self, *handles):
        for h in sorted(handles, key=lambda x: -self._order[id(x)]):
            cm = self._cms.pop(id(h))
            cm.__exit__(None, None, None)

    def psum(self, name, shape, dt=F32):
        return self.nc.psum_tensor(name, shape, dt).__enter__()

    def const(self, val):
        val = float(val)
        if val not in self.consts:
            idx = len(self.consts)
            assert idx < 16
            self.consts[val] = idx
            self.P.emit("gpsimd", lambda gp, i=idx, v=val: gp.memset(
                self.const_tile[:, i:i + 1], v),
                writes=[(self.const_tile, idx)])
        return self.const_tile[:, self.consts[val]:self.consts[val] + 1]

    def cread(self, val):
        return (self.const_tile, self.consts[float(val)])



def emit_ln_phase(c, x_src_fn, xn_fm, xn_tm=None, xnq_fm=None,
                  amax_out=None, tag="ln0", xf32_d=None):
    P = c.P
    quant = xnq_fm is not None
    xbuf = [c.sbuf(f"{tag}_x{i}", [128, D], F32) for i in range(2)]
    sqb = c.sbuf(tag + "_sq", [128, D], F32)
    stats = c.sbuf(tag + "_st", [128, 8], F32)
    xntile = [c.sbuf(f"{tag}_xnt{i}", [128, D], BF16) for i in range(2)]
    allocs = [xbuf[0], xbuf[1], sqb, stats, xntile[0], xntile[1]]
    if quant:
        qtile = [c.sbuf(f"{tag}_qt{i}", [128, D], BF16) for i in range(2)]
        am = c.sbuf(tag + "_am", [128, 64], F32)
        lnm = c.sbuf(tag + "_lnm", [128, 64], F32)
        sef = c.sbuf(tag + "_sef", [128, 64], F32)
        sei = c.sbuf(tag + "_sei", [128, 64], I32)
        amb = c.sbuf(tag + "_amb", [128, 64], BF16)
        invs = c.sbuf(tag + "_ivs", [128, 64], F32)
        sv = c.sbuf(tag + "_sv", [128, 64], F32)
        qs = c.sbuf(tag + "_qs", [128, D], F32)
        qi = c.sbuf(tag + "_qi", [128, D], I32)
        qp = c.sbuf(tag + "_qp", [128, D], F32)
        qe = c.sbuf(tag + "_qe", [128, D], F32)
        qm = sqb
        bc = lambda tile: bass.AP(tile, 0, [[64, 128], [1, 64], [0, 32]])
        allocs += [qtile[0], qtile[1], am, lnm, sef, sei, amb, invs, sv,
                   qs, qi, qp, qe]
        for v in (3.0 * LN2, -3.0 * LN2):
            c.const(v)

    done = []
    for t in range(NT):
        b = t % 2
        xb, xnt = xbuf[b], xntile[b]
        src_ap, src_h = x_src_fn(t)
        P.emit("sync", lambda s, a=src_ap, xb=xb: s.dma_start(out=xb[:], in_=a),
               reads=[src_h], writes=[xb], dma=True)
        P.emit("scalar", lambda s, xb=xb: s.activation(
            out=sqb[:], in_=xb[:], func=AF.Square), reads=[xb], writes=[sqb])
        P.emit("vector", lambda v, xb=xb: v.reduce_sum(
            out=stats[:, 0:1], in_=xb[:], axis=X),
            reads=[xb], writes=[(stats, 0)])
        P.emit("vector", lambda v: v.reduce_sum(
            out=stats[:, 1:2], in_=sqb[:], axis=X),
            reads=[sqb], writes=[(stats, 1)])
        P.emit("vector", lambda v: v.tensor_scalar(
            out=stats[:, 2:3], in0=stats[:, 0:1], scalar1=1.0 / D, scalar2=None,
            op0=ALU.mult), reads=[(stats, 0)], writes=[(stats, 2)])
        P.emit("vector", lambda v: v.tensor_scalar(
            out=stats[:, 3:4], in0=stats[:, 1:2], scalar1=1.0 / D, scalar2=None,
            op0=ALU.mult), reads=[(stats, 1)], writes=[(stats, 3)])
        P.emit("vector", lambda v: v.scalar_tensor_tensor(
            out=stats[:, 4:5], in0=stats[:, 2:3], scalar=stats[:, 2:3],
            in1=stats[:, 3:4], op0=ALU.mult, op1=ALU.subtract),
            reads=[(stats, 2), (stats, 3)], writes=[(stats, 4)])
        P.emit("vector", lambda v: v.tensor_scalar(
            out=stats[:, 4:5], in0=stats[:, 4:5], scalar1=-1.0, scalar2=1e-5,
            op0=ALU.mult, op1=ALU.add), reads=[(stats, 4)], writes=[(stats, 4)])
        P.emit("vector", lambda v: v.reciprocal(
            out=stats[:, 5:6], in_=stats[:, 4:5]),
            reads=[(stats, 4)], writes=[(stats, 5)])
        P.emit("scalar", lambda s: s.activation(
            out=stats[:, 6:7], in_=stats[:, 5:6], func=AF.Ln),
            reads=[(stats, 5)], writes=[(stats, 6)])
        P.emit("scalar", lambda s: s.activation(
            out=stats[:, 7:8], in_=stats[:, 6:7], func=AF.Exp, scale=0.5),
            reads=[(stats, 6)], writes=[(stats, 7)])
        P.emit("vector", lambda v, xb=xb: v.tensor_scalar(
            out=xb[:], in0=xb[:], scalar1=stats[:, 2:3],
            scalar2=stats[:, 7:8], op0=ALU.subtract, op1=ALU.mult),
            reads=[xb, (stats, 2), (stats, 7)], writes=[xb])
        if xf32_d is not None:
            done.append(P.emit("gpsimd", lambda g, xb=xb, t=t: g.dma_start(
                out=xf32_d[ts_(t, 128), :], in_=xb[:]),
                reads=[xb], writes=[(xf32_d, t)], dma=True))
        if xn_fm is not None or xn_tm is not None:
            P.emit("vector", lambda v, xb=xb, xnt=xnt: v.tensor_copy(
                out=xnt[:], in_=xb[:]), reads=[xb], writes=[xnt])
        if xn_tm is not None:
            P.emit("vector", lambda v, xnt=xnt, t=t: v.tensor_copy(
                out=xn_tm[:, ts_(t, D)], in_=xnt[:]),
                reads=[xnt], writes=[(xn_tm, t)])
        if amax_out is not None:
            P.emit("vector", lambda v, xb=xb, t=t: v.tensor_reduce(
                out=amax_out[:, t:t + 1], in_=xb[:],
                axis=X, op=ALU.max, apply_absolute_value=True),
                reads=[xb], writes=[(amax_out, t)])
        if xn_fm is not None:
            tr1 = P.emit("scalar", lambda s, xnt=xnt, t=t: s.dma_start(
                out=bass.AP(xn_fm, t * 128, [[ND * T, 128], [T, ND], [1, 128]]),
                in_=xnt[:], transpose=True),
                reads=[xnt], writes=[(xn_fm, t)], dma=True)
            done.append(tr1)

        if quant:
            P.emit("vector", lambda v, xb=xb: v.tensor_reduce(
                out=am[:], in_=bass.AP(xb, 0, [[D, 128], [32, 64], [1, 32]]),
                axis=X, op=ALU.max, apply_absolute_value=True),
                reads=[xb], writes=[am])
            P.emit("vector", lambda v: v.tensor_scalar(
                out=am[:], in0=am[:], scalar1=2.0 ** -30, scalar2=None,
                op0=ALU.max), reads=[am], writes=[am])
            emit_exact_exp2(c, lambda: am[:], [am], sef, sef, amb, lnm, sei,
                            128, 64)
            P.emit("vector", lambda v: v.tensor_scalar(
                out=invs[:], in0=sef[:], scalar1=-float(2 ** 23),
                scalar2=float(129 * 2 ** 23), op0=ALU.mult, op1=ALU.add),
                reads=[sef], writes=[invs])
            P.emit("vector", lambda v: v.tensor_copy(out=sei[:], in_=invs[:]),
                   reads=[invs], writes=[sei])
            P.emit("vector", lambda v, xb=xb: v.tensor_tensor(
                out=qs[:], in0=xb[:], in1=bc(sei.bitcast(F32)), op=ALU.mult),
                reads=[xb, sei], writes=[qs])
            P.emit("scalar", lambda s: s.activation(
                out=qp[:], in_=qs[:], func=AF.Square), reads=[qs], writes=[qp])
            P.emit("scalar", lambda s: s.activation(
                out=qp[:], in_=qp[:], func=AF.Ln), reads=[qp], writes=[qp])
            P.emit("vector", lambda v: v.tensor_scalar(
                out=qp[:], in0=qp[:], scalar1=1.0 / (2 * LN2), scalar2=1e-3,
                op0=ALU.mult, op1=ALU.add), reads=[qp], writes=[qp])
            P.emit("vector", lambda v: v.tensor_scalar(
                out=qp[:], in0=qp[:], scalar1=0.0, scalar2=None, op0=ALU.max),
                reads=[qp], writes=[qp])
            P.emit("vector", lambda v: v.tensor_copy(out=qi[:], in_=qp[:]),
                   reads=[qp], writes=[qi])
            P.emit("vector", lambda v: v.tensor_copy(out=qe[:], in_=qi[:]),
                   reads=[qi], writes=[qe])
            P.emit("vector", lambda v: v.tensor_tensor(
                out=qm[:], in0=qe[:], in1=qp[:], op=ALU.is_gt),
                reads=[qe, qp], writes=[qm])
            P.emit("vector", lambda v: v.tensor_tensor(
                out=qp[:], in0=qe[:], in1=qm[:], op=ALU.subtract),
                reads=[qe, qm], writes=[qp])
            P.emit("scalar", lambda s: s.activation(
                out=qe[:], in_=qp[:], func=AF.Exp, scale=-LN2,
                bias=c.const(3.0 * LN2)),
                reads=[qp, c.cread(3.0 * LN2)], writes=[qe])
            P.emit("vector", lambda v: v.tensor_tensor(
                out=qs[:], in0=qs[:], in1=qe[:], op=ALU.mult),
                reads=[qs, qe], writes=[qs])
            P.emit("vector", lambda v: v.tensor_scalar(
                out=qs[:], in0=qs[:], scalar1=0.5, scalar2=None, op0=ALU.add),
                reads=[qs], writes=[qs])
            P.emit("vector", lambda v: v.tensor_copy(out=qi[:], in_=qs[:]),
                   reads=[qs], writes=[qi])
            P.emit("vector", lambda v: v.tensor_copy(out=qe[:], in_=qi[:]),
                   reads=[qi], writes=[qe])
            P.emit("vector", lambda v: v.tensor_tensor(
                out=qm[:], in0=qe[:], in1=qs[:], op=ALU.is_gt),
                reads=[qe, qs], writes=[qm])
            P.emit("vector", lambda v: v.tensor_tensor(
                out=qs[:], in0=qe[:], in1=qm[:], op=ALU.subtract),
                reads=[qe, qm], writes=[qs])
            P.emit("scalar", lambda s: s.activation(
                out=qe[:], in_=qp[:], func=AF.Exp, scale=LN2,
                bias=c.const(-3.0 * LN2)),
                reads=[qp, c.cread(-3.0 * LN2)], writes=[qe])
            P.emit("vector", lambda v: v.tensor_tensor(
                out=qs[:], in0=qs[:], in1=qe[:], op=ALU.mult),
                reads=[qs, qe], writes=[qs])
            P.emit("vector", lambda v: v.tensor_scalar(
                out=qs[:], in0=qs[:], scalar1=7.5, scalar2=-7.5,
                op0=ALU.min, op1=ALU.max), reads=[qs], writes=[qs])
            P.emit("vector", lambda v: v.tensor_scalar(
                out=sv[:], in0=sef[:], scalar1=float(2 ** 23),
                scalar2=float(125 * 2 ** 23), op0=ALU.mult, op1=ALU.add),
                reads=[sef], writes=[sv])
            P.emit("vector", lambda v: v.tensor_copy(out=sei[:], in_=sv[:]),
                   reads=[sv], writes=[sei])
            P.emit("vector", lambda v, qt=qtile[b]: v.tensor_tensor(
                out=qt[:], in0=qs[:], in1=bc(sei.bitcast(F32)), op=ALU.mult),
                reads=[qs, sei], writes=[qtile[b]])
            tr2 = P.emit("scalar", lambda s, qt=qtile[b], t=t: s.dma_start(
                out=bass.AP(xnq_fm, t * 128, [[ND * T, 128], [T, ND], [1, 128]]),
                in_=qt[:], transpose=True),
                reads=[qtile[b]], writes=[(xnq_fm, t)], dma=True)
            done.append(tr2)
    return done, allocs


def _build_launch_a():
    nc = bass.Bass("TRN2", target_bir_lowering=False)
    h = {}
    EI = lambda n, shp, dt=F32: nc.dram_tensor(n, shp, dt, kind="ExternalInput")
    DI = lambda n, shp, dt: nc.dram_tensor(n, shp, dt)
    EO = lambda n, shp, dt=F32: nc.dram_tensor(n, shp, dt,
                                               kind="ExternalOutput")
    h["x0"] = EI("x0", [T, D])
    h["w_in"] = EI("w_in", [D, DIN0])
    h["cw0"] = EI("cw0", [CONVD0, 4])
    h["cb0"] = EI("cb0", [CONVD0, 1])
    h["dtb0"] = EI("dtb0", [64, 1])
    h["A0"] = EI("A0", [64, 1])
    h["D0r"] = EI("D0r", [1, 64])
    h["nw0"] = EI("nw0", [INTER0, 1])
    h["w_out0"] = EI("w_out0", [INTER0, D])
    h["wg"] = EI("wg", [8192, D])
    h["wu"] = EI("wu", [8192, D])
    h["wd"] = EI("wd", [D, 8192])
    h["w_qkvT"] = EI("w_qkvT", [D, CONV1])
    h["w_zT"] = EI("w_zT", [D, VDIM1])
    h["w_baT"] = EI("w_baT", [D, 64])
    h["cw1"] = EI("cw1", [CONV1, 4])
    h["dtb1"] = EI("dtb1", [32, 1])
    h["alog1"] = EI("alog1", [32, 1])
    h["nw1"] = EI("nw1", [1, 128])
    h["w_outT1"] = EI("w_outT1", [VDIM1, D])
    h["gate_tm_d"] = DI("gate_tm", [T, INTER0], BF16)
    h["xc_tm_d"] = DI("xc_tm", [T, INTER0], BF16)
    h["y_tm_d"] = DI("y_tm", [T, INTER0], BF16)
    h["wdq_fm_d"] = DI("wdq_fm", [8192, D], BF16)
    h["x1a_d"] = DI("x1a", [T, D], F32)
    h["x1b_d"] = DI("x1b", [T, D], F32)
    h["xf_d"] = DI("xf", [T, D], F32)
    h["x1_d"] = DI("x1", [T, D], F32)
    h["qn_fm_d"] = DI("qn_fm", [2048, T], BF16)
    h["kn_fm_d"] = DI("kn_fm", [2048, T], BF16)
    h["kn_tm_d"] = DI("kn_tm", [T, 2048], BF16)
    h["v_tm_d"] = DI("v_tm", [T, VDIM1], BF16)
    h["z1_tm_d"] = DI("z1_tm", [T, VDIM1], BF16)
    h["yq_fm_d"] = DI("yq_fm", [VDIM1, T], BF16)
    h["x2p_d"] = EO("x2p", [T, D])
    h["xn1f_d"] = EO("xn1f", [T, D])
    h["lmax_d"] = EO("lmax", [1, 1])
    import contextlib
    with contextlib.ExitStack() as ctx:
        c = Core(nc)
        xnq_fm = c.sbuf("xnq_fm", [128, ND * T], BF16, side="right")
        block = ctx.enter_context(nc.Block())
        P = c.P
        res = {}
        x0src = lambda t: (h["x0"][ts_(t, 128), :], h["x0"])
        _, lns = emit_ln_phase(c, x0src, None, xnq_fm=xnq_fm, tag="lnq")
        P.fence()
        c.free(*lns)
        h_f = dict(h)
        h_f["x1_d"] = h["xf_d"]
        emit_l0_ffn(c, xnq_fm, h_f, res)
        P.fence()
        c.free(*res.pop("_scratch5"))
        c.free(res.pop("interq_q"))
        c.free(xnq_fm)
        xn_fm = c.sbuf("xn_fm", [128, ND * T], BF16)
        _, lns = emit_ln_phase(c, x0src, xn_fm, tag="lnf")
        P.fence()
        c.free(*lns)
        emit_l0_proj(c, xn_fm, h, res)
        P.fence()
        c.free(xn_fm, *res.pop("_scratch"))
        emit_l0_ssd(c, h, res)
        P.fence()
        c.free(*res.pop("_scratch2"))
        c.free(res.pop("C_fm"), res.pop("B_fm"), res.pop("dt_f"),
               res.pop("A_sb"))
        emit_l0_zout(c, h, res)
        P.fence()
        c.free(*res.pop("_scratch3"))
        h_w = dict(h)
        h_w["x1p_d"] = h["x1_d"]
        emit_l0_wout(c, h_w, res)
        P.fence()
        c.free(*res.pop("_scratch4"))
        c.free(res.pop("z_fm"))
        xn_fm = c.sbuf("xn_fm2", [128, ND * T], BF16)
        amax = c.sbuf("ln1_amax", [128, NT], F32)
        x1src = lambda t: (h["x1_d"][ts_(t, 128), :], h["x1_d"])
        _, lns = emit_ln_phase(c, x1src, xn_fm, amax_out=amax,
                               xf32_d=h["xn1f_d"], tag="ln1")
        id2 = c.sbuf("lm_id", [128, 128], F32)
        lmr = c.sbuf("lm_r", [128, 1], F32)
        lm1 = c.sbuf("lm_1", [1, 1], F32)
        P.emit("gpsimd", lambda g: g.memset(id2[:], 1.0), writes=[(id2, 0)])
        P.emit("gpsimd", lambda g: g.affine_select(
            out=id2[:], in_=id2[:], pattern=[[1, 128]],
            compare_op=ALU.is_equal, fill=0.0, base=0, channel_multiplier=-1),
            reads=[(id2, 0)], writes=[(id2, 1)])
        P.emit("vector", lambda v: v.tensor_reduce(
            out=lmr[:], in_=amax[:], axis=X, op=ALU.max),
            reads=[(amax, t) for t in range(NT)], writes=[lmr])
        P.emit("tensor", lambda te: te.transpose(
            c.pp[0][0:1, 0:128], lmr[:], id2[:]),
            reads=[lmr, (id2, 1)], writes=[c.pp[0]])
        P.emit("vector", lambda v: v.tensor_reduce(
            out=lm1[:], in_=c.pp[0][0:1, 0:128], axis=X, op=ALU.max),
            reads=[c.pp[0]], writes=[lm1])
        P.emit("gpsimd", lambda g: g.dma_start(out=h["lmax_d"][:, :],
                                               in_=lm1[:]),
               reads=[lm1], writes=[h["lmax_d"]], dma=True)
        P.fence()
        c.free(lm1, lmr, id2, *lns)
        c.free(amax)
        emit_l1_proj(c, xn_fm, h, res)
        P.fence()
        c.free(xn_fm, *res.pop("_scratch6"))
        emit_l1_delta(c, h, res)
        P.fence()
        c.free(*res.pop("_scratch7"))
        c.free(res.pop("beta_f"), res.pop("cgb"), res.pop("cg"))
        emit_l1_wout(c, h, res)
        P.build(block)
    return nc


def _build_launch_b():
    nc = bass.Bass("TRN2", target_bir_lowering=False)
    h = {}
    h["fwT"] = nc.dram_tensor("fwT", [D, D], BF16, kind="ExternalInput")
    h["fb"] = nc.dram_tensor("fb", [1, D], BF16, kind="ExternalInput")
    h["gmax"] = nc.dram_tensor("gmax", [1, 8], F32, kind="ExternalInput")
    h["xn1f_d"] = nc.dram_tensor("xn1f", [T, D], F32, kind="ExternalInput")
    h["x2p_d"] = nc.dram_tensor("x2p", [T, D], F32, kind="ExternalInput")
    h["out_d"] = nc.dram_tensor("out", [T, D], F32, kind="ExternalOutput")
    import contextlib
    with contextlib.ExitStack() as ctx:
        c = Core(nc)
        block = ctx.enter_context(nc.Block())
        emit_ffn1(c, h, {})
        c.P.build(block)
    return nc


def kernel(**inputs):
    import numpy as np
    g = lambda k: np.asarray(inputs[k])
    ca = np.ascontiguousarray
    x = g("L0_attn_hidden_states")
    base = {
        "w_in": ca(g("L0_attn_W_in")),
        "cw0": ca(g("L0_attn_conv_w")),
        "cb0": ca(g("L0_attn_conv_b").reshape(CONVD0, 1)),
        "dtb0": ca(g("L0_attn_dt_bias").reshape(64, 1)),
        "A0": ca(g("L0_attn_A").reshape(64, 1)),
        "D0r": ca(g("L0_attn_D").reshape(1, 64)),
        "nw0": ca(g("L0_attn_norm_weight").reshape(INTER0, 1)),
        "w_out0": ca(g("L0_attn_W_out")),
        "wg": ca(g("L0_ffn_w_gate")),
        "wu": ca(g("L0_ffn_w_up")),
        "wd": ca(g("L0_ffn_w_down")),
        "w_qkvT": ca(g("L1_attn_W_qkv").T),
        "w_zT": ca(g("L1_attn_W_z").T),
        "w_baT": ca(np.concatenate([g("L1_attn_W_b"),
                                    g("L1_attn_W_a")], 0).T),
        "cw1": ca(g("L1_attn_conv_w").reshape(CONV1, 4)),
        "dtb1": ca(g("L1_attn_dt_bias").reshape(32, 1)),
        "alog1": ca(g("L1_attn_A_log").reshape(32, 1)),
        "nw1": ca(g("L1_attn_norm_w").reshape(1, 128)),
        "w_outT1": ca(g("L1_attn_W_out").T),
    }
    in_a = [dict(base, x0=ca(x[i])) for i in range(8)]
    nc_a = _build_launch_a()
    ra = run_bass_kernel_spmd(nc_a, in_a, core_ids=list(range(8)))
    gmax = np.concatenate(
        [np.asarray(ra.results[i]["lmax"]).reshape(1, 1) for i in range(8)],
        axis=1).astype(np.float32)
    fwT = ca(g("L1_ffn_weight").T)
    fb = ca(g("L1_ffn_bias").reshape(1, D))
    in_b = [{"fwT": fwT, "fb": fb, "gmax": gmax,
             "xn1f": np.asarray(ra.results[i]["xn1f"]),
             "x2p": np.asarray(ra.results[i]["x2p"])} for i in range(8)]
    nc_b = _build_launch_b()
    rb = run_bass_kernel_spmd(nc_b, in_b, core_ids=list(range(8)))
    return np.stack([np.asarray(rb.results[i]["out"]) for i in range(8)], 0)



INTER0 = 4096
CONVD0 = 4352
DIN0 = 8512


def emit_l0_proj(c, xn_fm, h, res):
    P = c.P
    wbuf = [c.sbuf(f"l0w{i}", [128, 16, 128], BF16) for i in range(2)]
    ps = [c.pp[0], c.pp[1]]
    xcbuf = c.sbuf("l0xc", [128, 2051], F32)
    cacc = c.sbuf("l0cacc", [128, T], F32)
    ctile = [c.sbuf(f"l0ct{i}", [128, T], BF16) for i in range(2)]
    gtile = [c.sbuf(f"l0gt{i}", [128, 512], BF16) for i in range(2)]
    gstag = [c.sbuf(f"l0gs{i}", [128, 4, 128], BF16) for i in range(2)]
    tstag = [c.sbuf(f"l0ts{i}", [128, 16, 128], BF16) for i in range(2)]
    cw_sb = c.sbuf("l0cw", [128, 34, 4], F32)
    cb_sb = c.sbuf("l0cb", [128, 34], F32)
    dtb_sb = c.sbuf("l0dtb", [64, 1], F32)
    A_sb = c.sbuf("l0A", [64, 1], F32, side="right")
    dtraw = c.sbuf("l0dtraw", [64, T], F32)
    dt_f = c.sbuf("l0dt", [64, T], F32, side="right")
    dttmp = c.sbuf("l0dttmp", [64, T], F32)
    B_fm = c.sbuf("l0B", [128, T], BF16, side="right")
    C_fm = c.sbuf("l0C", [128, T], BF16, side="right")
    res.update(B_fm=B_fm, C_fm=C_fm, dt_f=dt_f, A_sb=A_sb)

    P.emit("sync", lambda s: s.dma_start(
        out=cw_sb[:], in_=bass.AP(h["cw0"], 0, [[4, 128], [128 * 4, 34], [1, 4]])),
        reads=[h["cw0"]], writes=[cw_sb], dma=True)
    P.emit("sync", lambda s: s.dma_start(
        out=cb_sb[:], in_=bass.AP(h["cb0"], 0, [[1, 128], [128, 34], [1, 1]]),
        allow_slow_non_contiguous=True),
        reads=[h["cb0"]], writes=[cb_sb], dma=True)
    P.emit("sync", lambda s: s.dma_start(out=dtb_sb[:], in_=h["dtb0"][:, :]),
           reads=[h["dtb0"]], writes=[dtb_sb], dma=True)
    P.emit("sync", lambda s: s.dma_start(out=A_sb[:], in_=h["A0"][:, :]),
           reads=[h["A0"]], writes=[A_sb], dma=True)
    P.emit("gpsimd", lambda g: g.memset(xcbuf[:, 0:3], 0.0), writes=[xcbuf])

    for nt in range(67):
        b = nt % 2
        wb = wbuf[b]
        ncols = 64 if nt == 66 else 128
        P.emit("gpsimd", lambda g, nt=nt, wb=wb, ncols=ncols: g.dma_start(
            out=wb[:, :, 0:ncols],
            in_=bass.AP(h["w_in"], nt * 128,
                        [[DIN0, 128], [128 * DIN0, 16], [1, ncols]])),
            reads=[h["w_in"]], writes=[wb], dma=True)
        for tc in range(4):
            pb = ps[tc % 2]
            for k in range(16):
                P.emit("tensor", lambda te, wb=wb, k=k, tc=tc, pb=pb, ncols=ncols:
                       te.matmul(pb[0:ncols, :], wb[:, k, 0:ncols],
                                 xn_fm[:, k * T + tc * 512: k * T + (tc + 1) * 512],
                                 start=(k == 0), stop=(k == 15)),
                       reads=[wb, xn_fm], writes=[pb])
            if nt < 32:
                gt, gs = gtile[tc % 2], gstag[tc % 2]
                P.emit("scalar", lambda s, pb=pb, gt=gt: s.copy(
                    out=gt[:], in_=pb[:]), reads=[pb], writes=[gt])
                P.emit("scalar", lambda s, gt=gt, gs=gs: s.dma_start(
                    out=gs[:], in_=gt[:], transpose=True),
                    reads=[gt], writes=[gs], dma=True)
                P.emit("sync", lambda s, gs=gs, nt=nt, tc=tc: s.dma_start(
                    out=bass.AP(h["gate_tm_d"], (tc * 512) * INTER0 + nt * 128,
                                [[INTER0, 128], [128 * INTER0, 4], [1, 128]]),
                    in_=gs[:]), reads=[gs], writes=[(h["gate_tm_d"], (nt, tc))],
                    dma=True)
            elif nt < 66:
                P.emit("scalar", lambda s, pb=pb, tc=tc: s.copy(
                    out=xcbuf[:, 3 + tc * 512: 3 + (tc + 1) * 512], in_=pb[:]),
                    reads=[pb], writes=[xcbuf])
            else:
                P.emit("scalar", lambda s, pb=pb, tc=tc: s.copy(
                    out=dtraw[:, tc * 512:(tc + 1) * 512], in_=pb[0:64, :]),
                    reads=[pb], writes=[dtraw])
        if 32 <= nt < 66:
            j = nt - 32
            P.emit("vector", lambda v, j=j: v.tensor_scalar(
                out=cacc[:], in0=xcbuf[:, 0:T], scalar1=cw_sb[:, j, 0:1],
                scalar2=None, op0=ALU.mult),
                reads=[xcbuf, cw_sb], writes=[cacc])
            for k in range(1, 4):
                P.emit("vector", lambda v, j=j, k=k: v.scalar_tensor_tensor(
                    out=cacc[:], in0=xcbuf[:, k:k + T],
                    scalar=cw_sb[:, j, k:k + 1], in1=cacc[:],
                    op0=ALU.mult, op1=ALU.add),
                    reads=[xcbuf, cw_sb, cacc], writes=[cacc])
            dst = ctile[b] if j < 32 else (B_fm if j == 32 else C_fm)
            P.emit("scalar", lambda s, j=j, dst=dst: s.activation(
                out=dst[:], in_=cacc[:], func=AF.Silu, bias=cb_sb[:, j:j + 1]),
                reads=[cacc, cb_sb], writes=[dst])
            if j < 32:
                ct, tst = ctile[b], tstag[b]
                P.emit("scalar", lambda s, ct=ct, tst=tst: s.dma_start(
                    out=tst[:], in_=ct[:], transpose=True),
                    reads=[ct], writes=[tst], dma=True)
                P.emit("sync", lambda s, tst=tst, j=j: s.dma_start(
                    out=bass.AP(h["xc_tm_d"], j * 128,
                                [[INTER0, 128], [128 * INTER0, 16], [1, 128]]),
                    in_=tst[:]), reads=[tst], writes=[(h["xc_tm_d"], j)], dma=True)
    P.emit("vector", lambda v: v.tensor_scalar(
        out=dtraw[:], in0=dtraw[:], scalar1=dtb_sb[:, 0:1], scalar2=None,
        op0=ALU.add), reads=[dtraw, dtb_sb], writes=[dtraw])
    P.emit("vector", lambda v: v.tensor_scalar(
        out=dttmp[:], in0=dtraw[:], scalar1=60.0, scalar2=None,
        op0=ALU.min), reads=[dtraw], writes=[dttmp])
    P.emit("scalar", lambda s: s.activation(
        out=dttmp[:], in_=dttmp[:], func=AF.Exp), reads=[dttmp], writes=[dttmp])
    P.emit("vector", lambda v: v.tensor_scalar(
        out=dttmp[:], in0=dttmp[:], scalar1=1.0, scalar2=None,
        op0=ALU.add), reads=[dttmp], writes=[dttmp])
    P.emit("scalar", lambda s: s.activation(
        out=dttmp[:], in_=dttmp[:], func=AF.Ln), reads=[dttmp], writes=[dttmp])
    P.emit("vector", lambda v: v.tensor_tensor(
        out=dt_f[:], in0=dttmp[:], in1=dtraw[:], op=ALU.max),
        reads=[dttmp, dtraw], writes=[dt_f])
    res["_scratch"] = [wbuf[0], wbuf[1], xcbuf, cacc, ctile[0], ctile[1],
                       gtile[0], gtile[1], gstag[0], gstag[1], tstag[0],
                       tstag[1], dtraw, dttmp, cw_sb, cb_sb, dtb_sb]
    return res



def emit_l0_ssd(c, h, res):
    P = c.P
    NH, HD, NS, L = 64, 64, 128, 128
    NCH = T // L
    B_fm, C_fm, dt_f, A_sb = res["B_fm"], res["C_fm"], res["dt_f"], res["A_sb"]

    dA = c.sbuf("s_dA", [64, T], F32)
    cA = c.sbuf("s_cA", [64, T], F32)
    negcA = c.sbuf("s_ncA", [64, L], F32)
    ident = c.sbuf("s_id", [128, 128], F32)
    identb = c.sbuf("s_idb", [128, 128], BF16)
    maskMB = c.sbuf("s_mb", [128, 128], F32)
    ones1 = c.sbuf("s_one1", [1, 128], F32)
    lhsT_q = [c.sbuf(f"s_lhsT{i}", [2, 16 * L], F32) for i in range(2)]
    rhs_q = [c.sbuf(f"s_rhs{i}", [2, 16 * L], F32) for i in range(2)]
    stack = c.sbuf("s_stk", [128, 128], F32)
    trbuf = c.sbuf("s_tr", [128, 128], F32)
    ecat = c.sbuf("s_ecat", [128, 64], F32)
    replast = c.sbuf("s_rpl", [128, 64], F32)
    dsdtT = c.sbuf("s_dsdt", [128, 64], F32)
    cdrep = c.sbuf("s_cdr", [128, 64], F32)
    larow = c.sbuf("s_larow", [1, 64], F32)
    cbt = c.sbuf("s_cbt", [128, 128], BF16)
    btm = c.sbuf("s_btm", [128, 128], BF16)
    xcw = c.sbuf("s_xcw", [128, 4096], BF16)
    mh = [c.sbuf(f"s_mh{i}", [128, 512], BF16) for i in range(2)]
    etile = c.sbuf("s_et", [128, 512], F32)
    ytile = c.sbuf("s_yt", [128, 4096], BF16)
    yf32 = c.sbuf("s_yf", [128, 512], F32)
    S_sb = c.sbuf("s_S", [128, 4096], F32)
    S_bf = c.sbuf("s_Sb", [128, 4096], BF16)
    ddiag = c.sbuf("s_ddg", [128, NH * 128], BF16)
    drep = c.sbuf("s_drp", [128, 64], F32)
    dr_row = c.sbuf("s_drow", [1, 64], F32)
    xtm = [c.sbuf(f"s_xtm{i}", [128, 4096], BF16) for i in range(2)]

    pp = c.pp

    P.emit("gpsimd", lambda g: g.memset(ident[:], 1.0), writes=[ident])
    P.emit("gpsimd", lambda g: g.affine_select(
        out=ident[:], in_=ident[:], pattern=[[1, 128]],
        compare_op=ALU.is_equal, fill=0.0, base=0, channel_multiplier=-1),
        reads=[ident], writes=[ident])
    P.emit("vector", lambda v: v.tensor_copy(out=identb[:], in_=ident[:]),
           reads=[ident], writes=[identb])
    P.emit("gpsimd", lambda g: g.memset(maskMB[:], 0.0), writes=[maskMB])
    P.emit("gpsimd", lambda g: g.affine_select(
        out=maskMB[:], in_=maskMB[:], pattern=[[1, 128]],
        compare_op=ALU.is_ge, fill=-1e30, base=0, channel_multiplier=-1),
        reads=[maskMB], writes=[maskMB])
    P.emit("gpsimd", lambda g: g.memset(ones1[:], 1.0), writes=[ones1])
    for i in range(2):
        P.emit("gpsimd", lambda g, i=i: g.memset(lhsT_q[i][:, :], 1.0),
               writes=[(lhsT_q[i], "ones"), (lhsT_q[i], "cA")])
        P.emit("gpsimd", lambda g, i=i: g.memset(rhs_q[i][:, :], 1.0),
               writes=[(rhs_q[i], "ones"), (rhs_q[i], "cA")])
    P.emit("gpsimd", lambda g: g.memset(S_sb[:], 0.0), writes=[S_sb])
    P.emit("gpsimd", lambda g: g.memset(S_bf[:], 0.0), writes=[S_bf])
    P.emit("sync", lambda s: s.dma_start(out=dr_row[:], in_=h["D0r"][:, :]),
           reads=[h["D0r"]], writes=[dr_row], dma=True)
    P.emit("tensor", lambda te: te.matmul(
        pp[0][:, 0:64], ones1[:], dr_row[:], start=True, stop=True),
        reads=[ones1, dr_row], writes=[pp[0]])
    P.emit("scalar", lambda s: s.copy(out=drep[:], in_=pp[0][:, 0:64]),
           reads=[pp[0]], writes=[drep])
    for hh in range(NH):
        P.emit("vector", lambda v, hh=hh: v.tensor_scalar(
            out=ddiag[:, ts_(hh, 128)], in0=identb[:], scalar1=drep[:, hh:hh + 1],
            scalar2=None, op0=ALU.mult),
            reads=[identb, drep], writes=[(ddiag, hh)])
    P.emit("vector", lambda v: v.tensor_scalar(
        out=dA[:], in0=dt_f[:], scalar1=A_sb[:, 0:1], scalar2=None,
        op0=ALU.mult), reads=[dt_f, A_sb], writes=[dA])
    for ci in range(NCH):
        P.emit("vector", lambda v, ci=ci: v.tensor_tensor_scan(
            out=cA[:, ts_(ci, L)], data0=dA[:, ts_(ci, L)],
            data1=dA[:, ts_(ci, L)], initial=0.0, op0=ALU.add, op1=ALU.bypass),
            reads=[dA], writes=[(cA, ci)])

    for ci in range(NCH):
        bb = ci % 2
        cAc = (cA, ci)
        P.emit("sync", lambda s, ci=ci, d=xtm[bb]: s.dma_start(
            out=d[:], in_=h["xc_tm_d"][ts_(ci, 128), :]),
            reads=[(h["xc_tm_d"], j) for j in range(32)], writes=[xtm[bb]],
            dma=True)
        P.emit("vector", lambda v, ci=ci: v.tensor_scalar(
            out=negcA[:], in0=cA[:, ts_(ci, L)], scalar1=-1.0, scalar2=None,
            op0=ALU.mult), reads=[cAc], writes=[negcA])
        P.emit("vector", lambda v, ci=ci: v.tensor_copy(
            out=stack[0:64, :], in_=cA[:, ts_(ci, L)]),
            reads=[cAc], writes=[(stack, 0)])
        P.emit("vector", lambda v, ci=ci: v.tensor_copy(
            out=stack[64:128, :], in_=dt_f[:, ts_(ci, L)]),
            reads=[dt_f], writes=[(stack, 1)])
        P.emit("tensor", lambda te: te.transpose(pp[2][:, 0:128], stack[:], ident[:]),
               reads=[(stack, 0), (stack, 1), ident], writes=[pp[2]])
        P.emit("scalar", lambda s: s.copy(out=trbuf[:], in_=pp[2][:, 0:128]),
               reads=[pp[2]], writes=[trbuf])
        P.emit("tensor", lambda te, ci=ci: te.transpose(
            pp[3].bitcast(BF16)[:, 0:128], B_fm[:, ts_(ci, L)], identb[:]),
            reads=[B_fm, identb], writes=[pp[3]])
        P.emit("scalar", lambda s: s.copy(out=btm[:], in_=pp[3].bitcast(BF16)[:, 0:128]),
               reads=[pp[3]], writes=[btm])
        P.emit("scalar", lambda s, ci=ci: s.dma_start(
            out=larow[:], in_=cA[:, ci * L + L - 1: ci * L + L],
            allow_slow_non_contiguous=True),
            reads=[cAc], writes=[larow], dma=True)
        P.emit("tensor", lambda te: te.matmul(
            pp[2][:, 128:192], ones1[:], larow[:],
            start=True, stop=True), reads=[ones1, larow], writes=[pp[2]])
        P.emit("scalar", lambda s: s.copy(out=replast[:], in_=pp[2][:, 128:192]),
               reads=[pp[2]], writes=[replast])
        P.emit("scalar", lambda s: s.activation(
            out=ecat[:], in_=trbuf[:, 0:64], func=AF.Exp),
            reads=[trbuf], writes=[ecat])
        P.emit("vector", lambda v: v.tensor_tensor(
            out=dsdtT[:], in0=replast[:], in1=trbuf[:, 0:64], op=ALU.subtract),
            reads=[replast, trbuf], writes=[dsdtT])
        P.emit("scalar", lambda s: s.activation(
            out=dsdtT[:], in_=dsdtT[:], func=AF.Exp),
            reads=[dsdtT], writes=[dsdtT])
        P.emit("vector", lambda v: v.tensor_tensor(
            out=dsdtT[:], in0=dsdtT[:], in1=trbuf[:, 64:128], op=ALU.mult),
            reads=[dsdtT, trbuf], writes=[dsdtT])
        P.emit("scalar", lambda s: s.activation(
            out=cdrep[:], in_=replast[:], func=AF.Exp),
            reads=[replast], writes=[cdrep])
        P.emit("tensor", lambda te, ci=ci: te.matmul(
            pp[2][:, 256:384], B_fm[:, ts_(ci, L)],
            C_fm[:, ts_(ci, L)], start=True, stop=True),
            reads=[B_fm, C_fm], writes=[pp[2]])
        P.emit("scalar", lambda s: s.copy(out=cbt[:], in_=pp[2][:, 256:384]),
               reads=[pp[2]], writes=[cbt])
        P.emit("vector", lambda v, d=xtm[bb]: v.tensor_tensor(
            out=xcw[:], in0=d[:],
            in1=bass.AP(dsdtT, 0, [[64, 128], [1, 64], [0, 64]]), op=ALU.mult),
            reads=[xtm[bb], dsdtT], writes=[xcw])

        for q in range(4):
            qb = q % 2
            lq, rq = lhsT_q[qb], rhs_q[qb]
            P.emit("scalar", lambda s, q=q, lq=lq: s.dma_start(
                out=lq[0:1, :], in_=negcA[ts_(q, 16), :],
                allow_slow_non_contiguous=True),
                reads=[negcA], writes=[(lq, "cA")], dma=True)
            P.emit("scalar", lambda s, q=q, ci=ci, rq=rq: s.dma_start(
                out=rq[1:2, :], in_=cA[ts_(q, 16), ts_(ci, L)],
                allow_slow_non_contiguous=True),
                reads=[cAc], writes=[(rq, "cA")], dma=True)
            for g4 in range(4):
                gi = g4 % 2
                mtile = mh[gi]
                t1p = pp[2] if gi == 0 else pp[3]
                ya = pp[4 + gi]
                yb_ = pp[6 + gi]
                for hh4 in range(4):
                    hl = g4 * 4 + hh4
                    P.emit("tensor", lambda te, hl=hl, hh4=hh4, t1p=t1p,
                           lq=lq, rq=rq: te.matmul(
                        t1p[:, ts_(hh4, 128)], lq[:, ts_(hl, 128)],
                        rq[:, ts_(hl, 128)], start=True, stop=True),
                        reads=[(lq, "cA"), (lq, "ones"), (rq, "cA"),
                               (rq, "ones")], writes=[t1p])
                P.emit("vector", lambda v, t1p=t1p: v.tensor_tensor(
                    out=etile[:], in0=t1p[:],
                    in1=bass.AP(maskMB, 0, [[128, 128], [0, 4], [1, 128]]),
                    op=ALU.add), reads=[t1p, maskMB], writes=[etile])
                P.emit("scalar", lambda s: s.activation(
                    out=etile[:], in_=etile[:], func=AF.Exp),
                    reads=[etile], writes=[etile])
                hh0 = q * 16 + g4 * 4
                P.emit("vector", lambda v, hh0=hh0: v.tensor_tensor(
                    out=etile[:], in0=etile[:],
                    in1=bass.AP(trbuf, 64 + hh0, [[128, 128], [1, 4], [0, 128]]),
                    op=ALU.mult), reads=[etile, trbuf], writes=[etile])
                P.emit("vector", lambda v, mtile=mtile: v.tensor_tensor(
                    out=mtile[:], in0=etile[:],
                    in1=bass.AP(cbt, 0, [[128, 128], [0, 4], [1, 128]]),
                    op=ALU.mult), reads=[etile, cbt], writes=[mtile])
                for hh4 in range(4):
                    hh = hh0 + hh4
                    col = hh4 * 64
                    P.emit("tensor", lambda te, hh=hh, hh4=hh4, mtile=mtile,
                           ya=ya, col=col, d=xtm[bb]: te.matmul(
                        ya[:, col:col + 64], mtile[:, ts_(hh4, 128)],
                        d[:, ts_(hh, 64)], start=True, stop=False),
                        reads=[mtile, xtm[bb]], writes=[ya])
                    P.emit("tensor", lambda te, hh=hh, ya=ya, col=col,
                           d=xtm[bb]: te.matmul(
                        ya[:, col:col + 64], ddiag[:, ts_(hh, 128)],
                        d[:, ts_(hh, 64)], start=False, stop=True),
                        reads=[(ddiag, hh), xtm[bb]], writes=[ya])
                    P.emit("tensor", lambda te, hh=hh, ci=ci, yb_=yb_,
                           col=col: te.matmul(
                        yb_[:, col:col + 64], C_fm[:, ts_(ci, 128)],
                        S_bf[:, ts_(hh, 64)], start=True, stop=True),
                        reads=[C_fm, S_bf], writes=[yb_])
                P.emit("vector", lambda v, yb_=yb_, hh0=hh0: v.tensor_tensor(
                    out=yf32[:, 0:256], in0=yb_[:, 0:256],
                    in1=bass.AP(ecat, hh0, [[64, 128], [1, 4], [0, 64]]),
                    op=ALU.mult), reads=[yb_, ecat], writes=[yf32])
                P.emit("vector", lambda v, ya=ya, hh0=hh0: v.tensor_tensor(
                    out=ytile[:, hh0 * 64:hh0 * 64 + 256], in0=yf32[:, 0:256],
                    in1=ya[:, 0:256],
                    op=ALU.add), reads=[yf32, ya], writes=[(ytile, hh0)])
        for g8 in range(8):
            stp = pp[4 + (g8 % 2)]
            for hh8 in range(8):
                hh = g8 * 8 + hh8
                P.emit("tensor", lambda te, hh=hh, hh8=hh8, stp=stp: te.matmul(
                    stp[:, hh8 * 64:(hh8 + 1) * 64], btm[:],
                    xcw[:, ts_(hh, 64)], start=True, stop=True),
                    reads=[btm, xcw], writes=[stp])
            P.emit("vector", lambda v, g8=g8: v.tensor_tensor(
                out=S_sb[:, g8 * 512:(g8 + 1) * 512],
                in0=S_sb[:, g8 * 512:(g8 + 1) * 512],
                in1=bass.AP(cdrep, g8 * 8, [[64, 128], [1, 8], [0, 64]]),
                op=ALU.mult), reads=[S_sb, cdrep], writes=[S_sb])
            P.emit("vector", lambda v, g8=g8, stp=stp: v.tensor_tensor(
                out=S_sb[:, g8 * 512:(g8 + 1) * 512],
                in0=S_sb[:, g8 * 512:(g8 + 1) * 512], in1=stp[:],
                op=ALU.add), reads=[S_sb, stp], writes=[S_sb])
            P.emit("vector", lambda v, g8=g8: v.tensor_copy(
                out=S_bf[:, g8 * 512:(g8 + 1) * 512],
                in_=S_sb[:, g8 * 512:(g8 + 1) * 512]),
                reads=[S_sb], writes=[S_bf])
        P.emit("gpsimd", lambda g, ci=ci: g.dma_start(
            out=h["y_tm_d"][ts_(ci, 128), :], in_=ytile[:]),
            reads=[(ytile, hh0) for hh0 in range(0, 64, 4)],
            writes=[(h["y_tm_d"], ci)], dma=True)
    res["_scratch2"] = [dA, cA, negcA, ident, identb, maskMB, ones1,
                        lhsT_q[0], lhsT_q[1], rhs_q[0], rhs_q[1], stack,
                        trbuf, ecat, replast, dsdtT, cdrep, larow, cbt, btm,
                        xcw, mh[0], mh[1], etile, ytile, yf32, S_sb, S_bf,
                        ddiag, drep, dr_row, xtm[0], xtm[1]]
    return res



def emit_l0_zout(c, h, res):
    P = c.P
    z_fm = c.sbuf("z_fm", [128, 32 * T], BF16, side="right")
    ybuf = [c.sbuf("z_y0", [128, 4096], BF16)] * 2
    gbuf = [c.sbuf("z_g0", [128, 4096], BF16)] * 2
    sg = c.sbuf("z_sg", [128, 4096], BF16)
    u = c.sbuf("z_u", [128, 4096], F32)
    sq = c.sbuf("z_sq", [128, 4096], BF16)
    zst = c.sbuf("z_st", [128, 4], F32)
    ztile = [c.sbuf("z_zt0", [128, 4096], BF16)] * 2
    zstag = [c.sbuf("z_zs0", [128, 32, 128], BF16)] * 2

    for t in range(NT):
        b = t % 2
        yb, gb, zt, zs = ybuf[b], gbuf[b], ztile[b], zstag[b]
        P.emit("sync", lambda s, t=t, yb=yb: s.dma_start(
            out=yb[:], in_=h["y_tm_d"][ts_(t, 128), :]),
            reads=[(h["y_tm_d"], t)], writes=[yb], dma=True)
        P.emit("sync", lambda s, t=t, gb=gb: s.dma_start(
            out=gb[:], in_=h["gate_tm_d"][ts_(t, 128), :]),
            reads=[(h["gate_tm_d"], (nt, tc)) for nt in range(32)
                   for tc in range(4)], writes=[gb], dma=True)
        P.emit("scalar", lambda s, gb=gb: s.activation(
            out=sg[:], in_=gb[:], func=AF.Silu), reads=[gb], writes=[sg])
        P.emit("vector", lambda v, yb=yb: v.tensor_tensor(
            out=u[:], in0=yb[:], in1=sg[:], op=ALU.mult),
            reads=[yb, sg], writes=[u])
        P.emit("scalar", lambda s: s.activation(
            out=sq[:], in_=u[:], func=AF.Square), reads=[u], writes=[sq])
        P.emit("vector", lambda v: v.reduce_sum(
            out=zst[:, 0:1], in_=sq[:], axis=X), reads=[sq], writes=[(zst, 0)])
        P.emit("vector", lambda v: v.tensor_scalar(
            out=zst[:, 1:2], in0=zst[:, 0:1], scalar1=1.0 / 4096, scalar2=1e-5,
            op0=ALU.mult, op1=ALU.add), reads=[(zst, 0)], writes=[(zst, 1)])
        P.emit("vector", lambda v: v.reciprocal(
            out=zst[:, 2:3], in_=zst[:, 1:2]), reads=[(zst, 1)], writes=[(zst, 2)])
        P.emit("scalar", lambda s: s.activation(
            out=zst[:, 3:4], in_=zst[:, 2:3], func=AF.Ln),
            reads=[(zst, 2)], writes=[(zst, 3)])
        P.emit("scalar", lambda s: s.activation(
            out=zst[:, 3:4], in_=zst[:, 3:4], func=AF.Exp, scale=0.5),
            reads=[(zst, 3)], writes=[(zst, 3)])
        P.emit("vector", lambda v, zt=zt: v.tensor_scalar(
            out=zt[:], in0=u[:], scalar1=zst[:, 3:4], scalar2=None,
            op0=ALU.mult), reads=[u, (zst, 3)], writes=[zt])
        P.emit("scalar", lambda s, zt=zt, zs=zs: s.dma_start(
            out=zs[:], in_=zt[:], transpose=True),
            reads=[zt], writes=[zs], dma=True)
        P.emit("gpsimd", lambda g, zs=zs, t=t: g.dma_start(
            out=bass.AP(z_fm, t * 128, [[32 * T, 128], [T, 32], [1, 128]]),
            in_=zs[:]), reads=[zs], writes=[(z_fm, t)], dma=True)
    res["z_fm"] = z_fm
    res["_scratch3"] = [ybuf[0], gbuf[0], sg, u, sq, zst,
                        ztile[0], zstag[0]]
    return res


def emit_l0_wout(c, h, res):
    P = c.P
    z_fm = res["z_fm"]
    nw_sb = c.sbuf("wo_nw", [128, 32], F32)
    wdc = c.sbuf("wo_w", [128, 32, 256], BF16)
    x0b = [c.sbuf(f"wo_x{i}", [128, 256], F32) for i in range(2)]
    ob = [c.sbuf(f"wo_o{i}", [128, 256], F32) for i in range(2)]
    pp = c.pp

    P.emit("sync", lambda s: s.dma_start(
        out=nw_sb[:], in_=bass.AP(h["nw0"], 0, [[1, 128], [128, 32], [1, 1]]),
        allow_slow_non_contiguous=True),
        reads=[h["nw0"]], writes=[nw_sb], dma=True)

    for dc in range(8):
        P.emit("gpsimd", lambda g, dc=dc: g.dma_start(
            out=wdc[:], in_=bass.AP(h["w_out0"], dc * 256,
                                    [[D, 128], [128 * D, 32], [1, 256]])),
            reads=[h["w_out0"]], writes=[(wdc, "raw")], dma=True)
        for k in range(32):
            P.emit("vector", lambda v, k=k: v.tensor_scalar(
                out=wdc[:, k, :], in0=wdc[:, k, :], scalar1=nw_sb[:, k:k + 1],
                scalar2=None, op0=ALU.mult),
                reads=[(wdc, "raw"), nw_sb], writes=[(wdc, "raw"), (wdc, k)])
        for t in range(NT):
            bank = pp[t // 2]
            half = (t % 2) * 256
            for k in range(32):
                P.emit("tensor", lambda te, k=k, t=t, bank=bank, half=half:
                       te.matmul(
                    bank[:, half:half + 256],
                    z_fm[:, k * T + t * 128: k * T + (t + 1) * 128],
                    wdc[:, k, :], start=(k == 0), stop=(k == 31)),
                    reads=[(z_fm, t), (wdc, k)], writes=[bank])
            b2 = t % 2
            P.emit("sync", lambda s, t=t, dc=dc, xb=x0b[b2]: s.dma_start(
                out=xb[:], in_=h["x0"][ts_(t, 128), dc * 256:(dc + 1) * 256]),
                reads=[h["x0"]], writes=[x0b[b2]], dma=True)
            P.emit("vector", lambda v, bank=bank, half=half, xb=x0b[b2],
                   o=ob[b2]: v.tensor_tensor(
                out=o[:], in0=bank[:, half:half + 256], in1=xb[:], op=ALU.add),
                reads=[bank, x0b[b2]], writes=[ob[b2]])
            if "xf_d" in h:
                P.emit("sync", lambda s, t=t, dc=dc, xb=x0b[b2]: s.dma_start(
                    out=xb[:],
                    in_=h["xf_d"][ts_(t, 128), dc * 256:(dc + 1) * 256]),
                    reads=[h["xf_d"]], writes=[x0b[b2]], dma=True)
                P.emit("vector", lambda v, xb=x0b[b2], o=ob[b2]:
                       v.tensor_tensor(
                    out=o[:], in0=o[:], in1=xb[:], op=ALU.add),
                    reads=[ob[b2], x0b[b2]], writes=[ob[b2]])
            P.emit("gpsimd", lambda g, t=t, dc=dc, o=ob[b2]: g.dma_start(
                out=h["x1p_d"][ts_(t, 128), dc * 256:(dc + 1) * 256], in_=o[:]),
                reads=[ob[b2]], writes=[(h["x1p_d"], (t, dc))], dma=True)
    res["_scratch4"] = [nw_sb, wdc, x0b[0], x0b[1], ob[0], ob[1]]
    return res



def emit_exact_exp2(c, a_ap_fn, a_reads, out, out_key, tb, tf, ti, pc, W):
    P = c.P
    P.emit("vector", lambda v: v.tensor_copy(out=tb[0:pc, 0:W], in_=a_ap_fn()),
           reads=list(a_reads), writes=[tb])
    P.emit("vector", lambda v: v.tensor_copy(out=tf[0:pc, 0:W],
                                             in_=tb[0:pc, 0:W]),
           reads=[tb], writes=[tf])
    P.emit("vector", lambda v: v.tensor_copy(
        out=out[0:pc, 0:W], in_=tf.bitcast(I32)[0:pc, 0:W]),
        reads=[tf], writes=[out_key])
    P.emit("vector", lambda v: v.tensor_scalar(
        out=out[0:pc, 0:W], in0=out[0:pc, 0:W], scalar1=2.0 ** -23,
        scalar2=-127.0, op0=ALU.mult, op1=ALU.add),
        reads=[out_key], writes=[out_key])
    P.emit("vector", lambda v: v.tensor_copy(out=ti[0:pc, 0:W],
                                             in_=out[0:pc, 0:W]),
           reads=[out_key], writes=[ti])
    P.emit("vector", lambda v: v.tensor_copy(out=tf[0:pc, 0:W],
                                             in_=ti[0:pc, 0:W]),
           reads=[ti], writes=[tf])
    P.emit("vector", lambda v: v.tensor_tensor(
        out=out[0:pc, 0:W], in0=tf[0:pc, 0:W], in1=out[0:pc, 0:W],
        op=ALU.is_gt), reads=[tf, out_key], writes=[out_key])
    P.emit("vector", lambda v: v.tensor_tensor(
        out=out[0:pc, 0:W], in0=tf[0:pc, 0:W], in1=out[0:pc, 0:W],
        op=ALU.subtract), reads=[tf, out_key], writes=[out_key])
    P.emit("vector", lambda v: v.tensor_scalar(
        out=tf[0:pc, 0:W], in0=out[0:pc, 0:W], scalar1=float(2 ** 23),
        scalar2=float(127 * 2 ** 23), op0=ALU.mult, op1=ALU.add),
        reads=[out_key], writes=[tf])
    P.emit("vector", lambda v: v.tensor_copy(out=ti[0:pc, 0:W],
                                             in_=tf[0:pc, 0:W]),
           reads=[tf], writes=[ti])
    P.emit("vector", lambda v: v.tensor_tensor(
        out=tf[0:pc, 0:W], in0=ti.bitcast(F32)[0:pc, 0:W], in1=a_ap_fn(),
        op=ALU.is_gt), reads=[ti] + list(a_reads), writes=[tf])
    P.emit("vector", lambda v: v.tensor_tensor(
        out=out[0:pc, 0:W], in0=out[0:pc, 0:W], in1=tf[0:pc, 0:W],
        op=ALU.subtract), reads=[out_key, tf], writes=[out_key])


def mk_mx_scratch(c, W, tag, rne=False, share_qb=None):
    nb = W // 32
    scr = {
        "qs": c.sbuf(tag + "_qs", [128, W], F32),
        "qp": c.sbuf(tag + "_qp", [128, W], F32),
        "qe": c.sbuf(tag + "_qe", [128, W], F32),
        "qm": c.sbuf(tag + "_qm", [128, W], F32),
        "qi": c.sbuf(tag + "_qi", [128, W], I32),
        "am": c.sbuf(tag + "_am", [128, nb], F32),
        "lnm": c.sbuf(tag + "_lnm", [128, nb], F32),
        "sef": c.sbuf(tag + "_sef", [128, nb], F32),
        "sei": c.sbuf(tag + "_sei", [128, nb], I32),
        "ivs": c.sbuf(tag + "_ivs", [128, nb], F32),
        "sv": c.sbuf(tag + "_sv", [128, nb], F32),
        "amb": c.sbuf(tag + "_amb", [128, nb], BF16),
        "W": W,
        "list": None,
    }
    if rne:
        scr["qb"] = share_qb if share_qb is not None else c.sbuf(
            tag + "_qb", [128, W], BF16)
        scr["qx"] = c.sbuf(tag + "_qx", [128, W], I32)
        scr["qb_shared"] = share_qb is not None
    return scr


def mx_scratch_list(scr):
    ks = ["qs", "qp", "qe", "qm", "qi", "am", "lnm", "sef", "sei", "ivs",
          "sv", "amb"]
    if "qb" in scr:
        ks += ["qb", "qx"]
    if scr.get("qb_shared"):
        ks.remove("qb")
    return [scr[k] for k in ks]


def emit_mx_quant(c, src, src_key, dst, dst_key, scr, mfrac, emax, maxn,
                  pcount=128, src_off=0, dst_off=0, rne=False):
    P = c.P
    W = scr["W"]
    nb = W // 32
    qs, qp, qe, qm, qi = scr["qs"], scr["qp"], scr["qe"], scr["qm"], scr["qi"]
    am, lnm, sef, sei = scr["am"], scr["lnm"], scr["sef"], scr["sei"]
    ivs, sv, amb = scr["ivs"], scr["sv"], scr["amb"]
    pc = pcount
    bcst_i = lambda: bass.AP(sei.bitcast(F32), 0, [[nb, pc], [1, nb], [0, 32]])
    for v in (mfrac * LN2, -mfrac * LN2):
        c.const(v)
    srcK = [src_key] if not isinstance(src_key, list) else src_key

    sw = src.shape[1]
    P.emit("vector", lambda v: v.tensor_reduce(
        out=am[0:pc, :], in_=bass.AP(src, src_off, [[sw, pc], [32, nb], [1, 32]]),
        axis=X, op=ALU.max, apply_absolute_value=True),
        reads=srcK, writes=[am])
    P.emit("vector", lambda v: v.tensor_scalar(
        out=am[0:pc, :], in0=am[0:pc, :], scalar1=2.0 ** -30, scalar2=None,
        op0=ALU.max), reads=[am], writes=[am])
    emit_exact_exp2(c, lambda: am[0:pc, :], [am], sef, sef, amb, lnm, sei,
                    pc, nb)
    P.emit("vector", lambda v: v.tensor_scalar(
        out=ivs[0:pc, :], in0=sef[0:pc, :], scalar1=-float(2 ** 23),
        scalar2=float((127 + emax) * 2 ** 23), op0=ALU.mult, op1=ALU.add),
        reads=[sef], writes=[ivs])
    P.emit("vector", lambda v: v.tensor_copy(out=sei[0:pc, :], in_=ivs[0:pc, :]),
           reads=[ivs], writes=[sei])
    P.emit("vector", lambda v: v.tensor_tensor(
        out=qs[0:pc, :], in0=bass.AP(src, src_off, [[sw, pc], [1, W]]),
        in1=bcst_i(), op=ALU.mult), reads=srcK + [sei], writes=[qs])

    if not rne:
        P.emit("scalar", lambda s: s.activation(
            out=qp[0:pc, :], in_=qs[0:pc, :], func=AF.Square),
            reads=[qs], writes=[qp])
        P.emit("vector", lambda v: v.tensor_scalar(
            out=qp[0:pc, :], in0=qp[0:pc, :], scalar1=1e-30, scalar2=None,
            op0=ALU.max), reads=[qp], writes=[qp])
        P.emit("scalar", lambda s: s.activation(
            out=qp[0:pc, :], in_=qp[0:pc, :], func=AF.Ln),
            reads=[qp], writes=[qp])
        P.emit("vector", lambda v: v.tensor_scalar(
            out=qp[0:pc, :], in0=qp[0:pc, :], scalar1=1.0 / (2 * LN2),
            scalar2=1e-3, op0=ALU.mult, op1=ALU.add), reads=[qp], writes=[qp])
        P.emit("vector", lambda v: v.tensor_scalar(
            out=qp[0:pc, :], in0=qp[0:pc, :], scalar1=0.0, scalar2=None,
            op0=ALU.max), reads=[qp], writes=[qp])
        P.emit("vector", lambda v: v.tensor_copy(out=qi[0:pc, :],
                                                 in_=qp[0:pc, :]),
               reads=[qp], writes=[qi])
        P.emit("vector", lambda v: v.tensor_copy(out=qe[0:pc, :],
                                                 in_=qi[0:pc, :]),
               reads=[qi], writes=[qe])
        P.emit("vector", lambda v: v.tensor_tensor(
            out=qm[0:pc, :], in0=qe[0:pc, :], in1=qp[0:pc, :], op=ALU.is_gt),
            reads=[qe, qp], writes=[qm])
        P.emit("vector", lambda v: v.tensor_tensor(
            out=qp[0:pc, :], in0=qe[0:pc, :], in1=qm[0:pc, :],
            op=ALU.subtract), reads=[qe, qm], writes=[qp])
        P.emit("scalar", lambda s: s.activation(
            out=qe[0:pc, :], in_=qp[0:pc, :], func=AF.Exp, scale=-LN2,
            bias=c.const(mfrac * LN2)),
            reads=[qp, c.cread(mfrac * LN2)], writes=[qe])
        P.emit("vector", lambda v: v.tensor_tensor(
            out=qs[0:pc, :], in0=qs[0:pc, :], in1=qe[0:pc, :], op=ALU.mult),
            reads=[qs, qe], writes=[qs])
        P.emit("vector", lambda v: v.tensor_scalar(
            out=qs[0:pc, :], in0=qs[0:pc, :], scalar1=0.5, scalar2=None,
            op0=ALU.add), reads=[qs], writes=[qs])
        P.emit("vector", lambda v: v.tensor_copy(out=qi[0:pc, :],
                                                 in_=qs[0:pc, :]),
               reads=[qs], writes=[qi])
        P.emit("vector", lambda v: v.tensor_copy(out=qe[0:pc, :],
                                                 in_=qi[0:pc, :]),
               reads=[qi], writes=[qe])
        P.emit("vector", lambda v: v.tensor_tensor(
            out=qm[0:pc, :], in0=qe[0:pc, :], in1=qs[0:pc, :], op=ALU.is_gt),
            reads=[qe, qs], writes=[qm])
        P.emit("vector", lambda v: v.tensor_tensor(
            out=qs[0:pc, :], in0=qe[0:pc, :], in1=qm[0:pc, :],
            op=ALU.subtract), reads=[qe, qm], writes=[qs])
        P.emit("scalar", lambda s: s.activation(
            out=qe[0:pc, :], in_=qp[0:pc, :], func=AF.Exp, scale=LN2,
            bias=c.const(-mfrac * LN2)),
            reads=[qp, c.cread(-mfrac * LN2)], writes=[qe])
        P.emit("vector", lambda v: v.tensor_tensor(
            out=qs[0:pc, :], in0=qs[0:pc, :], in1=qe[0:pc, :], op=ALU.mult),
            reads=[qs, qe], writes=[qs])
        P.emit("vector", lambda v: v.tensor_scalar(
            out=qs[0:pc, :], in0=qs[0:pc, :], scalar1=float(maxn),
            scalar2=float(-maxn), op0=ALU.min, op1=ALU.max),
            reads=[qs], writes=[qs])
        qfin = qs
    else:
        qb, qx = scr["qb"], scr["qx"]
        P.emit("scalar", lambda s: s.activation(
            out=qe[0:pc, :], in_=qs[0:pc, :], func=AF.Abs),
            reads=[qs], writes=[qe])
        emit_exact_exp2(c, lambda: qe[0:pc, :], [qe], qp, qp, qb, qm, qi,
                        pc, W)
        P.emit("vector", lambda v: v.tensor_scalar(
            out=qp[0:pc, :], in0=qp[0:pc, :], scalar1=0.0, scalar2=None,
            op0=ALU.max), reads=[qp], writes=[qp])
        P.emit("vector", lambda v: v.tensor_scalar(
            out=qm[0:pc, :], in0=qp[0:pc, :], scalar1=-float(2 ** 23),
            scalar2=float((127 + mfrac) * 2 ** 23), op0=ALU.mult, op1=ALU.add),
            reads=[qp], writes=[qm])
        P.emit("vector", lambda v: v.tensor_copy(out=qx[0:pc, :],
                                                 in_=qm[0:pc, :]),
               reads=[qm], writes=[qx])
        P.emit("vector", lambda v: v.tensor_tensor(
            out=qs[0:pc, :], in0=qs[0:pc, :],
            in1=qx.bitcast(F32)[0:pc, 0:W], op=ALU.mult),
            reads=[qs, qx], writes=[qs])
        P.emit("vector", lambda v: v.tensor_scalar(
            out=qe[0:pc, :], in0=qs[0:pc, :], scalar1=0.5, scalar2=None,
            op0=ALU.add), reads=[qs], writes=[qe])
        P.emit("vector", lambda v: v.tensor_copy(out=qi[0:pc, :],
                                                 in_=qe[0:pc, :]),
               reads=[qe], writes=[qi])
        P.emit("vector", lambda v: v.tensor_copy(out=qm[0:pc, :],
                                                 in_=qi[0:pc, :]),
               reads=[qi], writes=[qm])
        P.emit("vector", lambda v: v.tensor_tensor(
            out=qe[0:pc, :], in0=qm[0:pc, :], in1=qe[0:pc, :], op=ALU.is_gt),
            reads=[qm, qe], writes=[qe])
        P.emit("vector", lambda v: v.tensor_tensor(
            out=qm[0:pc, :], in0=qm[0:pc, :], in1=qe[0:pc, :],
            op=ALU.subtract), reads=[qm, qe], writes=[qm])
        P.emit("vector", lambda v: v.scalar_tensor_tensor(
            out=qe[0:pc, :], in0=qm[0:pc, :], scalar=0.5, in1=qs[0:pc, :],
            op0=ALU.subtract, op1=ALU.is_equal),
            reads=[qm, qs], writes=[qe])
        P.emit("vector", lambda v: v.tensor_scalar(
            out=qs[0:pc, :], in0=qp[0:pc, :], scalar1=float(2 ** 23),
            scalar2=float((127 - mfrac) * 2 ** 23), op0=ALU.mult, op1=ALU.add),
            reads=[qp], writes=[qs])
        P.emit("vector", lambda v: v.tensor_copy(out=qx[0:pc, :],
                                                 in_=qs[0:pc, :]),
               reads=[qs], writes=[qx])
        P.emit("vector", lambda v: v.tensor_scalar(
            out=qp[0:pc, :], in0=qm[0:pc, :], scalar1=0.5, scalar2=None,
            op0=ALU.mult), reads=[qm], writes=[qp])
        P.emit("vector", lambda v: v.tensor_copy(out=qi[0:pc, :],
                                                 in_=qp[0:pc, :]),
               reads=[qp], writes=[qi])
        P.emit("vector", lambda v: v.tensor_copy(out=qs[0:pc, :],
                                                 in_=qi[0:pc, :]),
               reads=[qi], writes=[qs])
        P.emit("vector", lambda v: v.tensor_tensor(
            out=qs[0:pc, :], in0=qs[0:pc, :], in1=qp[0:pc, :],
            op=ALU.subtract), reads=[qs, qp], writes=[qs])
        P.emit("scalar", lambda s: s.activation(
            out=qs[0:pc, :], in_=qs[0:pc, :], func=AF.Square, scale=2.0),
            reads=[qs], writes=[qs])
        P.emit("vector", lambda v: v.tensor_tensor(
            out=qe[0:pc, :], in0=qe[0:pc, :], in1=qs[0:pc, :], op=ALU.mult),
            reads=[qe, qs], writes=[qe])
        P.emit("vector", lambda v: v.tensor_tensor(
            out=qm[0:pc, :], in0=qm[0:pc, :], in1=qe[0:pc, :],
            op=ALU.subtract), reads=[qm, qe], writes=[qm])
        P.emit("vector", lambda v: v.tensor_tensor(
            out=qm[0:pc, :], in0=qm[0:pc, :],
            in1=qx.bitcast(F32)[0:pc, 0:W], op=ALU.mult),
            reads=[qm, qx], writes=[qm])
        P.emit("vector", lambda v: v.tensor_scalar(
            out=qm[0:pc, :], in0=qm[0:pc, :], scalar1=float(maxn),
            scalar2=float(-maxn), op0=ALU.min, op1=ALU.max),
            reads=[qm], writes=[qm])
        qfin = qm

    P.emit("vector", lambda v: v.tensor_scalar(
        out=sv[0:pc, :], in0=sef[0:pc, :], scalar1=float(2 ** 23),
        scalar2=float((127 - emax) * 2 ** 23), op0=ALU.mult, op1=ALU.add),
        reads=[sef], writes=[sv])
    P.emit("vector", lambda v: v.tensor_copy(out=sei[0:pc, :], in_=sv[0:pc, :]),
           reads=[sv], writes=[sei])
    dw = dst.shape[1]
    P.emit("vector", lambda v, qfin=qfin: v.tensor_tensor(
        out=bass.AP(dst, dst_off, [[dw, pc], [1, W]]), in0=qfin[0:pc, :],
        in1=bcst_i(), op=ALU.mult),
        reads=[qfin, sei], writes=[dst_key])



def emit_l0_ffn(c, xnq_fm, h, res):
    P = c.P
    pp = c.pp
    scrA = mk_mx_scratch(c, 1024, "f0qA", rne=True)
    scrs = [scrA, scrA]
    W2 = scrs[0]["W"]
    qcnt = [0]
    wgq = c.sbuf("f0_wgq", [128, 4096], BF16)
    wuq = c.sbuf("f0_wuq", [128, 4096], BF16)
    wq2 = c.sbuf("f0_wq2", [128, 2048], BF16)
    gate_sb = c.sbuf("f0_gate", [128, 4096], BF16)
    sg = c.sbuf("f0_sg", [128, 256], BF16)
    inter_sb = c.sbuf("f0_int", [128, 2048], F32)
    intq = wq2
    tstag = c.sbuf("f0_tst", [128, 2048], BF16)
    interq_q = c.sbuf("f0_iq", [128, 16 * T], BF16, side="right")
    res["interq_q"] = interq_q
    wdall = c.sbuf("f0_wda", [128, 16, 256], BF16)
    wdt = [None, None]
    x1b = [c.sbuf("f0_x10", [128, 256], F32)] * 2
    ob = [c.sbuf("f0_o0", [128, 256], F32)] * 2
    I0 = 8192

    def quant_band(src_rows, wh, colw, dst, dst_key, mfrac, maxn):
        for cc in range(colw // W2):
            scr = scrs[qcnt[0] % 2]
            qcnt[0] += 1
            wraw = scr["qp"]
            P.emit("sync", lambda s, cc=cc, src_rows=src_rows, wh=wh,
                   wraw=wraw, scr=scr: s.dma_start(
                out=wraw[:, :],
                in_=wh[src_rows:src_rows + 128, cc * W2:(cc + 1) * W2]),
                reads=[wh], writes=[scr["qp"]], dma=True)
            emit_mx_quant(c, wraw, scr["qp"], dst, dst_key, scr,
                          mfrac=mfrac, emax=2.0, maxn=maxn, dst_off=cc * W2,
                          rne=True)

    for hb in range(16):
        for cc in range(4):
            quant_band(hb * 128, h["wd"], 0, wq2, wq2, 1.0, 6.0) if False else None
            for sub in range(2048 // W2):
                scr = scrs[qcnt[0] % 2]
                qcnt[0] += 1
                wraw = scr["qp"]
                P.emit("sync", lambda s, hb=hb, cc=cc, sub=sub, wraw=wraw,
                       scr=scr: s.dma_start(
                    out=wraw[:, :],
                    in_=h["wd"][ts_(hb, 128),
                                cc * 2048 + sub * W2:cc * 2048 + (sub + 1) * W2]),
                    reads=[h["wd"]], writes=[scr["qp"]], dma=True)
                emit_mx_quant(c, wraw, scr["qp"], wq2, wq2, scr,
                              mfrac=1.0, emax=2.0, maxn=6.0, dst_off=sub * W2,
                              rne=True)
            P.emit("scalar", lambda s: s.dma_start(
                out=bass.AP(tstag, 0, [[2048, 128], [128, 16], [1, 128]]),
                in_=wq2[:], transpose=True),
                reads=[wq2], writes=[(tstag, "w")], dma=True)
            P.emit("sync", lambda s, hb=hb, cc=cc: s.dma_start(
                out=bass.AP(h["wdq_fm_d"], cc * 2048 * D + hb * 128,
                            [[D, 128], [128 * D, 16], [1, 128]]),
                in_=bass.AP(tstag, 0, [[2048, 128], [128, 16], [1, 128]])),
                reads=[(tstag, "w")], writes=[(h["wdq_fm_d"], (hb, cc))],
                dma=True)

    for p4 in range(4):
        for n in range(8):
            ibase = p4 * 2048 + n * 256
            for wh, dst in ((h["wg"], wgq), (h["wu"], wuq)):
                for band in range(2):
                    quant_band(ibase + band * 128, wh, 2048, wq2, wq2,
                               1.0, 6.0)
                    P.emit("scalar", lambda s, dst=dst, band=band: s.dma_start(
                        out=bass.AP(dst, band * 128,
                                    [[4096, 128], [256, 16], [1, 128]]),
                        in_=wq2[:], transpose=True),
                        reads=[wq2], writes=[(dst, band)], dma=True)
            for t in range(NT):
                bank, half = pp[t // 2], (t % 2) * 256
                for k in range(16):
                    P.emit("tensor", lambda te, t=t, k=k, bank=bank, half=half:
                           te.matmul(
                        bank[:, half:half + 256],
                        xnq_fm[:, k * T + t * 128:k * T + (t + 1) * 128],
                        wgq[:, k * 256:(k + 1) * 256],
                        start=(k == 0), stop=(k == 15)),
                        reads=[(xnq_fm, t), (wgq, 0), (wgq, 1)],
                        writes=[bank])
                P.emit("scalar", lambda s, t=t, bank=bank, half=half: s.copy(
                    out=gate_sb[:, t * 256:(t + 1) * 256],
                    in_=bank[:, half:half + 256]),
                    reads=[bank], writes=[(gate_sb, t)])
            for th in range(2):
                for tt in range(8):
                    t = th * 8 + tt
                    bank, half = pp[t // 2], (t % 2) * 256
                    for k in range(16):
                        P.emit("tensor", lambda te, t=t, k=k, bank=bank,
                               half=half: te.matmul(
                            bank[:, half:half + 256],
                            xnq_fm[:, k * T + t * 128:k * T + (t + 1) * 128],
                            wuq[:, k * 256:(k + 1) * 256],
                            start=(k == 0), stop=(k == 15)),
                            reads=[(xnq_fm, t), (wuq, 0), (wuq, 1)],
                            writes=[bank])
                    P.emit("scalar", lambda s, t=t: s.activation(
                        out=sg[:], in_=gate_sb[:, t * 256:(t + 1) * 256],
                        func=AF.Silu), reads=[(gate_sb, t)], writes=[sg])
                    P.emit("vector", lambda v, tt=tt, bank=bank, half=half:
                           v.tensor_tensor(
                        out=inter_sb[:, tt * 256:(tt + 1) * 256], in0=sg[:],
                        in1=bank[:, half:half + 256], op=ALU.mult),
                        reads=[sg, bank], writes=[(inter_sb, tt)])
                for sub in range(2048 // W2):
                    scr = scrs[qcnt[0] % 2]
                    qcnt[0] += 1
                    emit_mx_quant(c, inter_sb,
                                  [(inter_sb, tt) for tt in range(8)],
                                  intq, intq, scr, mfrac=3.0, emax=2.0,
                                  maxn=7.5, src_off=sub * W2,
                                  dst_off=sub * W2)
                P.emit("scalar", lambda s: s.dma_start(
                    out=bass.AP(tstag, 0, [[2048, 128], [128, 16], [1, 128]]),
                    in_=intq[:], transpose=True),
                    reads=[intq], writes=[(tstag, "w")], dma=True)
                for ih in range(2):
                    P.emit("scalar", lambda s, n=n, th=th, ih=ih: s.dma_start(
                        out=bass.AP(interq_q, (2 * n + ih) * T + th * 1024,
                                    [[16 * T, 128], [128, 8], [1, 128]]),
                        in_=bass.AP(tstag, ih * 128,
                                    [[2048, 128], [256, 8], [1, 128]])),
                        reads=[(tstag, "w")],
                        writes=[(interq_q, (2 * n + ih, th))], dma=True)
        src_d = None if p4 == 0 else (h["x1a_d"] if p4 == 1 else
                                      (h["x1b_d"] if p4 == 2 else h["x1a_d"]))
        dst_d = h["x1a_d"] if p4 == 0 else (h["x1b_d"] if p4 == 1 else
                                            (h["x1a_d"] if p4 == 2 else h["x1_d"]))
        for hc in range(8):
            P.emit("sync", lambda s, p4=p4, hc=hc: s.dma_start(
                out=wdall[:], in_=bass.AP(
                    h["wdq_fm_d"], p4 * 16 * 128 * D + hc * 256,
                    [[D, 128], [128 * D, 16], [1, 256]])),
                reads=[(h["wdq_fm_d"], (hb, cc)) for hb in range(16)
                       for cc in range(4)],
                writes=[wdall], dma=True)
            for t in range(NT):
                bank, half = pp[t // 2], (t % 2) * 256
                b2 = t % 2
                for k in range(16):
                    P.emit("tensor", lambda te, t=t, k=k, n2=None, bank=bank,
                           half=half: te.matmul(
                        bank[:, half:half + 256],
                        interq_q[:, k * T + t * 128:k * T + (t + 1) * 128],
                        wdall[:, k, :], start=(k == 0), stop=(k == 15)),
                        reads=[(interq_q, (k, t // 8)), wdall], writes=[bank])
                if p4 == 0:
                    P.emit("scalar", lambda s, bank=bank, half=half,
                           o=ob[b2]: s.copy(
                        out=o[:], in_=bank[:, half:half + 256]),
                        reads=[bank], writes=[ob[b2]])
                else:
                    P.emit("sync", lambda s, t=t, hc=hc, xb=x1b[b2], sd=src_d:
                           s.dma_start(
                        out=xb[:], in_=sd[ts_(t, 128), hc * 256:(hc + 1) * 256]),
                        reads=[(src_d, (t, hc))], writes=[x1b[b2]], dma=True)
                    P.emit("vector", lambda v, bank=bank, half=half,
                           xb=x1b[b2], o=ob[b2]: v.tensor_tensor(
                        out=o[:], in0=bank[:, half:half + 256], in1=xb[:],
                        op=ALU.add), reads=[bank, x1b[b2]], writes=[ob[b2]])
                P.emit("gpsimd", lambda g, t=t, hc=hc, o=ob[b2], dd=dst_d:
                       g.dma_start(
                    out=dd[ts_(t, 128), hc * 256:(hc + 1) * 256], in_=o[:]),
                    reads=[ob[b2]], writes=[(dst_d, (t, hc))], dma=True)
    res["_scratch5"] = (mx_scratch_list(scrs[0])
        + [wgq, wuq, wq2, gate_sb, sg, inter_sb, tstag, wdall,
           x1b[0], ob[0]])
    return res



CONV1 = 8192
VDIM1 = 4096


def emit_l1_proj(c, xn_fm, h, res):
    P = c.P
    pp = c.pp
    wbuf = [c.sbuf(f"l1w{i}", [128, 16, 128], BF16) for i in range(2)]
    xcbuf = c.sbuf("l1xc", [128, 2051], F32)
    cacc = c.sbuf("l1cacc", [128, T], F32)
    ctile = [c.sbuf(f"l1ct{i}", [128, T], BF16) for i in range(2)]
    cw_sb = c.sbuf("l1cw", [128, 64, 4], F32)
    tstag = [c.sbuf(f"l1ts{i}", [128, 16, 128], BF16) for i in range(2)]
    sqt = c.sbuf("l1sq", [128, T], F32)
    nrow = c.sbuf("l1nr", [1, T], F32)
    nrep = c.sbuf("l1nrep", [128, T], F32)
    ones1 = c.sbuf("l1one", [1, 128], F32)
    onescol = c.sbuf("l1onec", [128, 1], F32)
    ntile = c.sbuf("l1nt", [128, T], BF16)
    ba_sb = c.sbuf("l1ba", [64, T], F32)
    dtb_sb = c.sbuf("l1dtb", [32, 1], F32)
    alog_sb = c.sbuf("l1alog", [32, 1], F32)
    cg = c.sbuf("l1cg", [32, T], F32, side="right")
    cgb = c.sbuf("l1cgb", [32, T], F32, side="right")
    beta_f = c.sbuf("l1beta", [32, T], F32, side="right")
    gtmp = c.sbuf("l1gtmp", [32, T], F32)
    gtmp2 = c.sbuf("l1gt2", [32, T], F32)
    res.update(cg=cg, cgb=cgb, beta_f=beta_f)

    P.emit("sync", lambda s: s.dma_start(
        out=cw_sb[:], in_=bass.AP(h["cw1"], 0, [[4, 128], [128 * 4, 64], [1, 4]])),
        reads=[h["cw1"]], writes=[cw_sb], dma=True)
    P.emit("sync", lambda s: s.dma_start(out=dtb_sb[:], in_=h["dtb1"][:, :]),
           reads=[h["dtb1"]], writes=[dtb_sb], dma=True)
    P.emit("sync", lambda s: s.dma_start(out=alog_sb[:], in_=h["alog1"][:, :]),
           reads=[h["alog1"]], writes=[alog_sb], dma=True)
    P.emit("gpsimd", lambda g: g.memset(xcbuf[:, 0:3], 0.0), writes=[xcbuf])
    P.emit("gpsimd", lambda g: g.memset(ones1[:], 1.0), writes=[ones1])
    P.emit("gpsimd", lambda g: g.memset(onescol[:], 1.0), writes=[onescol])

    for nt in range(64):
        b = nt % 2
        wb = wbuf[b]
        P.emit("gpsimd", lambda g, nt=nt, wb=wb: g.dma_start(
            out=wb[:], in_=bass.AP(h["w_qkvT"], nt * 128,
                                   [[CONV1, 128], [128 * CONV1, 16], [1, 128]])),
            reads=[h["w_qkvT"]], writes=[wb], dma=True)
        for tc in range(4):
            pb = pp[tc % 2]
            for k in range(16):
                P.emit("tensor", lambda te, wb=wb, k=k, tc=tc, pb=pb:
                       te.matmul(pb[:, :], wb[:, k, :],
                                 xn_fm[:, k * T + tc * 512:k * T + (tc + 1) * 512],
                                 start=(k == 0), stop=(k == 15)),
                       reads=[wb, (xn_fm, "all")], writes=[pb])
            P.emit("scalar", lambda s, pb=pb, tc=tc: s.copy(
                out=xcbuf[:, 3 + tc * 512:3 + (tc + 1) * 512], in_=pb[:]),
                reads=[pb], writes=[xcbuf])
        ct = ctile[b]
        P.emit("vector", lambda v, nt=nt: v.tensor_scalar(
            out=cacc[:], in0=xcbuf[:, 0:T], scalar1=cw_sb[:, nt, 0:1],
            scalar2=None, op0=ALU.mult), reads=[xcbuf, cw_sb], writes=[cacc])
        for k in range(1, 4):
            P.emit("vector", lambda v, nt=nt, k=k: v.scalar_tensor_tensor(
                out=cacc[:], in0=xcbuf[:, k:k + T],
                scalar=cw_sb[:, nt, k:k + 1], in1=cacc[:],
                op0=ALU.mult, op1=ALU.add),
                reads=[xcbuf, cw_sb, cacc], writes=[cacc])
        P.emit("scalar", lambda s, ct=ct: s.activation(
            out=ct[:], in_=cacc[:], func=AF.Silu), reads=[cacc], writes=[ct])
        if nt < 32:
            P.emit("scalar", lambda s, ct=ct: s.activation(
                out=sqt[:], in_=ct[:], func=AF.Square), reads=[ct], writes=[sqt])
            for tc in range(4):
                P.emit("tensor", lambda te, tc=tc: te.matmul(
                    pp[2][0:1, 0:512], onescol[:],
                    sqt[:, tc * 512:(tc + 1) * 512], start=True, stop=True),
                    reads=[sqt, onescol], writes=[pp[2]])
                P.emit("scalar", lambda s, tc=tc: s.copy(
                    out=nrow[0:1, tc * 512:(tc + 1) * 512],
                    in_=pp[2][0:1, 0:512]), reads=[pp[2]], writes=[nrow])
            P.emit("vector", lambda v: v.tensor_scalar(
                out=nrow[:], in0=nrow[:], scalar1=1e-24, scalar2=None,
                op0=ALU.max), reads=[nrow], writes=[nrow])
            P.emit("scalar", lambda s: s.activation(
                out=nrow[:], in_=nrow[:], func=AF.Ln), reads=[nrow], writes=[nrow])
            P.emit("scalar", lambda s: s.activation(
                out=nrow[:], in_=nrow[:], func=AF.Exp, scale=-0.5),
                reads=[nrow], writes=[nrow])
            for tc in range(4):
                P.emit("tensor", lambda te, tc=tc: te.matmul(
                    pp[3][:, 0:512], ones1[:],
                    nrow[0:1, tc * 512:(tc + 1) * 512], start=True, stop=True),
                    reads=[nrow, ones1], writes=[pp[3]])
                P.emit("scalar", lambda s, tc=tc: s.copy(
                    out=nrep[:, tc * 512:(tc + 1) * 512], in_=pp[3][:, 0:512]),
                    reads=[pp[3]], writes=[nrep])
            P.emit("vector", lambda v, ct=ct: v.tensor_tensor(
                out=ntile[:], in0=ct[:], in1=nrep[:], op=ALU.mult),
                reads=[ct, nrep], writes=[ntile])
            dst_fm = h["qn_fm_d"] if nt < 16 else h["kn_fm_d"]
            jj = nt if nt < 16 else nt - 16
            P.emit("sync", lambda s, jj=jj, dst=dst_fm: s.dma_start(
                out=dst[ts_(jj, 128), :], in_=ntile[:]),
                reads=[ntile], writes=[(dst_fm, jj)], dma=True)
            if nt >= 16:
                tst = tstag[b]
                P.emit("scalar", lambda s, tst=tst: s.dma_start(
                    out=tst[:], in_=ntile[:], transpose=True),
                    reads=[ntile], writes=[tst], dma=True)
                P.emit("sync", lambda s, tst=tst, jj=jj: s.dma_start(
                    out=bass.AP(h["kn_tm_d"], jj * 128,
                                [[2048, 128], [128 * 2048, 16], [1, 128]]),
                    in_=tst[:]), reads=[tst], writes=[(h["kn_tm_d"], jj)],
                    dma=True)
        else:
            jj = nt - 32
            tst = tstag[b]
            P.emit("scalar", lambda s, ct=ct, tst=tst: s.dma_start(
                out=tst[:], in_=ct[:], transpose=True),
                reads=[ct], writes=[tst], dma=True)
            P.emit("sync", lambda s, tst=tst, jj=jj: s.dma_start(
                out=bass.AP(h["v_tm_d"], jj * 128,
                            [[VDIM1, 128], [128 * VDIM1, 16], [1, 128]]),
                in_=tst[:]), reads=[tst], writes=[(h["v_tm_d"], jj)], dma=True)

    for nt in range(32):
        b = nt % 2
        wb = wbuf[b]
        P.emit("gpsimd", lambda g, nt=nt, wb=wb: g.dma_start(
            out=wb[:], in_=bass.AP(h["w_zT"], nt * 128,
                                   [[VDIM1, 128], [128 * VDIM1, 16], [1, 128]])),
            reads=[h["w_zT"]], writes=[wb], dma=True)
        ct = ctile[b]
        for tc in range(4):
            pb = pp[tc % 2]
            for k in range(16):
                P.emit("tensor", lambda te, wb=wb, k=k, tc=tc, pb=pb:
                       te.matmul(pb[:, :], wb[:, k, :],
                                 xn_fm[:, k * T + tc * 512:k * T + (tc + 1) * 512],
                                 start=(k == 0), stop=(k == 15)),
                       reads=[wb, (xn_fm, "all")], writes=[pb])
            P.emit("scalar", lambda s, pb=pb, tc=tc, ct=ct: s.copy(
                out=ct[:, tc * 512:(tc + 1) * 512], in_=pb[:]),
                reads=[pb], writes=[ct])
        tst = tstag[b]
        P.emit("scalar", lambda s, ct=ct, tst=tst: s.dma_start(
            out=tst[:], in_=ct[:], transpose=True),
            reads=[ct], writes=[tst], dma=True)
        P.emit("sync", lambda s, tst=tst, nt=nt: s.dma_start(
            out=bass.AP(h["z1_tm_d"], nt * 128,
                        [[VDIM1, 128], [128 * VDIM1, 16], [1, 128]]),
            in_=tst[:]), reads=[tst], writes=[(h["z1_tm_d"], nt)], dma=True)

    P.emit("gpsimd", lambda g: g.dma_start(
        out=wbuf[0][:, :, 0:64],
        in_=bass.AP(h["w_baT"], 0, [[64, 128], [128 * 64, 16], [1, 64]])),
        reads=[h["w_baT"]], writes=[wbuf[0]], dma=True)
    for tc in range(4):
        pb = pp[tc % 2]
        for k in range(16):
            P.emit("tensor", lambda te, k=k, tc=tc, pb=pb: te.matmul(
                pb[0:64, :], wbuf[0][:, k, 0:64],
                xn_fm[:, k * T + tc * 512:k * T + (tc + 1) * 512],
                start=(k == 0), stop=(k == 15)),
                reads=[wbuf[0], (xn_fm, "all")], writes=[pb])
        P.emit("scalar", lambda s, pb=pb, tc=tc: s.copy(
            out=ba_sb[:, tc * 512:(tc + 1) * 512], in_=pb[0:64, :]),
            reads=[pb], writes=[ba_sb])
    P.emit("scalar", lambda s: s.activation(
        out=beta_f[:], in_=ba_sb[0:32, :], func=AF.Sigmoid),
        reads=[ba_sb], writes=[beta_f])
    P.emit("scalar", lambda s: s.activation(
        out=gtmp2[:], in_=beta_f[:], func=AF.Ln),
        reads=[beta_f], writes=[gtmp2])
    P.emit("scalar", lambda s: s.activation(
        out=alog_sb[:], in_=alog_sb[:], func=AF.Exp),
        reads=[alog_sb], writes=[alog_sb])
    P.emit("vector", lambda v: v.tensor_scalar(
        out=alog_sb[:], in0=alog_sb[:], scalar1=-1.0, scalar2=None,
        op0=ALU.mult), reads=[alog_sb], writes=[alog_sb])
    P.emit("vector", lambda v: v.tensor_scalar(
        out=gtmp[:], in0=ba_sb[32:64, :], scalar1=dtb_sb[:, 0:1], scalar2=None,
        op0=ALU.add), reads=[ba_sb, dtb_sb], writes=[gtmp])
    P.emit("vector", lambda v: v.tensor_scalar(
        out=cg[:], in0=gtmp[:], scalar1=60.0, scalar2=None,
        op0=ALU.min), reads=[gtmp], writes=[cg])
    P.emit("scalar", lambda s: s.activation(
        out=cg[:], in_=cg[:], func=AF.Exp), reads=[cg], writes=[cg])
    P.emit("vector", lambda v: v.tensor_scalar(
        out=cg[:], in0=cg[:], scalar1=1.0, scalar2=None,
        op0=ALU.add), reads=[cg], writes=[cg])
    P.emit("scalar", lambda s: s.activation(
        out=cg[:], in_=cg[:], func=AF.Ln), reads=[cg], writes=[cg])
    P.emit("vector", lambda v: v.tensor_tensor(
        out=gtmp[:], in0=cg[:], in1=gtmp[:], op=ALU.max),
        reads=[cg, gtmp], writes=[gtmp])
    P.emit("vector", lambda v: v.tensor_scalar(
        out=gtmp[:], in0=gtmp[:], scalar1=alog_sb[:, 0:1], scalar2=None,
        op0=ALU.mult), reads=[gtmp, alog_sb], writes=[gtmp])
    for ci in range(16):
        P.emit("vector", lambda v, ci=ci: v.tensor_tensor_scan(
            out=cg[:, ts_(ci, 128)], data0=gtmp[:, ts_(ci, 128)],
            data1=gtmp[:, ts_(ci, 128)], initial=0.0, op0=ALU.add,
            op1=ALU.bypass), reads=[gtmp], writes=[(cg, ci)])
    P.emit("vector", lambda v: v.tensor_tensor(
        out=cgb[:], in0=bass.AP(cg, 0, [[T, 32], [1, T]]), in1=gtmp2[:],
        op=ALU.add), reads=[(cg, ci) for ci in range(16)] + [gtmp2],
        writes=[cgb])
    res["_scratch6"] = [wbuf[0], wbuf[1], xcbuf, cacc, ctile[0], ctile[1],
                       cw_sb, tstag[0], tstag[1], sqt, nrow, nrep, ones1,
                       onescol, ntile, ba_sb, dtb_sb, alog_sb, gtmp, gtmp2]
    return res



def emit_l1_delta(c, h, res):
    P = c.P
    pp = c.pp
    cg, cgb, beta_f = res["cg"], res["cgb"], res["beta_f"]
    NH, HD = 32, 128
    NCH = 16

    qn_c = [c.sbuf("l1_qn0", [128, 2048], BF16)] * 2
    kn_c = [c.sbuf("l1_kn0", [128, 2048], BF16)] * 2
    ktm_c = [c.sbuf("l1_ktm0", [128, 2048], BF16)] * 2
    v_c = [c.sbuf("l1_v0", [128, 4096], BF16)] * 2
    z_c = [c.sbuf("l1_z0", [128, 4096], BF16)] * 2
    S_sb = c.sbuf("l1_S", [128, 4096], F32)
    S_bf = c.sbuf("l1_Sb", [128, 4096], BF16)
    trb = c.sbuf("l1_trb", [128, 64], F32)
    stack = c.sbuf("l1_stk", [128, 128], F32)
    gamT = c.sbuf("l1_gT", [128, 32], F32)
    bgT = c.sbuf("l1_bgT", [128, 32], F32)
    dsT = c.sbuf("l1_dsT", [128, 32], F32)
    cdT = c.sbuf("l1_cdT", [128, 32], F32)
    larow = c.sbuf("l1_lar", [1, 32], F32)
    rpl = c.sbuf("l1_rpl", [128, 32], F32)
    ones1 = c.sbuf("l1_one", [1, 128], F32)
    ident = c.sbuf("l1_id", [128, 128], F32)
    identb = c.sbuf("l1_idb", [128, 128], BF16)
    mbS = c.sbuf("l1_mbS", [128, 128], F32)
    mbT = c.sbuf("l1_mbT", [128, 128], F32)
    mbQ = c.sbuf("l1_mbQ", [128, 128], F32)
    negcg = c.sbuf("l1_ncg", [32, 128], F32)
    stgA = c.sbuf("l1_sgA", [2, 512], F32)
    stgB = c.sbuf("l1_sgB", [2, 512], F32)
    stgC = c.sbuf("l1_sgC", [2, 512], F32)
    stgD = c.sbuf("l1_sgD", [2, 512], F32)
    stgE = c.sbuf("l1_sgE", [2, 512], F32)
    gneg = c.sbuf("l1_G", [128, 2048], BF16)
    gqk = c.sbuf("l1_Gqk", [128, 2048], BF16)
    etile = c.sbuf("l1_et", [128, 512], F32)
    nst = [c.sbuf(f"l1_nst{i}", [128, 512], BF16) for i in range(2)]
    nts = [c.sbuf(f"l1_nts{i}", [128, 512], BF16) for i in range(2)]
    aqk = c.sbuf("l1_aqk", [128, 512], BF16)
    rX = c.sbuf("l1_X", [128, 512], BF16)
    uU = c.sbuf("l1_U", [128, 512], BF16)
    o_sb = c.sbuf("l1_o", [128, 4096], BF16)
    usq = c.sbuf("l1_usq", [128, 4096], BF16)
    rtmp = c.sbuf("l1_rtmp", [128, 512], F32)
    orms = c.sbuf("l1_orms", [128, 32], F32)
    nw1rep = c.sbuf("l1_nw1", [128, 128], F32)
    nw1row = c.sbuf("l1_nw1r", [1, 128], F32)
    ostag = c.sbuf("l1_ost", [128, 2048], BF16)

    P.emit("gpsimd", lambda g: g.memset(ones1[:], 1.0), writes=[ones1])
    P.emit("gpsimd", lambda g: g.memset(ident[:], 1.0), writes=[ident])
    P.emit("gpsimd", lambda g: g.affine_select(
        out=ident[:], in_=ident[:], pattern=[[1, 128]],
        compare_op=ALU.is_equal, fill=0.0, base=0, channel_multiplier=-1),
        reads=[ident], writes=[ident])
    P.emit("vector", lambda v: v.tensor_copy(out=identb[:], in_=ident[:]),
           reads=[ident], writes=[identb])
    for mb, base, cm, cop in ((mbS, -1, -1, ALU.is_ge), (mbT, -1, 1, ALU.is_ge),
                              (mbQ, 0, -1, ALU.is_ge)):
        pat = [[1, 128]] if mb is not mbT else [[-1, 128]]
        P.emit("gpsimd", lambda g, mb=mb: g.memset(mb[:], 0.0), writes=[mb])
        P.emit("gpsimd", lambda g, mb=mb, pat=pat, base=base, cm=cm, cop=cop:
               g.affine_select(out=mb[:], in_=mb[:], pattern=pat,
                               compare_op=cop, fill=-1e30, base=base,
                               channel_multiplier=cm),
               reads=[mb], writes=[mb])
    P.emit("gpsimd", lambda g: g.memset(S_sb[:], 0.0), writes=[S_sb])
    P.emit("gpsimd", lambda g: g.memset(S_bf[:], 0.0), writes=[S_bf])
    for stg, fill in ((stgA, 1.0), (stgB, 1.0), (stgC, 1.0), (stgD, 1.0),
                      (stgE, 1.0)):
        P.emit("gpsimd", lambda g, stg=stg, fill=fill: g.memset(stg[:], fill),
               writes=[(stg, "ones"), (stg, "cg")])
    P.emit("sync", lambda s: s.dma_start(out=nw1row[:], in_=h["nw1"][:, :]),
           reads=[h["nw1"]], writes=[nw1row], dma=True)
    P.emit("tensor", lambda te: te.matmul(
        pp[0][:, 0:128], ones1[:], nw1row[:], start=True, stop=True),
        reads=[ones1, nw1row], writes=[pp[0]])
    P.emit("scalar", lambda s: s.copy(out=nw1rep[:], in_=pp[0][:, 0:128]),
           reads=[pp[0]], writes=[nw1rep])

    for ci in range(NCH):
        bb = ci % 2
        qn, kn, ktm, vv, zz = qn_c[bb], kn_c[bb], ktm_c[bb], v_c[bb], z_c[bb]
        P.emit("sync", lambda s, ci=ci, d=qn: s.dma_start(
            out=bass.AP(d, 0, [[2048, 128], [128, 16], [1, 128]]),
            in_=bass.AP(h["qn_fm_d"], ci * 128, [[T, 128], [128 * T, 16], [1, 128]])),
            reads=[(h["qn_fm_d"], j) for j in range(16)], writes=[qn], dma=True)
        P.emit("sync", lambda s, ci=ci, d=kn: s.dma_start(
            out=bass.AP(d, 0, [[2048, 128], [128, 16], [1, 128]]),
            in_=bass.AP(h["kn_fm_d"], ci * 128, [[T, 128], [128 * T, 16], [1, 128]])),
            reads=[(h["kn_fm_d"], j) for j in range(16)], writes=[kn], dma=True)
        P.emit("sync", lambda s, ci=ci, d=ktm: s.dma_start(
            out=d[:], in_=h["kn_tm_d"][ts_(ci, 128), :]),
            reads=[(h["kn_tm_d"], j) for j in range(16)], writes=[ktm], dma=True)
        P.emit("sync", lambda s, ci=ci, d=vv: s.dma_start(
            out=d[:], in_=h["v_tm_d"][ts_(ci, 128), :]),
            reads=[(h["v_tm_d"], j) for j in range(32)], writes=[vv], dma=True)
        P.emit("sync", lambda s, ci=ci, d=zz: s.dma_start(
            out=d[:], in_=h["z1_tm_d"][ts_(ci, 128), :]),
            reads=[(h["z1_tm_d"], j) for j in range(32)], writes=[zz], dma=True)
        P.emit("vector", lambda v, ci=ci: v.tensor_copy(
            out=stack[0:32, :], in_=cg[:, ts_(ci, 128)]),
            reads=[(cg, ci)], writes=[(stack, 0)])
        P.emit("vector", lambda v, ci=ci: v.tensor_copy(
            out=stack[32:64, :], in_=beta_f[:, ts_(ci, 128)]),
            reads=[beta_f], writes=[(stack, 1)])
        P.emit("tensor", lambda te: te.transpose(pp[1][:, 0:128], stack[:],
                                                 ident[:]),
               reads=[(stack, 0), (stack, 1), ident], writes=[pp[1]])
        P.emit("scalar", lambda s: s.copy(out=trb[:], in_=pp[1][:, 0:64]),
               reads=[pp[1]], writes=[trb])
        P.emit("scalar", lambda s: s.activation(
            out=gamT[:], in_=trb[:, 0:32], func=AF.Exp),
            reads=[trb], writes=[gamT])
        P.emit("vector", lambda v: v.tensor_tensor(
            out=bgT[:], in0=trb[:, 32:64], in1=gamT[:], op=ALU.mult),
            reads=[trb, gamT], writes=[bgT])
        P.emit("scalar", lambda s, ci=ci: s.dma_start(
            out=larow[:], in_=cg[:, ci * 128 + 127: ci * 128 + 128],
            allow_slow_non_contiguous=True),
            reads=[(cg, ci)], writes=[larow], dma=True)
        P.emit("tensor", lambda te: te.matmul(
            pp[1][:, 128:160], ones1[:], larow[:], start=True, stop=True),
            reads=[ones1, larow], writes=[pp[1]])
        P.emit("scalar", lambda s: s.copy(out=rpl[:], in_=pp[1][:, 128:160]),
               reads=[pp[1]], writes=[rpl])
        P.emit("vector", lambda v: v.tensor_tensor(
            out=dsT[:], in0=rpl[:], in1=trb[:, 0:32], op=ALU.subtract),
            reads=[rpl, trb], writes=[dsT])
        P.emit("scalar", lambda s: s.activation(
            out=dsT[:], in_=dsT[:], func=AF.Exp), reads=[dsT], writes=[dsT])
        P.emit("scalar", lambda s: s.activation(
            out=cdT[:], in_=rpl[:], func=AF.Exp), reads=[rpl], writes=[cdT])
        P.emit("vector", lambda v, ci=ci: v.tensor_scalar(
            out=negcg[:], in0=cg[:, ts_(ci, 128)], scalar1=-1.0, scalar2=None,
            op0=ALU.mult), reads=[(cg, ci)], writes=[negcg])
        for kh in range(16):
            gp_ = pp[6 + kh % 2]
            P.emit("tensor", lambda te, kh=kh, gp_=gp_, kn=kn: te.matmul(
                gp_[:, 0:128], kn[:, ts_(kh, 128)], kn[:, ts_(kh, 128)],
                start=True, stop=True), reads=[kn], writes=[gp_])
            P.emit("vector", lambda v, kh=kh, gp_=gp_: v.tensor_scalar(
                out=gneg[:, ts_(kh, 128)], in0=gp_[:, 0:128], scalar1=-1.0,
                scalar2=None, op0=ALU.mult), reads=[gp_], writes=[(gneg, kh)])
            P.emit("tensor", lambda te, kh=kh, gp_=gp_, kn=kn, qn=qn: te.matmul(
                gp_[:, 128:256], kn[:, ts_(kh, 128)], qn[:, ts_(kh, 128)],
                start=True, stop=True), reads=[kn, qn], writes=[gp_])
            P.emit("scalar", lambda s, kh=kh, gp_=gp_: s.copy(
                out=gqk[:, ts_(kh, 128)], in_=gp_[:, 128:256]),
                reads=[gp_], writes=[(gqk, kh)])
        for sp in range(8):
            h0 = sp * 4
            P.emit("scalar", lambda s, sp=sp: s.dma_start(
                out=stgA[0:1, :], in_=negcg[ts_(sp, 4), :],
                allow_slow_non_contiguous=True),
                reads=[negcg], writes=[(stgA, "cg")], dma=True)
            P.emit("scalar", lambda s, sp=sp, ci=ci: s.dma_start(
                out=stgB[1:2, :], in_=cgb[ts_(sp, 4), ts_(ci, 128)],
                allow_slow_non_contiguous=True),
                reads=[cgb], writes=[(stgB, "cg")], dma=True)
            P.emit("scalar", lambda s, sp=sp, ci=ci: s.dma_start(
                out=stgC[0:1, :], in_=cgb[ts_(sp, 4), ts_(ci, 128)],
                allow_slow_non_contiguous=True),
                reads=[cgb], writes=[(stgC, "cg")], dma=True)
            P.emit("scalar", lambda s, sp=sp: s.dma_start(
                out=stgD[1:2, :], in_=negcg[ts_(sp, 4), :],
                allow_slow_non_contiguous=True),
                reads=[negcg], writes=[(stgD, "cg")], dma=True)
            P.emit("scalar", lambda s, sp=sp, ci=ci: s.dma_start(
                out=stgE[1:2, :], in_=cg[ts_(sp, 4), ts_(ci, 128)],
                allow_slow_non_contiguous=True),
                reads=[(cg, ci)], writes=[(stgE, "cg")], dma=True)
            for g4 in range(1):
                hh0 = h0
                s4 = slice(0, 512)
                gb = bass.AP(gneg, (hh0 // 2) * 128,
                             [[2048, 128], [128, 2], [0, 2], [1, 128]])
                for lay, stg_l, stg_r, mb, dstt in (
                        (0, stgA, stgB, mbS, nst[0]),
                        (1, stgC, stgD, mbT, nts[0])):
                    pb = pp[0 + lay]
                    for h4 in range(4):
                        P.emit("tensor", lambda te, h4=h4, stg_l=stg_l,
                               stg_r=stg_r, pb=pb, s4=s4: te.matmul(
                            pb[:, h4 * 128:(h4 + 1) * 128],
                            stg_l[:, s4.start + h4 * 128:s4.start + (h4 + 1) * 128],
                            stg_r[:, s4.start + h4 * 128:s4.start + (h4 + 1) * 128],
                            start=True, stop=True),
                            reads=[(stg_l, "cg"), (stg_l, "ones"),
                                   (stg_r, "cg"), (stg_r, "ones")],
                            writes=[pb])
                    P.emit("vector", lambda v, pb=pb, mb=mb: v.tensor_tensor(
                        out=etile[:], in0=pb[:],
                        in1=bass.AP(mb, 0, [[128, 128], [0, 4], [1, 128]]),
                        op=ALU.add), reads=[pb, mb], writes=[etile])
                    P.emit("scalar", lambda s: s.activation(
                        out=etile[:], in_=etile[:], func=AF.Exp),
                        reads=[etile], writes=[etile])
                    P.emit("vector", lambda v, dstt=dstt, gb=gb: v.tensor_tensor(
                        out=dstt[:], in0=etile[:], in1=gb, op=ALU.mult),
                        reads=[etile, gneg], writes=[dstt])
                pb = pp[2]
                for h4 in range(4):
                    hh = hh0 + h4
                    P.emit("tensor", lambda te, hh=hh, h4=h4, pb=pb, kn=kn:
                           te.matmul(
                        pb[:, h4 * 128:(h4 + 1) * 128],
                        kn[:, ts_(hh // 2, 128)], S_bf[:, ts_(hh, 128)],
                        start=True, stop=True),
                        reads=[kn, S_bf], writes=[pb])
                P.emit("vector", lambda v, pb=pb, hh0=hh0: v.tensor_tensor(
                    out=etile[:], in0=pb[:],
                    in1=bass.AP(bgT, hh0, [[32, 128], [1, 4], [0, 128]]),
                    op=ALU.mult), reads=[pb, bgT], writes=[etile])
                P.emit("vector", lambda v, hh0=hh0, vv=vv: v.tensor_tensor(
                    out=rtmp[:], in0=vv[:, hh0 * 128:(hh0 + 4) * 128],
                    in1=bass.AP(trb, 32 + hh0, [[64, 128], [1, 4], [0, 128]]),
                    op=ALU.mult), reads=[vv, trb], writes=[rtmp])
                P.emit("vector", lambda v: v.tensor_tensor(
                    out=rX[:], in0=rtmp[:], in1=etile[:],
                    op=ALU.subtract), reads=[rtmp, etile], writes=[rX])
                cur = 0
                for lev in range(7):
                    pb = pp[3]
                    for h4 in range(4):
                        P.emit("tensor", lambda te, h4=h4, pb=pb, cur=cur:
                               te.matmul(
                            pb[:, h4 * 128:(h4 + 1) * 128],
                            nst[cur][:, h4 * 128:(h4 + 1) * 128],
                            rX[:, h4 * 128:(h4 + 1) * 128],
                            start=True, stop=False),
                            reads=[nst[cur], rX], writes=[pb])
                        P.emit("tensor", lambda te, h4=h4, pb=pb: te.matmul(
                            pb[:, h4 * 128:(h4 + 1) * 128], identb[:],
                            rX[:, h4 * 128:(h4 + 1) * 128],
                            start=False, stop=True),
                            reads=[identb, rX], writes=[pb])
                    P.emit("scalar", lambda s, pb=pb: s.copy(
                        out=rX[:], in_=pb[:]), reads=[pb], writes=[rX])
                    if lev < 6:
                        pb2 = pp[4]
                        pb3 = pp[5]
                        for h4 in range(4):
                            P.emit("tensor", lambda te, h4=h4, pb2=pb2, cur=cur:
                                   te.matmul(
                                pb2[:, h4 * 128:(h4 + 1) * 128],
                                nts[cur][:, h4 * 128:(h4 + 1) * 128],
                                nst[cur][:, h4 * 128:(h4 + 1) * 128],
                                start=True, stop=True),
                                reads=[nts[cur], nst[cur]], writes=[pb2])
                            P.emit("tensor", lambda te, h4=h4, pb3=pb3, cur=cur:
                                   te.matmul(
                                pb3[:, h4 * 128:(h4 + 1) * 128],
                                nst[cur][:, h4 * 128:(h4 + 1) * 128],
                                nts[cur][:, h4 * 128:(h4 + 1) * 128],
                                start=True, stop=True),
                                reads=[nts[cur], nst[cur]], writes=[pb3])
                        P.emit("scalar", lambda s, pb2=pb2, cur=cur: s.copy(
                            out=nst[1 - cur][:], in_=pb2[:]),
                            reads=[pb2], writes=[nst[1 - cur]])
                        P.emit("scalar", lambda s, pb3=pb3, cur=cur: s.copy(
                            out=nts[1 - cur][:], in_=pb3[:]),
                            reads=[pb3], writes=[nts[1 - cur]])
                        cur = 1 - cur
                pb = pp[0]
                for h4 in range(4):
                    P.emit("tensor", lambda te, h4=h4, pb=pb, s4=s4: te.matmul(
                        pb[:, h4 * 128:(h4 + 1) * 128],
                        stgA[:, s4.start + h4 * 128:s4.start + (h4 + 1) * 128],
                        stgE[:, s4.start + h4 * 128:s4.start + (h4 + 1) * 128],
                        start=True, stop=True),
                        reads=[(stgA, "cg"), (stgA, "ones"), (stgE, "cg"),
                               (stgE, "ones")], writes=[pb])
                P.emit("vector", lambda v, pb=pb: v.tensor_tensor(
                    out=etile[:], in0=pb[:],
                    in1=bass.AP(mbQ, 0, [[128, 128], [0, 4], [1, 128]]),
                    op=ALU.add), reads=[pb, mbQ], writes=[etile])
                P.emit("scalar", lambda s: s.activation(
                    out=etile[:], in_=etile[:], func=AF.Exp),
                    reads=[etile], writes=[etile])
                gbq = bass.AP(gqk, (hh0 // 2) * 128,
                              [[2048, 128], [128, 2], [0, 2], [1, 128]])
                P.emit("vector", lambda v, gbq=gbq: v.tensor_tensor(
                    out=aqk[:], in0=etile[:], in1=gbq, op=ALU.mult),
                    reads=[etile, gqk], writes=[aqk])
                pb = pp[2]
                for h4 in range(4):
                    hh = hh0 + h4
                    P.emit("tensor", lambda te, hh=hh, h4=h4, pb=pb, qn=qn:
                           te.matmul(
                        pb[:, h4 * 128:(h4 + 1) * 128],
                        qn[:, ts_(hh // 2, 128)], S_bf[:, ts_(hh, 128)],
                        start=True, stop=True),
                        reads=[qn, S_bf], writes=[pb])
                pb2 = pp[3]
                for h4 in range(4):
                    P.emit("tensor", lambda te, h4=h4, pb2=pb2: te.matmul(
                        pb2[:, h4 * 128:(h4 + 1) * 128],
                        aqk[:, h4 * 128:(h4 + 1) * 128],
                        rX[:, h4 * 128:(h4 + 1) * 128],
                        start=True, stop=True),
                        reads=[aqk, rX], writes=[pb2])
                P.emit("vector", lambda v, pb=pb, hh0=hh0: v.tensor_tensor(
                    out=etile[:], in0=pb[:],
                    in1=bass.AP(gamT, hh0, [[32, 128], [1, 4], [0, 128]]),
                    op=ALU.mult), reads=[pb, gamT], writes=[etile])
                P.emit("vector", lambda v, pb2=pb2, hh0=hh0: v.tensor_tensor(
                    out=o_sb[:, hh0 * 128:(hh0 + 4) * 128], in0=etile[:],
                    in1=pb2[:], op=ALU.add),
                    reads=[etile, pb2], writes=[(o_sb, hh0)])
                P.emit("vector", lambda v, hh0=hh0: v.tensor_tensor(
                    out=uU[:], in0=rX[:],
                    in1=bass.AP(dsT, hh0, [[32, 128], [1, 4], [0, 128]]),
                    op=ALU.mult), reads=[rX, dsT], writes=[uU])
                pb3 = pp[5]
                for h4 in range(4):
                    hh = hh0 + h4
                    P.emit("tensor", lambda te, hh=hh, h4=h4, pb3=pb3,
                           ktm=ktm: te.matmul(
                        pb3[:, h4 * 128:(h4 + 1) * 128],
                        ktm[:, ts_(hh // 2, 128)],
                        uU[:, h4 * 128:(h4 + 1) * 128],
                        start=True, stop=True),
                        reads=[ktm, uU], writes=[pb3])
                P.emit("vector", lambda v, hh0=hh0: v.tensor_tensor(
                    out=S_sb[:, hh0 * 128:(hh0 + 4) * 128],
                    in0=S_sb[:, hh0 * 128:(hh0 + 4) * 128],
                    in1=bass.AP(cdT, hh0, [[32, 128], [1, 4], [0, 128]]),
                    op=ALU.mult), reads=[S_sb, cdT], writes=[S_sb])
                P.emit("vector", lambda v, hh0=hh0, pb3=pb3: v.tensor_tensor(
                    out=S_sb[:, hh0 * 128:(hh0 + 4) * 128],
                    in0=S_sb[:, hh0 * 128:(hh0 + 4) * 128], in1=pb3[:],
                    op=ALU.add), reads=[S_sb, pb3], writes=[S_sb])
                P.emit("vector", lambda v, hh0=hh0: v.tensor_copy(
                    out=S_bf[:, hh0 * 128:(hh0 + 4) * 128],
                    in_=S_sb[:, hh0 * 128:(hh0 + 4) * 128]),
                    reads=[S_sb], writes=[S_bf])
        P.emit("scalar", lambda s, zz=zz: s.activation(
            out=usq[:], in_=zz[:], func=AF.Silu), reads=[zz], writes=[usq])
        P.emit("vector", lambda v: v.tensor_tensor(
            out=o_sb[:], in0=o_sb[:], in1=usq[:], op=ALU.mult),
            reads=[(o_sb, hh0_) for hh0_ in range(0, 32, 4)] + [usq],
            writes=[(o_sb, "f")])
        P.emit("scalar", lambda s: s.activation(
            out=usq[:], in_=o_sb[:], func=AF.Square),
            reads=[(o_sb, "f")], writes=[usq])
        P.emit("vector", lambda v: v.tensor_reduce(
            out=orms[:], in_=bass.AP(usq, 0, [[4096, 128], [128, 32], [1, 128]]),
            axis=X, op=ALU.add), reads=[usq], writes=[orms])
        P.emit("vector", lambda v: v.tensor_scalar(
            out=orms[:], in0=orms[:], scalar1=1.0 / 128, scalar2=1e-6,
            op0=ALU.mult, op1=ALU.add), reads=[orms], writes=[orms])
        P.emit("vector", lambda v: v.tensor_scalar(
            out=orms[:], in0=orms[:], scalar1=1e-30, scalar2=None,
            op0=ALU.max), reads=[orms], writes=[orms])
        P.emit("scalar", lambda s: s.activation(
            out=orms[:], in_=orms[:], func=AF.Ln), reads=[orms], writes=[orms])
        P.emit("scalar", lambda s: s.activation(
            out=orms[:], in_=orms[:], func=AF.Exp, scale=-0.5),
            reads=[orms], writes=[orms])
        P.emit("vector", lambda v: v.tensor_tensor(
            out=o_sb[:], in0=o_sb[:],
            in1=bass.AP(orms, 0, [[32, 128], [1, 32], [0, 128]]),
            op=ALU.mult), reads=[(o_sb, "f"), orms], writes=[(o_sb, "f")])
        P.emit("vector", lambda v: v.tensor_tensor(
            out=o_sb[:], in0=o_sb[:],
            in1=bass.AP(nw1rep, 0, [[128, 128], [0, 32], [1, 128]]),
            op=ALU.mult), reads=[(o_sb, "f"), nw1rep], writes=[(o_sb, "f")])
        for hf in range(2):
            P.emit("scalar", lambda s, hf=hf: s.dma_start(
                out=bass.AP(ostag, 0, [[2048, 128], [128, 16], [1, 128]]),
                in_=o_sb[:, hf * 2048:(hf + 1) * 2048], transpose=True),
                reads=[(o_sb, "f")], writes=[ostag], dma=True)
            P.emit("gpsimd", lambda g, ci=ci, hf=hf: g.dma_start(
                out=bass.AP(h["yq_fm_d"], hf * 2048 * T + ci * 128,
                            [[T, 128], [128 * T, 16], [1, 128]]),
                in_=bass.AP(ostag, 0, [[2048, 128], [128, 16], [1, 128]])),
                reads=[ostag], writes=[(h["yq_fm_d"], (ci, hf))], dma=True)
    res["_scratch7"] = [qn_c[0], kn_c[0], ktm_c[0],
                        v_c[0], z_c[0], S_sb, S_bf, trb, stack,
                        gamT, bgT, dsT, cdT, larow, rpl, ones1, ident, identb,
                        mbS, mbT, mbQ, negcg, stgA, stgB, stgC, stgD, stgE,
                        gneg, gqk, etile, nst[0], nst[1], nts[0], nts[1], aqk, rX,
                        uU, o_sb, usq, rtmp, orms, nw1rep, nw1row, ostag]
    return res



def emit_l1_wout(c, h, res):
    P = c.P
    pp = c.pp
    yq_sb = c.sbuf("w1_yq", [128, 32 * T], BF16)
    wdc = [c.sbuf(f"w1_w{i}", [128, 32, 256], BF16) for i in range(2)]
    x1b = [c.sbuf(f"w1_x{i}", [128, 256], F32) for i in range(2)]
    ob = [c.sbuf(f"w1_o{i}", [128, 256], F32) for i in range(2)]
    P.emit("sync", lambda s: s.dma_start(
        out=bass.AP(yq_sb, 0, [[32 * T, 128], [T, 32], [1, T]]),
        in_=bass.AP(h["yq_fm_d"], 0, [[T, 128], [128 * T, 32], [1, T]])),
        reads=[(h["yq_fm_d"], (ci, hf)) for ci in range(16) for hf in range(2)],
        writes=[yq_sb], dma=True)
    for dc in range(8):
        wb = wdc[dc % 2]
        P.emit("gpsimd", lambda g, dc=dc, wb=wb: g.dma_start(
            out=wb[:], in_=bass.AP(h["w_outT1"], dc * 256,
                                   [[D, 128], [128 * D, 32], [1, 256]])),
            reads=[h["w_outT1"]], writes=[wb], dma=True)
        for t in range(NT):
            bank = pp[t // 2]
            half = (t % 2) * 256
            for k in range(32):
                P.emit("tensor", lambda te, k=k, t=t, bank=bank, half=half,
                       wb=wb: te.matmul(
                    bank[:, half:half + 256],
                    yq_sb[:, k * T + t * 128: k * T + (t + 1) * 128],
                    wb[:, k, :], start=(k == 0), stop=(k == 31)),
                    reads=[yq_sb, wb], writes=[bank])
            b2 = t % 2
            P.emit("sync", lambda s, t=t, dc=dc, xb=x1b[b2]: s.dma_start(
                out=xb[:], in_=h["x1_d"][ts_(t, 128), dc * 256:(dc + 1) * 256]),
                reads=[h["x1_d"]], writes=[x1b[b2]], dma=True)
            P.emit("vector", lambda v, bank=bank, half=half, xb=x1b[b2],
                   o=ob[b2]: v.tensor_tensor(
                out=o[:], in0=bank[:, half:half + 256], in1=xb[:], op=ALU.add),
                reads=[bank, x1b[b2]], writes=[ob[b2]])
            P.emit("gpsimd", lambda g, t=t, dc=dc, o=ob[b2]: g.dma_start(
                out=h["x2p_d"][ts_(t, 128), dc * 256:(dc + 1) * 256], in_=o[:]),
                reads=[ob[b2]], writes=[(h["x2p_d"], (t, dc))], dma=True)
    res["_scratch8"] = [yq_sb, wdc[0], wdc[1], x1b[0], x1b[1], ob[0], ob[1]]
    return res



def emit_ffn1(c, h, res):
    P = c.P
    pp = c.pp
    wq_fm = c.sbuf("f1_wq", [128, 16 * 2048], BF16, side="right")
    res["wq_fm"] = wq_fm
    qs = c.sbuf("f1_qs", [128, 2048], F32)
    qp = c.sbuf("f1_qp", [128, 2048], F32)
    qe = c.sbuf("f1_qe", [128, 2048], F32)
    qm = c.sbuf("f1_qm", [128, 2048], F32)
    qf = c.sbuf("f1_qf", [128, 2048], F32)
    sgn = c.sbuf("f1_sg", [128, 2048], F32)
    qi = c.sbuf("f1_qi", [128, 2048], I32)
    qx = c.sbuf("f1_qx", [128, 2048], I32)
    qb = c.sbuf("f1_qb", [128, 2048], BF16)
    fw_sb = c.sbuf("f1_fw", [128, 2048], BF16)
    xf = [c.sbuf("f1_xf0", [128, 2048], F32)] * 2
    xqb = c.sbuf("f1_xqb", [128, 2048], BF16)
    xstag = c.sbuf("f1_xst", [128, 2048], BF16)
    yq16 = c.sbuf("f1_y16", [128, 2048], BF16)
    yf32 = c.sbuf("f1_yf", [128, 2048], F32)
    x2b = [c.sbuf("f1_x20", [128, 2048], F32)] * 2
    ob = [c.sbuf("f1_o0", [128, 2048], F32)] * 2
    bias_rep = c.sbuf("f1_bias", [128, 2048], BF16)
    wm = c.sbuf("f1_wm", [128, 16], F32)
    sc = c.sbuf("f1_sc", [1, 16], F32)
    sci = c.sbuf("f1_sci", [1, 16], I32)
    scb = c.sbuf("f1_scb", [1, 16], BF16)
    g8 = c.sbuf("f1_g8", [1, 8], F32)
    rep = c.sbuf("f1_rep", [128, 8], F32)
    ones_r = c.sbuf("f1_or", [1, 128], F32)
    ones_rb = c.sbuf("f1_orb", [1, 128], BF16)
    ident = c.sbuf("f1_id", [128, 128], F32)
    c.const(3.0 * LN2)
    c.const(-3.0 * LN2)

    P.emit("gpsimd", lambda g: g.memset(sc[:], 0.0),
           writes=[(sc, kk) for kk in
                   ("ws", "xs", "b", "p2f", "m2", 5, 6, 7, 8, 9, 10, 11)])
    P.emit("gpsimd", lambda g: g.memset(ones_r[:], 1.0), writes=[ones_r])
    P.emit("gpsimd", lambda g: g.memset(ones_rb[:], 1.0), writes=[ones_rb])
    P.emit("gpsimd", lambda g: g.memset(ident[:], 1.0), writes=[(ident, 0)])
    P.emit("gpsimd", lambda g: g.affine_select(
        out=ident[:], in_=ident[:], pattern=[[1, 128]],
        compare_op=ALU.is_equal, fill=0.0, base=0, channel_multiplier=-1),
        reads=[(ident, 0)], writes=[(ident, 1)])
    for k in range(16):
        P.emit("sync", lambda s, k=k: s.dma_start(
            out=fw_sb[:], in_=h["fwT"][ts_(k, 128), :]),
            reads=[h["fwT"]], writes=[fw_sb], dma=True)
        P.emit("vector", lambda v, k=k: v.tensor_reduce(
            out=wm[:, k:k + 1], in_=fw_sb[:], axis=X, op=ALU.max,
            apply_absolute_value=True), reads=[fw_sb], writes=[(wm, k)])
    P.emit("vector", lambda v: v.tensor_reduce(
        out=wm[:, 0:1], in_=wm[:, 0:16], axis=X, op=ALU.max),
        reads=[(wm, k) for k in range(16)], writes=[(wm, "m"), (wm, 0)])
    P.emit("tensor", lambda te: te.transpose(pp[0][0:1, 0:128], wm[:, 0:1],
                                             ident[:]),
           reads=[(wm, "m"), (ident, 1)], writes=[pp[0]])
    P.emit("scalar", lambda s: s.copy(out=sc[0:1, 8:9], in_=pp[0][0:1, 0:1]),
           reads=[pp[0]], writes=[(sc, 8)])
    P.emit("vector", lambda v: v.tensor_reduce(
        out=sc[0:1, 0:1], in_=pp[0][0:1, 0:128], axis=X, op=ALU.max),
        reads=[pp[0]], writes=[(sc, "ws")])
    P.emit("vector", lambda v: v.tensor_scalar(
        out=sc[0:1, 0:1], in0=sc[0:1, 0:1], scalar1=1e-6, scalar2=None,
        op0=ALU.max), reads=[(sc, "ws")], writes=[(sc, "ws")])
    P.emit("sync", lambda s: s.dma_start(out=g8[:], in_=h["gmax"][:, :]),
           reads=[h["gmax"]], writes=[g8], dma=True)
    P.emit("vector", lambda v: v.tensor_reduce(
        out=sc[0:1, 1:2], in_=g8[:], axis=X, op=ALU.max),
        reads=[g8], writes=[(sc, "xs")])
    P.emit("vector", lambda v: v.tensor_scalar(
        out=sc[0:1, 1:2], in0=sc[0:1, 1:2], scalar1=1e-6, scalar2=None,
        op0=ALU.max), reads=[(sc, "xs")], writes=[(sc, "xs")])
    wsx = c.sbuf("f1_wsx", [1, 1], F32)
    bxo = c.sbuf("f1_bxo", [1, 1], F32)
    tbx = c.sbuf("f1_tbx", [1, 1], BF16)
    tfx = c.sbuf("f1_tfx", [1, 1], F32)
    tix = c.sbuf("f1_tix", [1, 1], I32)
    P.emit("vector", lambda v: v.tensor_copy(out=wsx[:], in_=sc[0:1, 0:1]),
           reads=[(sc, "ws")], writes=[wsx])
    emit_exact_exp2(c, lambda: wsx[0:1, 0:1], [wsx], bxo, bxo, tbx, tfx, tix,
                    1, 1)
    P.emit("vector", lambda v: v.tensor_scalar(
        out=sc[0:1, 3:4], in0=bxo[0:1, 0:1], scalar1=-float(2 ** 23),
        scalar2=float(134 * 2 ** 23), op0=ALU.mult, op1=ALU.add),
        reads=[bxo], writes=[(sc, "p2f")])
    P.emit("vector", lambda v: v.tensor_copy(out=sci[0:1, 2:3],
                                             in_=sc[0:1, 3:4]),
           reads=[(sc, "p2f")], writes=[(sci, "p2")])
    P.emit("vector", lambda v: v.tensor_tensor(
        out=sc[0:1, 4:5], in0=sc[0:1, 0:1],
        in1=sci.bitcast(F32)[0:1, 2:3], op=ALU.mult),
        reads=[(sc, "ws"), (sci, "p2")], writes=[(sc, "m2")])
    for src_j, dst_j, sk in ((4, 5, "m2"), (1, 6, "xs")):
        P.emit("vector", lambda v, src_j=src_j, dst_j=dst_j: v.reciprocal(
            out=sc[0:1, dst_j:dst_j + 1], in_=sc[0:1, src_j:src_j + 1]),
            reads=[(sc, sk)], writes=[(sc, dst_j)])
        P.emit("vector", lambda v, src_j=src_j, dst_j=dst_j: v.tensor_tensor(
            out=sc[0:1, 7:8], in0=sc[0:1, src_j:src_j + 1],
            in1=sc[0:1, dst_j:dst_j + 1], op=ALU.mult),
            reads=[(sc, sk), (sc, dst_j)], writes=[(sc, 7)])
        P.emit("vector", lambda v: v.tensor_scalar(
            out=sc[0:1, 7:8], in0=sc[0:1, 7:8], scalar1=-1.0, scalar2=2.0,
            op0=ALU.mult, op1=ALU.add), reads=[(sc, 7)], writes=[(sc, 7)])
        P.emit("vector", lambda v, dst_j=dst_j: v.tensor_tensor(
            out=sc[0:1, dst_j:dst_j + 1], in0=sc[0:1, dst_j:dst_j + 1],
            in1=sc[0:1, 7:8], op=ALU.mult),
            reads=[(sc, dst_j), (sc, 7)], writes=[(sc, dst_j)])
    P.emit("vector", lambda v: v.tensor_tensor(
        out=sc[0:1, 9:10], in0=sc[0:1, 0:1], in1=sc[0:1, 1:2], op=ALU.mult),
        reads=[(sc, "ws"), (sc, "xs")], writes=[(sc, 9)])
    P.emit("vector", lambda v: v.tensor_scalar(
        out=sc[0:1, 10:11], in0=sc[0:1, 4:5], scalar1=2.0, scalar2=None,
        op0=ALU.mult), reads=[(sc, "m2")], writes=[(sc, 10)])
    P.emit("vector", lambda v: v.tensor_copy(
        out=sc[0:1, 11:12], in_=sci.bitcast(F32)[0:1, 2:3]),
        reads=[(sci, "p2")], writes=[(sc, 11)])
    P.emit("tensor", lambda te: te.matmul(
        pp[1][:, 0:16], ones_r[:], sc[0:1, 0:16], start=True, stop=True),
        reads=[ones_r, (sc, 11), (sc, 5), (sc, "m2"), (sc, 10), (sc, 9),
               (sc, 6), (sc, 7), (sc, 8), (sc, "ws"), (sc, "xs"),
               (sc, "p2f")], writes=[pp[1]])
    P.emit("scalar", lambda s: s.copy(out=rep[:, 0:8], in_=pp[1][:, 0:8]),
           reads=[pp[1]], writes=[(rep, 0)])
    P.emit("scalar", lambda s: s.copy(out=wm[:, 0:8], in_=pp[1][:, 8:16]),
           reads=[pp[1]],
           writes=[(wm, "rep2"), (wm, "m")] + [(wm, kk) for kk in range(8)])
    p2c = lambda: wm[:, 3:4]
    rm2c = lambda: rep[:, 5:6]
    m2c = lambda: rep[:, 4:5]
    m2x2c = lambda: wm[:, 2:3]
    xswsc = lambda: wm[:, 1:2]
    rxsc = lambda: rep[:, 6:7]
    P.emit("sync", lambda s: s.dma_start(out=qb[0:1, 0:2048], in_=h["fb"][:, :]),
           reads=[h["fb"]], writes=[(qb, "bias")], dma=True)
    for occ in range(4):
        P.emit("tensor", lambda te, occ=occ: te.matmul(
            pp[2][:, 0:512], ones_rb[:], qb[0:1, occ * 512:(occ + 1) * 512],
            start=True, stop=True),
            reads=[ones_rb, (qb, "bias")], writes=[pp[2]])
        P.emit("scalar", lambda s, occ=occ: s.copy(
            out=bias_rep[:, occ * 512:(occ + 1) * 512], in_=pp[2][:, 0:512]),
            reads=[pp[2]], writes=[(bias_rep, occ)])

    for k in range(16):
        P.emit("sync", lambda s, k=k: s.dma_start(
            out=fw_sb[:], in_=h["fwT"][ts_(k, 128), :]),
            reads=[h["fwT"]], writes=[fw_sb], dma=True)
        P.emit("vector", lambda v: v.tensor_scalar(
            out=qs[:], in0=fw_sb[:], scalar1=p2c(), scalar2=None,
            op0=ALU.mult), reads=[fw_sb, (wm, "rep2")], writes=[qs])
        P.emit("scalar", lambda s: s.activation(
            out=sgn[:], in_=qs[:], func=AF.Sign), reads=[qs], writes=[sgn])
        P.emit("scalar", lambda s: s.activation(
            out=qe[:], in_=qs[:], func=AF.Abs), reads=[qs], writes=[qe])
        P.emit("vector", lambda v: v.tensor_scalar(
            out=qm[:], in0=qe[:], scalar1=rm2c(), scalar2=None, op0=ALU.mult),
            reads=[qe, (rep, 0)], writes=[qm])
        P.emit("vector", lambda v: v.tensor_scalar(
            out=qm[:], in0=qm[:], scalar1=1e-38, scalar2=None, op0=ALU.max),
            reads=[qm], writes=[qm])
        P.emit("scalar", lambda s: s.activation(
            out=qm[:], in_=qm[:], func=AF.Ln), reads=[qm], writes=[qm])
        P.emit("vector", lambda v: v.tensor_scalar(
            out=qm[:], in0=qm[:], scalar1=1.0 / LN2, scalar2=1e-3,
            op0=ALU.mult, op1=ALU.add), reads=[qm], writes=[qm])
        P.emit("vector", lambda v: v.tensor_copy(out=qi[:], in_=qm[:]),
               reads=[qm], writes=[qi])
        P.emit("vector", lambda v: v.tensor_copy(out=qp[:], in_=qi[:]),
               reads=[qi], writes=[qp])
        P.emit("vector", lambda v: v.tensor_tensor(
            out=qf[:], in0=qp[:], in1=qm[:], op=ALU.is_gt),
            reads=[qp, qm], writes=[qf])
        P.emit("vector", lambda v: v.tensor_tensor(
            out=qp[:], in0=qp[:], in1=qf[:], op=ALU.subtract),
            reads=[qp, qf], writes=[qp])
        P.emit("vector", lambda v: v.tensor_scalar(
            out=qp[:], in0=qp[:], scalar1=-6.0, scalar2=None, op0=ALU.max),
            reads=[qp], writes=[qp])
        P.emit("vector", lambda v: v.tensor_scalar(
            out=qm[:], in0=qp[:], scalar1=-float(2 ** 23),
            scalar2=float(130 * 2 ** 23), op0=ALU.mult, op1=ALU.add),
            reads=[qp], writes=[qm])
        P.emit("vector", lambda v: v.tensor_copy(out=qx[:], in_=qm[:]),
               reads=[qm], writes=[qx])
        P.emit("vector", lambda v: v.tensor_tensor(
            out=qs[:], in0=qe[:], in1=qx.bitcast(F32)[:, 0:2048],
            op=ALU.mult), reads=[qe, qx], writes=[qs])
        P.emit("vector", lambda v: v.tensor_scalar(
            out=qe[:], in0=qs[:], scalar1=2.0, scalar2=None, op0=ALU.mult),
            reads=[qs], writes=[qe])
        P.emit("vector", lambda v: v.tensor_scalar(
            out=qm[:], in0=qs[:], scalar1=rm2c(), scalar2=0.5,
            op0=ALU.mult, op1=ALU.add), reads=[qs, (rep, 0)], writes=[qm])
        P.emit("vector", lambda v: v.tensor_copy(out=qi[:], in_=qm[:]),
               reads=[qm], writes=[qi])
        P.emit("vector", lambda v: v.tensor_copy(out=qs[:], in_=qi[:]),
               reads=[qi], writes=[qs])
        P.emit("vector", lambda v: v.tensor_tensor(
            out=qf[:], in0=qs[:], in1=qm[:], op=ALU.is_gt),
            reads=[qs, qm], writes=[qf])
        P.emit("vector", lambda v: v.tensor_tensor(
            out=qs[:], in0=qs[:], in1=qf[:], op=ALU.subtract),
            reads=[qs, qf], writes=[qs])
        P.emit("vector", lambda v: v.tensor_scalar(
            out=qf[:], in0=qs[:], scalar1=0.5, scalar2=None, op0=ALU.mult),
            reads=[qs], writes=[qf])
        P.emit("vector", lambda v: v.tensor_copy(out=qi[:], in_=qf[:]),
               reads=[qf], writes=[qi])
        P.emit("vector", lambda v: v.tensor_copy(out=qm[:], in_=qi[:]),
               reads=[qi], writes=[qm])
        P.emit("vector", lambda v: v.tensor_tensor(
            out=qm[:], in0=qm[:], in1=qf[:], op=ALU.subtract),
            reads=[qm, qf], writes=[qm])
        P.emit("scalar", lambda s: s.activation(
            out=qm[:], in_=qm[:], func=AF.Square, scale=2.0),
            reads=[qm], writes=[qm])
        P.emit("vector", lambda v: v.tensor_copy(out=qb[:], in_=qm[:]),
               reads=[qm], writes=[(qb, "odd"), (qb, "bias")])
        P.emit("vector", lambda v: v.tensor_scalar(
            out=qm[:], in0=qs[:], scalar1=2.0, scalar2=-1.0,
            op0=ALU.mult, op1=ALU.add), reads=[qs], writes=[qm])
        P.emit("vector", lambda v: v.tensor_scalar(
            out=qm[:], in0=qm[:], scalar1=m2c(), scalar2=None, op0=ALU.mult),
            reads=[qm, (rep, 0)], writes=[qm])
        P.emit("vector", lambda v: v.tensor_tensor(
            out=qm[:], in0=qe[:], in1=qm[:], op=ALU.subtract),
            reads=[qe, qm], writes=[qm])
        P.emit("vector", lambda v: v.tensor_scalar(
            out=qe[:], in0=qm[:], scalar1=m2x2c(), scalar2=None,
            op0=ALU.subtract), reads=[qm, (wm, "rep2")], writes=[qe])
        P.emit("vector", lambda v: v.tensor_scalar(
            out=qf[:], in0=qm[:], scalar1=0.0, scalar2=None, op0=ALU.is_lt),
            reads=[qm], writes=[qf])
        P.emit("vector", lambda v: v.tensor_tensor(
            out=qs[:], in0=qs[:], in1=qf[:], op=ALU.subtract),
            reads=[qs, qf], writes=[qs])
        P.emit("vector", lambda v: v.tensor_scalar(
            out=qf[:], in0=qe[:], scalar1=0.0, scalar2=None, op0=ALU.is_gt),
            reads=[qe], writes=[qf])
        P.emit("vector", lambda v: v.tensor_tensor(
            out=qs[:], in0=qs[:], in1=qf[:], op=ALU.add),
            reads=[qs, qf], writes=[qs])
        P.emit("vector", lambda v: v.tensor_scalar(
            out=qf[:], in0=qm[:], scalar1=0.0, scalar2=None, op0=ALU.is_equal),
            reads=[qm], writes=[qf])
        P.emit("vector", lambda v: v.tensor_tensor(
            out=qf[:], in0=qf[:], in1=qb[:], op=ALU.mult),
            reads=[qf, (qb, "odd")], writes=[qf])
        P.emit("vector", lambda v: v.tensor_tensor(
            out=qs[:], in0=qs[:], in1=qf[:], op=ALU.subtract),
            reads=[qs, qf], writes=[qs])
        P.emit("vector", lambda v: v.tensor_scalar(
            out=qf[:], in0=qe[:], scalar1=0.0, scalar2=None, op0=ALU.is_equal),
            reads=[qe], writes=[qf])
        P.emit("vector", lambda v: v.tensor_tensor(
            out=qf[:], in0=qf[:], in1=qb[:], op=ALU.mult),
            reads=[qf, (qb, "odd")], writes=[qf])
        P.emit("vector", lambda v: v.tensor_tensor(
            out=qs[:], in0=qs[:], in1=qf[:], op=ALU.add),
            reads=[qs, qf], writes=[qs])
        P.emit("vector", lambda v: v.tensor_scalar(
            out=qm[:], in0=qp[:], scalar1=float(2 ** 23),
            scalar2=float(124 * 2 ** 23), op0=ALU.mult, op1=ALU.add),
            reads=[qp], writes=[qm])
        P.emit("vector", lambda v: v.tensor_copy(out=qx[:], in_=qm[:]),
               reads=[qm], writes=[qx])
        P.emit("vector", lambda v: v.tensor_tensor(
            out=qs[:], in0=qs[:], in1=qx.bitcast(F32)[:, 0:2048],
            op=ALU.mult), reads=[qs, qx], writes=[qs])
        P.emit("vector", lambda v: v.tensor_tensor(
            out=qs[:], in0=qs[:], in1=sgn[:], op=ALU.mult),
            reads=[qs, sgn], writes=[qs])
        P.emit("vector", lambda v, k=k: v.tensor_copy(
            out=wq_fm[:, ts_(k, 2048)], in_=qs[:]),
            reads=[qs], writes=[(wq_fm, k)])

    for t in range(NT):
        b = t % 2
        xb = xf[b]
        P.emit("sync", lambda s, t=t, xb=xb: s.dma_start(
            out=xb[:], in_=h["xn1f_d"][ts_(t, 128), :]),
            reads=[h["xn1f_d"]], writes=[xb], dma=True)
        P.emit("vector", lambda v, xb=xb: v.tensor_scalar(
            out=qs[:], in0=xb[:], scalar1=rxsc(), scalar2=None, op0=ALU.mult),
            reads=[xb, (rep, 0)], writes=[qs])
        P.emit("scalar", lambda s: s.activation(
            out=qp[:], in_=qs[:], func=AF.Square), reads=[qs], writes=[qp])
        P.emit("vector", lambda v: v.tensor_scalar(
            out=qp[:], in0=qp[:], scalar1=1e-38, scalar2=None, op0=ALU.max),
            reads=[qp], writes=[qp])
        P.emit("scalar", lambda s: s.activation(
            out=qp[:], in_=qp[:], func=AF.Ln), reads=[qp], writes=[qp])
        P.emit("vector", lambda v: v.tensor_scalar(
            out=qp[:], in0=qp[:], scalar1=1.0 / (2 * LN2), scalar2=1e-3,
            op0=ALU.mult, op1=ALU.add), reads=[qp], writes=[qp])
        P.emit("vector", lambda v: v.tensor_copy(out=qi[:], in_=qp[:]),
               reads=[qp], writes=[qi])
        P.emit("vector", lambda v: v.tensor_copy(out=qe[:], in_=qi[:]),
               reads=[qi], writes=[qe])
        P.emit("vector", lambda v: v.tensor_tensor(
            out=qm[:], in0=qe[:], in1=qp[:], op=ALU.is_gt),
            reads=[qe, qp], writes=[qm])
        P.emit("vector", lambda v: v.tensor_tensor(
            out=qp[:], in0=qe[:], in1=qm[:], op=ALU.subtract),
            reads=[qe, qm], writes=[qp])
        P.emit("vector", lambda v: v.tensor_scalar(
            out=qp[:], in0=qp[:], scalar1=-6.0, scalar2=None, op0=ALU.max),
            reads=[qp], writes=[qp])
        P.emit("scalar", lambda s: s.activation(
            out=qe[:], in_=qp[:], func=AF.Exp, scale=-LN2,
            bias=c.const(3.0 * LN2)),
            reads=[qp, c.cread(3.0 * LN2)], writes=[qe])
        P.emit("vector", lambda v: v.tensor_tensor(
            out=qm[:], in0=qs[:], in1=qe[:], op=ALU.mult),
            reads=[qs, qe], writes=[qm])
        P.emit("vector", lambda v: v.tensor_scalar(
            out=qm[:], in0=qm[:], scalar1=0.5, scalar2=None, op0=ALU.add),
            reads=[qm], writes=[qm])
        P.emit("vector", lambda v: v.tensor_copy(out=qi[:], in_=qm[:]),
               reads=[qm], writes=[qi])
        P.emit("vector", lambda v: v.tensor_copy(out=qs[:], in_=qi[:]),
               reads=[qi], writes=[qs])
        P.emit("vector", lambda v: v.tensor_tensor(
            out=qf[:], in0=qs[:], in1=qm[:], op=ALU.is_gt),
            reads=[qs, qm], writes=[qf])
        P.emit("vector", lambda v: v.tensor_tensor(
            out=qs[:], in0=qs[:], in1=qf[:], op=ALU.subtract),
            reads=[qs, qf], writes=[qs])
        P.emit("scalar", lambda s: s.activation(
            out=qe[:], in_=qp[:], func=AF.Exp, scale=LN2,
            bias=c.const(-3.0 * LN2)),
            reads=[qp, c.cread(-3.0 * LN2)], writes=[qe])
        P.emit("vector", lambda v: v.tensor_tensor(
            out=qs[:], in0=qs[:], in1=qe[:], op=ALU.mult),
            reads=[qs, qe], writes=[qs])
        P.emit("vector", lambda v: v.tensor_copy(out=xqb[:], in_=qs[:]),
               reads=[qs], writes=[xqb])
        P.emit("scalar", lambda s: s.dma_start(
            out=bass.AP(xstag, 0, [[2048, 128], [128, 16], [1, 128]]),
            in_=xqb[:], transpose=True), reads=[xqb], writes=[xstag],
            dma=True)
        for oc in range(4):
            for k in range(16):
                P.emit("tensor", lambda te, oc=oc, k=k: te.matmul(
                    pp[oc][:, 0:512], xstag[:, ts_(k, 128)],
                    wq_fm[:, k * 2048 + oc * 512:k * 2048 + (oc + 1) * 512],
                    start=(k == 0), stop=(k == 15)),
                    reads=[xstag, (wq_fm, k)], writes=[pp[oc]])
        P.emit("sync", lambda s, t=t, x2=x2b[b]: s.dma_start(
            out=x2[:], in_=h["x2p_d"][ts_(t, 128), :]),
            reads=[h["x2p_d"]], writes=[x2b[b]], dma=True)
        for oc in range(4):
            P.emit("vector", lambda v, oc=oc: v.tensor_scalar(
                out=yf32[:, oc * 512:(oc + 1) * 512], in0=pp[oc][:, 0:512],
                scalar1=xswsc(), scalar2=None, op0=ALU.mult),
                reads=[pp[oc], (wm, "rep2")], writes=[(yf32, oc)])
        P.emit("vector", lambda v: v.tensor_copy(out=yq16[:], in_=yf32[:]),
               reads=[(yf32, oc_) for oc_ in range(4)], writes=[yq16])
        P.emit("vector", lambda v: v.tensor_tensor(
            out=yq16[:], in0=yq16[:], in1=bias_rep[:], op=ALU.add),
            reads=[yq16] + [(bias_rep, occ) for occ in range(4)],
            writes=[yq16])
        P.emit("vector", lambda v: v.tensor_copy(out=yf32[:], in_=yq16[:]),
               reads=[yq16],
               writes=[(yf32, "up")] + [(yf32, oc_) for oc_ in range(4)])
        P.emit("vector", lambda v, b=b, o=ob[b]: v.tensor_tensor(
            out=o[:], in0=yf32[:], in1=x2b[b][:], op=ALU.add),
            reads=[(yf32, "up"), x2b[b]], writes=[ob[b]])
        P.emit("gpsimd", lambda g, t=t, o=ob[b]: g.dma_start(
            out=h["out_d"][ts_(t, 128), :], in_=o[:]),
            reads=[ob[b]], writes=[(h["out_d"], t)], dma=True)
    return res
